# version 10
# baseline (speedup 1.0000x reference)
import numpy as np
import concourse.bass as bass
import concourse.mybir as mybir
from concourse.bass_utils import run_bass_kernel_spmd

F32 = mybir.dt.float32
BF16 = mybir.dt.bfloat16
AF = mybir.ActivationFunctionType
ALU = mybir.AluOpType
AX = mybir.AxisListType

N_CORES = 8
D = 1024
NH = 16
HD = 64
I = 2816
TOK_PER_CORE = 2048
N_TILES = TOK_PER_CORE // 128
KS = D // 128
NT_W = I // 128
CHUNKS = [512, 512, 512, 512, 384, 384]
assert sum(CHUNKS) == I
EPS_LN = 1e-5
EPS_ATT = 1e-6
QTHR = [0.25, 0.75, 1.25, 1.75, 2.5, 3.5, 5.0]
PIPE_PREP = True
PIPE_B = True


def build_program(n_tiles=N_TILES, debug=False, phases="PAB"):
    nc = bass.Bass("TRN2", target_bir_lowering=False, detect_race_conditions=False)
    ntok = n_tiles * 128

    x_d = nc.dram_tensor("x", [ntok, D], F32, kind="ExternalInput").ap()
    wq_d = nc.dram_tensor("wq", [D, D], F32, kind="ExternalInput").ap()
    wk_d = nc.dram_tensor("wk", [D, D], F32, kind="ExternalInput").ap()
    wv_d = nc.dram_tensor("wv", [D, D], F32, kind="ExternalInput").ap()
    wo_d = nc.dram_tensor("wo", [D, D], F32, kind="ExternalInput").ap()
    bo_d = nc.dram_tensor("bo2", [1, D], F32, kind="ExternalInput").ap()
    wg_d = nc.dram_tensor("wg", [I, D], F32, kind="ExternalInput").ap()
    wu_d = nc.dram_tensor("wu", [I, D], F32, kind="ExternalInput").ap()
    wd_d = nc.dram_tensor("wd", [D, I], F32, kind="ExternalInput").ap()
    y_d = nc.dram_tensor("y", [ntok, D], F32, kind="ExternalOutput").ap()
    if debug:
        dxmid_d = nc.dram_tensor("dxmid", [ntok, D], F32, kind="ExternalOutput").ap()
        dwg_d = nc.dram_tensor("dwg", [128, I], F32, kind="ExternalOutput").ap()
        dwgl_d = nc.dram_tensor("dwgl", [128, I], F32, kind="ExternalOutput").ap()
        dxq_d = nc.dram_tensor("dxq", [128, D], F32, kind="ExternalOutput").ap()
        dxln_d = nc.dram_tensor("dxln", [128, D], F32, kind="ExternalOutput").ap()
        dq_d = nc.dram_tensor("dq", [128, D], F32, kind="ExternalOutput").ap()
        dk_d = nc.dram_tensor("dk", [128, D], F32, kind="ExternalOutput").ap()
        dv_d = nc.dram_tensor("dv", [128, D], F32, kind="ExternalOutput").ap()
        dattn_d = nc.dram_tensor("dattn", [128, D], F32, kind="ExternalOutput").ap()
        dqn_d = nc.dram_tensor("dqn", [128, D], F32, kind="ExternalOutput").ap()
        dstt_d = nc.dram_tensor("dstt", [128, 8], F32, kind="ExternalOutput").ap()
        dxt_d = nc.dram_tensor("dxt", [128, D], F32, kind="ExternalOutput").ap()

    xmid_d = nc.dram_tensor("xmid", [ntok, D], F32, kind="Internal").ap()
    wgqh_d = nc.dram_tensor("wgqh", [KS, 128, I], BF16, kind="Internal").ap()
    wgql_d = nc.dram_tensor("wgql", [KS, 128, I], BF16, kind="Internal").ap()
    wuqh_d = nc.dram_tensor("wuqh", [KS, 128, I], BF16, kind="Internal").ap()
    wuql_d = nc.dram_tensor("wuql", [KS, 128, I], BF16, kind="Internal").ap()
    wdqh_d = nc.dram_tensor("wdqh", [NT_W, 128, D], BF16, kind="Internal").ap()
    wdql_d = nc.dram_tensor("wdql", [NT_W, 128, D], BF16, kind="Internal").ap()

    from contextlib import ExitStack
    es = ExitStack()
    with es:
        sb = lambda name, shape, dt=F32: es.enter_context(nc.sbuf_tensor(name, shape, dt))
        ps = lambda name, shape, dt=F32: es.enter_context(nc.psum_tensor(name, shape, dt))
        sem = lambda name: es.enter_context(nc.semaphore(name))

        LD_SEMS = [sem(f"s_ld{i}") for i in range(4)]
        C_LD = [0, 0, 0, 0]
        ld_rr = [0]
        s_ld = LD_SEMS[0]
        s_st = sem("s_st")
        s_v = sem("s_v")
        s_a = sem("s_a")
        s_t = sem("s_t")
        s_g = sem("s_g")
        C = {"st": 0, "v": 0, "a": 0, "t": 0, "g": 0}
        SEM = {"st": s_st, "v": s_v, "a": s_a, "t": s_t, "g": s_g}
        ENG = {"v": "vector", "a": "scalar", "t": "tensor", "g": "gpsimd",
               "ld": "sync", "st": "gpsimd"}

        nums = sorted([h.num for h in SEM.values()] + [h.num for h in LD_SEMS])
        nc.all_engine_barrier()
        nc.gpsimd.dma_reset(range(nums[0], nums[-1] + 1))
        nc.gpsimd.sem_clear(range(nums[0], nums[-1] + 1))
        nc.all_engine_barrier()
        nc.all_engine_barrier()

        block = es.enter_context(nc.Block())

        def body(kind, fn):
            def runner(e):
                def w(dep, cnt=None):
                    if dep == "ld":
                        wld(e, ld_mark() if cnt is None else cnt)
                        return
                    c = C[dep] if cnt is None else cnt
                    if c > 0:
                        e.wait_ge(SEM[dep], c * 16 if dep == "st" else c)
                last = fn(e, w)
                if kind in ("ld", "st"):
                    pass
                else:
                    assert last is not None, "compute body must return last instr"
                    if kind == "v":
                        e.engine_nop().then_inc(SEM[kind], 1)
                    elif kind == "g":
                        e.memset(junkg[:], 0.0).then_inc(SEM[kind], 1)
                    else:
                        last.then_inc(SEM[kind], 1)
                    C[kind] += 1
            getattr(block, ENG[kind])(runner)

        LAST_LD = [(0, 0)]

        def dma(e, kind, out, in_):
            if kind == "ld":
                i = ld_rr[0] % 4
                ld_rr[0] += 1
                e.dma_start(out=out, in_=in_).then_inc(LD_SEMS[i], 16)
                C_LD[i] += 1
                LAST_LD[0] = (i, C_LD[i])
            else:
                e.dma_start(out=out, in_=in_).then_inc(SEM[kind], 16)
                C[kind] += 1

        def ld_mark():
            return LAST_LD[0]

        def wld(e, mark):
            i, c = mark
            if c > 0:
                e.wait_ge(LD_SEMS[i], c * 16)

        def vsp(e, n=2):
            last = None
            for _ in range(n):
                last = e.tensor_copy(out=junk[:], in_=junk[:])
            return last

        def asp(e, n=1):
            last = None
            for _ in range(n):
                last = e.activation(out=junka[:], in_=junka[:], func=AF.Copy)
            return last

        bo_rep = sb("bo_rep", [128, D])
        idb = sb("idb", [128, 128], BF16)
        c_eps = sb("c_eps", [128, 1])
        scores = sb("scores", [128, 160])
        stt = sb("stt", [128, 16])
        junk = sb("junk", [128, 128])
        junka = sb("junka", [128, 64])
        junkg = sb("junkg", [128, 4])

        qv = sb("qv", [128, D])
        qv2 = sb("qv2", [128, D])
        qu = sb("qu", [128, D])
        qsg = sb("qsg", [128, D], BF16)
        mA = sb("mA", [128, D], BF16); mB = sb("mB", [128, D], BF16)
        mC = sb("mC", [128, D], BF16)
        qmb = sb("qmb", [128, D // 16])
        qsc = sb("qsc", [128, D // 16])
        qrc = sb("qrc", [128, D // 16])
        qxq = sb("qxq", [128, D])
        qxh = sb("qxh", [128, D], BF16); qxl = sb("qxl", [128, D], BF16)

        es_a = ExitStack()
        sba = lambda name, shape, dt=F32: es_a.enter_context(nc.sbuf_tensor(name, shape, dt))
        stage = sba("stage", [128, D], BF16)
        stage2 = sba("stage2", [128, D], BF16)
        wqh = sba("wqh", [128, KS * D], BF16); wql = sba("wql", [128, KS * D], BF16)
        wkh = sba("wkh", [128, KS * D], BF16); wkl = sba("wkl", [128, KS * D], BF16)
        wvh = sba("wvh", [128, KS * D], BF16); wvl = sba("wvl", [128, KS * D], BF16)
        woh = sba("woh", [128, KS * D], BF16); wol = sba("wol", [128, KS * D], BF16)
        xa = sba("xa", [128, D]); xb = sba("xb", [128, D])
        qb = sba("qb", [128, D]); kb = sba("kb", [128, D]); vb = sba("vb", [128, D])
        kc = sba("kc", [128, D]); attn = sba("attn", [128, D])
        xh = sba("xh", [128, D], BF16); xl = sba("xl", [128, D], BF16)
        xhT = sba("xhT", [128, D], BF16); xlT = sba("xlT", [128, D], BF16)
        xov = sba("xov", [128, D])
        scrb = attn
        scr = qxq
        scr2 = qv

        pTA = ps("pTA", [128, 512])
        pTB = ps("pTB", [128, 512])
        pQ = ps("pQ", [128, D])
        pK = ps("pK", [128, D])
        pV = ps("pV", [128, D])

        def _setup_sync(e, w):
            dma(e, "ld", bo_rep[:], bo_d.partition_broadcast(128))
        body("ld", _setup_sync)

        def _setup_g(e, w):
            e.memset(idb[:], 0.0).then_inc(SEM["g"], 1)
            C["g"] += 1
            e.wait_ge(SEM["g"], C["g"])
            return e.affine_select(out=idb[:], in_=idb[:], compare_op=ALU.not_equal,
                                   fill=1.0, base=0, pattern=[[-1, 128]],
                                   channel_multiplier=1)
        body("g", _setup_g)
        G_SETUP = C["g"]

        def _setup_v(e, w):
            e.memset(c_eps[:], EPS_LN)
            e.memset(junka[:], 0.0)
            return e.memset(junk[:], 0.0)
        body("v", _setup_v)
        V_SETUP = C["v"]

        wd_pairs = [(wq_d, wqh, wql), (wk_d, wkh, wkl),
                    (wv_d, wvh, wvl), (wo_d, woh, wol)]
        raws = [qv, qu]
        last_split_v = {}
        idx = 0
        for (wdram, whi, wlo) in wd_pairs:
            for ksi in range(KS):
                rb = raws[idx % 2]
                vcnt_guard = last_split_v.get(idx % 2, 0)

                def _ld(e, w, rb=rb, wdram=wdram, ksi=ksi, vg=vcnt_guard):
                    if vg:
                        e.wait_ge(SEM["v"], vg)
                    dma(e, "ld", rb[:, 0:D], wdram[ksi * 128:(ksi + 1) * 128, :])
                body("ld", _ld)
                ld_now = ld_mark()

                def _sp(e, w, rb=rb, whi=whi, wlo=wlo, ksi=ksi, ld_now=ld_now):
                    w("ld", ld_now)
                    e.tensor_copy(out=whi[:, ksi * D:(ksi + 1) * D], in_=rb[:, 0:D])
                    return e.tensor_tensor(out=wlo[:, ksi * D:(ksi + 1) * D],
                                           in0=rb[:, 0:D],
                                           in1=whi[:, ksi * D:(ksi + 1) * D],
                                           op=ALU.subtract)
                body("v", _sp)
                last_split_v[idx % 2] = C["v"]
                idx += 1

        def emit_quant(F, qsrc=None, tdep=0, qh=None, ql=None):
            nb = F // 16
            if qsrc is None:
                qsrc = qv
            if qh is None:
                qh = qxh
            if ql is None:
                ql = qxl

            def _q1(e, w):
                if tdep:
                    e.wait_ge(SEM["t"], tdep)
                e.reduce_max(out=qmb[:, 0:nb],
                             in_=qsrc[:, 0:F].rearrange("p (b s) -> p b s", s=16),
                             axis=AX.X, apply_absolute_value=True)
                vsp(e)
                e.tensor_scalar(out=qsc[:, 0:nb], in0=qmb[:, 0:nb],
                                scalar1=1.0 / 6.0, scalar2=1e-12,
                                op0=ALU.mult, op1=ALU.max)
                vsp(e)
                e.reciprocal(out=qrc[:, 0:nb], in_=qsc[:, 0:nb])
                vsp(e)
                return e.tensor_tensor(
                    out=qsrc[:, 0:F].rearrange("p (b s) -> p b s", s=16),
                    in0=qsrc[:, 0:F].rearrange("p (b s) -> p b s", s=16),
                    in1=qrc[:, 0:nb].unsqueeze(2).broadcast_to([128, nb, 16]),
                    op=ALU.mult)
            body("v", _q1)
            v_q1 = C["v"]

            def _qa(e, w):
                w("v", v_q1)
                e.activation(out=qu[:, 0:F], in_=qsrc[:, 0:F], func=AF.Abs)
                return e.activation(out=qsg[:, 0:F], in_=qsrc[:, 0:F], func=AF.Tanh,
                                    scale=1e30)
            body("a", _qa)
            a_q = C["a"]

            def _q2(e, w):
                w("a", a_q)
                u = qu[:, 0:F]
                sp = (F < 512)
                e.tensor_scalar(out=mA[:, 0:F], in0=u, scalar1=QTHR[0], scalar2=None, op0=ALU.is_gt)
                e.tensor_scalar(out=mB[:, 0:F], in0=u, scalar1=QTHR[1], scalar2=None, op0=ALU.is_gt)
                if sp: vsp(e)
                e.tensor_tensor(out=mA[:, 0:F], in0=mA[:, 0:F], in1=mB[:, 0:F], op=ALU.add)
                e.tensor_scalar(out=mB[:, 0:F], in0=u, scalar1=QTHR[2], scalar2=None, op0=ALU.is_gt)
                e.tensor_scalar(out=mC[:, 0:F], in0=u, scalar1=QTHR[3], scalar2=None, op0=ALU.is_gt)
                if sp: vsp(e)
                e.tensor_tensor(out=mB[:, 0:F], in0=mB[:, 0:F], in1=mC[:, 0:F], op=ALU.add)
                if sp: vsp(e)
                e.tensor_tensor(out=mA[:, 0:F], in0=mA[:, 0:F], in1=mB[:, 0:F], op=ALU.add)
                e.tensor_scalar(out=mB[:, 0:F], in0=u, scalar1=QTHR[4], scalar2=None, op0=ALU.is_gt)
                e.tensor_scalar(out=mC[:, 0:F], in0=u, scalar1=QTHR[5], scalar2=None, op0=ALU.is_gt)
                if sp: vsp(e)
                e.tensor_tensor(out=mB[:, 0:F], in0=mB[:, 0:F], in1=mC[:, 0:F], op=ALU.add)
                e.tensor_scalar(out=mC[:, 0:F], in0=u, scalar1=QTHR[6], scalar2=None, op0=ALU.is_gt)
                if sp: vsp(e)
                e.scalar_tensor_tensor(out=mA[:, 0:F], in0=mA[:, 0:F], scalar=0.5,
                                       in1=mB[:, 0:F], op0=ALU.mult, op1=ALU.add)
                if sp: vsp(e)
                e.scalar_tensor_tensor(out=mA[:, 0:F], in0=mC[:, 0:F], scalar=2.0,
                                       in1=mA[:, 0:F], op0=ALU.mult, op1=ALU.add)
                if sp: vsp(e)
                e.tensor_tensor(out=mA[:, 0:F], in0=mA[:, 0:F], in1=qsg[:, 0:F], op=ALU.mult)
                if sp: vsp(e)
                e.tensor_tensor(out=qxq[:, 0:F].rearrange("p (b s) -> p b s", s=16),
                                in0=mA[:, 0:F].rearrange("p (b s) -> p b s", s=16),
                                in1=qsc[:, 0:nb].unsqueeze(2).broadcast_to([128, nb, 16]),
                                op=ALU.mult)
                if sp: vsp(e)
                e.tensor_copy(out=qh[:, 0:F], in_=qxq[:, 0:F])
                if sp: vsp(e)
                return e.tensor_tensor(out=ql[:, 0:F], in0=qxq[:, 0:F],
                                       in1=qh[:, 0:F], op=ALU.subtract)
            body("v", _q2)

        bank_free = {0: 0, 1: 0}

        def emit_T(srcs, dsts, deps=(), st_dep=0):
            gi = 0
            out_a = 0
            for g0 in range(0, len(srcs), 4):
                gs = list(range(g0, min(g0 + 4, len(srcs))))
                bank = pTA if gi % 2 == 0 else pTB
                bsel = gi % 2
                gdeps = list(deps) if g0 == 0 else []
                bf = bank_free[bsel]

                def _tr(e, w, gs=gs, bank=bank, gdeps=gdeps, bf=bf, srcs=srcs):
                    for dk, dc in gdeps:
                        w(dk, dc)
                    if bf:
                        w("a", bf)
                    last = None
                    for i, si_ in enumerate(gs):
                        last = e.matmul(out=bank[:, i * 128:(i + 1) * 128],
                                        lhsT=srcs[si_], rhs=idb[:],
                                        start=(i == 0), stop=(i == len(gs) - 1))
                    return last
                body("t", _tr)
                t_now = C["t"]

                dt0, dc0 = dsts[g0]
                def _ev(e, w, t_now=t_now, bank=bank, n=len(gs), dt0=dt0,
                        dc0=dc0, st_dep=st_dep):
                    if st_dep:
                        e.wait_ge(SEM["st"], st_dep * 16)
                    w("t", t_now)
                    e.activation(out=dt0[:, dc0:dc0 + n * 128],
                                 in_=bank[:, 0:n * 128], func=AF.Copy)
                    return asp(e, 1)
                body("a", _ev)
                bank_free[bsel] = C["a"]
                out_a = C["a"]
                gi += 1
            return out_a

        st_guard = {0: 0, 1: 0}
        prep_items = []
        for (wnat_d, dsth, dstl) in [(wg_d, wgqh_d, wgql_d), (wu_d, wuqh_d, wuql_d)]:
            for nt in range(NT_W):
                prep_items.append((wnat_d, nt, 0, D, dsth, dstl,
                                   [(i, nt * 128) for i in range(KS)]))
        WD_SLICES = [(0, 1024), (1024, 1024), (2048, 768)]
        for nt in range(KS):
            for (s0, SW) in WD_SLICES:
                prep_items.append((wd_d, nt, s0, SW, wdqh_d, wdql_d,
                                   [(s0 // 128 + i, nt * 128) for i in range(SW // 128)]))

        prep_m = {}
        def prep_back(n):
            (wnat_d, nt, s0, SW, dsth, dstl, blocks) = prep_items[n]
            m = prep_m[n]
            nblk = len(blocks)
            emit_T([qxh[:, i * 128:(i + 1) * 128] for i in range(nblk)],
                   [(stage, i * 128) for i in range(nblk)],
                   deps=[("v", m["v_q"])], st_dep=st_guard[0])
            v_ev = emit_T([qxl[:, i * 128:(i + 1) * 128] for i in range(nblk)],
                          [(stage2, i * 128) for i in range(nblk)],
                          st_dep=st_guard[0])
            m["t_tr"] = C["t"]

            def _spill(e, w, blocks=blocks, dsth=dsth, dstl=dstl, v_ev=v_ev):
                w("a", v_ev)
                for bi, (ksd, co) in enumerate(blocks):
                    dma(e, "st", dsth[ksd, :, co:co + 128],
                        stage[:, bi * 128:(bi + 1) * 128])
                    dma(e, "st", dstl[ksd, :, co:co + 128],
                        stage2[:, bi * 128:(bi + 1) * 128])
            body("st", _spill)
            st_guard[0] = C["st"]

        qvb = [qv, qv2]
        S1_V_DONE = C["v"]
        for n, (wnat_d, nt, s0, SW, dsth, dstl, blocks) in enumerate(prep_items):
            pm = prep_m.get(n - 2)
            def _ldw(e, w, wnat_d=wnat_d, nt=nt, s0=s0, SW=SW, n=n, pm=pm):
                if pm is not None:
                    e.wait_ge(SEM["v"], pm["v_q1"])
                    e.wait_ge(SEM["a"], pm["a_qa"])
                else:
                    e.wait_ge(SEM["v"], S1_V_DONE)
                dma(e, "ld", qvb[n % 2][:, 0:SW],
                    wnat_d[nt * 128:(nt + 1) * 128, s0:s0 + SW])
            body("ld", _ldw)
            ld_now = ld_mark()

            if PIPE_PREP and n >= 1:
                prep_back(n - 1)

            def _qdep(e, w, ld_now=ld_now):
                w("ld", ld_now)
                return e.engine_nop()
            body("v", _qdep)
            v_q1_pre = C["v"]
            emit_quant(SW, qsrc=qvb[n % 2],
                       tdep=(prep_m[n - 1]["t_tr"] if (PIPE_PREP and n >= 1) else 0))
            prep_m[n] = {"v_q": C["v"], "v_q1": v_q1_pre + 2,
                         "a_qa": C["a"]}
            if not PIPE_PREP:
                prep_back(n)
        if PIPE_PREP:
            prep_back(len(prep_items) - 1)

        PREP_ST = C["st"]
        if debug:
            def _dldw(e, w, pst=PREP_ST):
                e.wait_ge(SEM["st"], pst * 16)
                dma(e, "ld", qxh[:, 0:D], wgqh_d[0, :, 0:D])
                dma(e, "ld", qxl[:, 0:D], wgql_d[0, :, 0:D])
            body("ld", _dldw)
            dld = ld_mark()
            def _dcvt(e, w, dld=dld):
                w("ld", dld)
                e.tensor_copy(out=qv[:, 0:D], in_=qxh[:, 0:D])
                return e.tensor_copy(out=qu[:, 0:D], in_=qxl[:, 0:D])
            body("v", _dcvt)
            dv = C["v"]
            def _dst(e, w, dv=dv):
                w("v", dv)
                dma(e, "st", dwg_d[:, 0:D], qv[:, 0:D])
                dma(e, "st", dwgl_d[:, 0:D], qu[:, 0:D])
            body("st", _dst)

        TRI_OFF = [h * (h + 1) // 2 for h in range(NH + 1)]

        PREP_A_DONE = C["a"]
        a_resid_v = {}
        xmid_st = {}
        for t in range(n_tiles if "A" in phases else 0):
            xt = xa if t % 2 == 0 else xb
            def _ldx(e, w, t=t, xt=xt, vg=a_resid_v.get(t - 2, 0)):
                if vg:
                    e.wait_ge(SEM["v"], vg)
                dma(e, "ld", xt[:], x_d[t * 128:(t + 1) * 128, :])
            body("ld", _ldx)
            ld_x = ld_mark()

            def _lnv(e, w, ld_x=ld_x, xt=xt, t0=(t == 0)):
                w("ld", ld_x)
                if t0:
                    w("a", PREP_A_DONE)
                e.reduce_sum(out=stt[:, 0:1], in_=xt[:], axis=AX.X)
                e.tensor_tensor(out=scrb[:], in0=xt[:], in1=xt[:], op=ALU.mult)
                e.reduce_sum(out=stt[:, 1:2], in_=scrb[:], axis=AX.X)
                e.tensor_scalar(out=stt[:, 2:3], in0=stt[:, 0:1], scalar1=1.0 / D,
                                scalar2=None, op0=ALU.mult)
                vsp(e)
                e.tensor_scalar(out=stt[:, 3:4], in0=stt[:, 1:2], scalar1=1.0 / D,
                                scalar2=EPS_LN, op0=ALU.mult, op1=ALU.add)
                e.tensor_tensor(out=stt[:, 4:5], in0=stt[:, 2:3], in1=stt[:, 2:3],
                                op=ALU.mult)
                vsp(e)
                e.tensor_tensor(out=stt[:, 5:6], in0=stt[:, 3:4],
                                in1=stt[:, 4:5], op=ALU.subtract)
                return vsp(e)
            body("v", _lnv)
            v_var = C["v"]

            def _lna2(e, w, v_var=v_var):
                w("v", v_var)
                e.activation(out=stt[:, 6:7], in_=stt[:, 5:6], func=AF.Ln)
                asp(e, 2)
                e.activation(out=stt[:, 7:8], in_=stt[:, 6:7], func=AF.Exp,
                             scale=-0.5)
                return asp(e, 2)
            body("a", _lna2)
            a_sqrt = C["a"]

            def _lnv3(e, w, a_sqrt=a_sqrt, xt=xt):
                w("a", a_sqrt)
                vsp(e)
                e.tensor_tensor(out=stt[:, 8:9], in0=stt[:, 2:3], in1=stt[:, 7:8],
                                op=ALU.mult)
                vsp(e)
                e.tensor_scalar(out=scr[:, 0:D], in0=xt[:], scalar1=stt[:, 7:8],
                                scalar2=stt[:, 8:9], op0=ALU.mult, op1=ALU.subtract)
                e.tensor_copy(out=xh[:], in_=scr[:, 0:D])
                return e.tensor_tensor(out=xl[:], in0=scr[:, 0:D], in1=xh[:],
                                       op=ALU.subtract)
            body("v", _lnv3)
            v_split = C["v"]
            if debug and t == 0:
                def _dxln(e, w, v_split=v_split, xt=xt):
                    w("v", v_split)
                    dma(e, "st", dxln_d[:, :], scr[:, 0:D])
                    dma(e, "st", dstt_d[:, :], stt[:, 0:8])
                    dma(e, "st", dxt_d[:, :], xt[:])
                body("st", _dxln)
                def _dhold0(e, w, snow=C["st"]):
                    w("st", snow)
                    return e.engine_nop()
                body("v", _dhold0)

            emit_T([xh[:, i * 128:(i + 1) * 128] for i in range(KS)],
                   [(xhT, i * 128) for i in range(KS)], deps=[("v", v_split)])
            v_xT = emit_T([xl[:, i * 128:(i + 1) * 128] for i in range(KS)],
                          [(xlT, i * 128) for i in range(KS)])

            def _qkv(e, w, v_xT=v_xT):
                w("a", v_xT)
                last = None
                for ksi in range(KS):
                    sh = xhT[:, ksi * 128:(ksi + 1) * 128]
                    sl = xlT[:, ksi * 128:(ksi + 1) * 128]
                    first = ksi == 0
                    lastk = ksi == KS - 1
                    for (whi, wlo, pdst) in ((wqh, wql, pQ), (wkh, wkl, pK),
                                             (wvh, wvl, pV)):
                        for nch in range(2):
                            rh = whi[:, ksi * D + nch * 512: ksi * D + nch * 512 + 512]
                            rl = wlo[:, ksi * D + nch * 512: ksi * D + nch * 512 + 512]
                            od = pdst[:, nch * 512:(nch + 1) * 512]
                            last = e.matmul(out=od, lhsT=sh, rhs=rh,
                                            start=first, stop=False)
                            last = e.matmul(out=od, lhsT=sh, rhs=rl,
                                            start=False, stop=False)
                            last = e.matmul(out=od, lhsT=sl, rhs=rh,
                                            start=False, stop=lastk)
                return last
            body("t", _qkv)
            t_qkv = C["t"]

            def _evqkv(e, w, t_qkv=t_qkv):
                w("t", t_qkv)
                e.tensor_copy(out=qb[:], in_=pQ[:])
                e.tensor_copy(out=kb[:], in_=pK[:])
                return e.tensor_copy(out=vb[:], in_=pV[:])
            body("v", _evqkv)
            v_qkv = C["v"]
            if debug and t == 0:
                def _dqkv(e, w, v_qkv=v_qkv):
                    w("v", v_qkv)
                    dma(e, "st", dq_d[:, :], qb[:])
                    dma(e, "st", dk_d[:, :], kb[:])
                    dma(e, "st", dv_d[:, :], vb[:])
                body("st", _dqkv)
                def _dhold(e, w, snow=C["st"]):
                    w("st", snow)
                    return e.engine_nop()
                body("v", _dhold)

            def _smx(e, w, v_qkv=v_qkv):
                w("v", v_qkv)
                e.reduce_max(out=scores[:, 136:136 + NH],
                             in_=qb.rearrange("p (h d) -> p h d", d=HD), axis=AX.X)
                e.tensor_scalar(out=scrb[:], in0=kb[:], scalar1=0.0, scalar2=None,
                                op0=ALU.min)
                e.tensor_scalar(out=kb[:], in0=kb[:], scalar1=0.0,
                                scalar2=None, op0=ALU.max)
                return e.tensor_tensor(out=qb.rearrange("p (h d) -> p h d", d=HD),
                                in0=qb.rearrange("p (h d) -> p h d", d=HD),
                                in1=scores[:, 136:136 + NH].unsqueeze(2)
                                    .broadcast_to([128, NH, HD]), op=ALU.subtract)
            body("v", _smx)
            v_smx = C["v"]

            def _exp(e, w, v_smx=v_smx):
                w("v", v_smx)
                e.activation(out=qb[:], in_=qb[:], func=AF.Exp)
                return e.activation(out=scrb[:], in_=scrb[:], func=AF.Exp)
            body("a", _exp)
            a_exp = C["a"]

            def _mix1(e, w, a_exp=a_exp):
                w("a", a_exp)
                e.tensor_tensor(out=kb[:], in0=kb[:], in1=scrb[:], op=ALU.add)
                e.reduce_sum(out=scores[:, 136:136 + NH],
                             in_=qb.rearrange("p (h d) -> p h d", d=HD), axis=AX.X)
                def dstep(out_t, in_t, off):
                    e.tensor_copy(out=out_t[:, 0:off * HD], in_=in_t[:, 0:off * HD])
                    e.tensor_tensor(out=out_t[:, off * HD:D],
                                    in0=in_t[:, off * HD:D],
                                    in1=in_t[:, 0:D - off * HD], op=ALU.add)
                dstep(scrb, kb, 1)
                dstep(kc, scrb, 2)
                dstep(scrb, kc, 4)
                dstep(kc, scrb, 8)
                e.reciprocal(out=scores[:, 136:136 + NH], in_=scores[:, 136:136 + NH])
                e.tensor_scalar(out=kc[:], in0=kc[:], scalar1=EPS_ATT, scalar2=None,
                                op0=ALU.add)
                e.reciprocal(out=kc[:], in_=kc[:])
                e.tensor_tensor(out=qb.rearrange("p (h d) -> p h d", d=HD),
                                in0=qb.rearrange("p (h d) -> p h d", d=HD),
                                in1=scores[:, 136:136 + NH].unsqueeze(2)
                                    .broadcast_to([128, NH, HD]), op=ALU.mult)
                return e.tensor_tensor(out=qb[:], in0=qb[:], in1=kc[:], op=ALU.mult)
            body("v", _mix1)
            v_mix1 = C["v"]

            def _mix2(e, w, v_mix1=v_mix1):
                w("v", v_mix1)
                pbuf = [scr2, xov]

                def smult(h):
                    nh2 = h + 1
                    e.tensor_tensor(
                        out=pbuf[h % 2][:, 0:nh2 * HD].rearrange("p (a d) -> p a d", d=HD),
                        in0=qb[:, h * HD:(h + 1) * HD].unsqueeze(1)
                            .broadcast_to([128, nh2, HD]),
                        in1=kb[:, 0:nh2 * HD].rearrange("p (a d) -> p a d", d=HD),
                        op=ALU.mult)

                def sred(h):
                    nh2 = h + 1
                    e.reduce_sum(out=scores[:, TRI_OFF[h]:TRI_OFF[h] + nh2],
                                 in_=pbuf[h % 2][:, 0:nh2 * HD]
                                     .rearrange("p (a d) -> p a d", d=HD),
                                 axis=AX.X)

                for h in range(NH):
                    smult(h)
                    if h >= 1:
                        sred(h - 1)
                sred(NH - 1)

                def amult(h):
                    nh2 = h + 1
                    e.tensor_tensor(
                        out=pbuf[h % 2][:, 0:nh2 * HD].rearrange("p (a d) -> p a d", d=HD),
                        in0=vb[:, 0:nh2 * HD].rearrange("p (a d) -> p a d", d=HD),
                        in1=scores[:, TRI_OFF[h]:TRI_OFF[h] + nh2].unsqueeze(2)
                            .broadcast_to([128, nh2, HD]),
                        op=ALU.mult)

                def ared(h):
                    nh2 = h + 1
                    e.reduce_sum(
                        out=attn[:, h * HD:(h + 1) * HD],
                        in_=pbuf[h % 2][:, 0:nh2 * HD]
                            .rearrange("p (a d) -> p a d", d=HD).transpose([0, 2, 1]),
                        axis=AX.X)

                for h in range(NH):
                    amult(h)
                    if h >= 1:
                        ared(h - 1)
                ared(NH - 1)
                e.tensor_copy(out=xh[:], in_=attn[:])
                return e.tensor_tensor(out=xl[:], in0=attn[:], in1=xh[:],
                                       op=ALU.subtract)
            body("v", _mix2)
            v_mix2 = C["v"]
            if debug and t == 0:
                def _datt(e, w, v_mix2=v_mix2):
                    w("v", v_mix2)
                    dma(e, "st", dattn_d[:, :], attn[:])
                    dma(e, "st", dqn_d[:, :], qb[:])
                body("st", _datt)
                def _dhold2(e, w, snow=C["st"]):
                    w("st", snow)
                    return e.engine_nop()
                body("v", _dhold2)

            emit_T([xh[:, i * 128:(i + 1) * 128] for i in range(KS)],
                   [(xhT, i * 128) for i in range(KS)], deps=[("v", v_mix2)])
            v_aT = emit_T([xl[:, i * 128:(i + 1) * 128] for i in range(KS)],
                          [(xlT, i * 128) for i in range(KS)])

            def _omm(e, w, v_aT=v_aT):
                w("a", v_aT)
                last = None
                for ksi in range(KS):
                    sh = xhT[:, ksi * 128:(ksi + 1) * 128]
                    sl = xlT[:, ksi * 128:(ksi + 1) * 128]
                    first = ksi == 0
                    lastk = ksi == KS - 1
                    for nch in range(2):
                        rh = woh[:, ksi * D + nch * 512: ksi * D + nch * 512 + 512]
                        rl = wol[:, ksi * D + nch * 512: ksi * D + nch * 512 + 512]
                        od = pQ[:, nch * 512:(nch + 1) * 512]
                        last = e.matmul(out=od, lhsT=sh, rhs=rh, start=first, stop=False)
                        last = e.matmul(out=od, lhsT=sh, rhs=rl, start=False, stop=False)
                        last = e.matmul(out=od, lhsT=sl, rhs=rh, start=False, stop=lastk)
                return last
            body("t", _omm)
            t_o = C["t"]

            def _oev(e, w, t_o=t_o, xt=xt, sg=st_guard[1]):
                w("t", t_o)
                if sg:
                    e.wait_ge(SEM["st"], sg * 16)
                e.scalar_tensor_tensor(out=xov[:, 0:512], in0=pQ[:, 0:512],
                                       scalar=1.0, in1=bo_rep[:, 0:512],
                                       op0=ALU.mult, op1=ALU.add)
                e.scalar_tensor_tensor(out=xov[:, 512:1024], in0=pQ[:, 512:1024],
                                       scalar=1.0, in1=bo_rep[:, 512:1024],
                                       op0=ALU.mult, op1=ALU.add)
                return e.tensor_tensor(out=xov[:], in0=xov[:], in1=xt[:], op=ALU.add)
            body("v", _oev)
            a_resid_v[t] = C["v"]
            v_resid = C["v"]

            def _stx(e, w, t=t, v_resid=v_resid):
                w("v", v_resid)
                dma(e, "st", xmid_d[t * 128:(t + 1) * 128, :], xov[:])
                if debug:
                    dma(e, "st", dxmid_d[t * 128:(t + 1) * 128, :], xov[:])
            body("st", _stx)
            st_guard[1] = C["st"]
            xmid_st[t] = C["st"]

        A_T_DONE = C["t"]
        es_a.close()
        wgch = [sb(f"wgch{i}", [128, KS * 512], BF16) for i in range(2)]
        wgcl = [sb(f"wgcl{i}", [128, KS * 512], BF16) for i in range(2)]
        wuch = [sb(f"wuch{i}", [128, KS * 512], BF16) for i in range(2)]
        wucl = [sb(f"wucl{i}", [128, KS * 512], BF16) for i in range(2)]
        wdch = [sb(f"wdch{i}", [128, 4 * D], BF16) for i in range(2)]
        wdcl = [sb(f"wdcl{i}", [128, 4 * D], BF16) for i in range(2)]
        xqTh = [sb(f"xqTh{i}", [128, D], BF16) for i in range(2)]
        xqTl = [sb(f"xqTl{i}", [128, D], BF16) for i in range(2)]
        xmidb = [sb(f"xmidb{i}", [128, D]) for i in range(2)]
        hsc = sb("hsc", [128, 512])
        hsc2 = sb("hsc2", [128, 512])
        qxh2 = sb("qxh2", [128, 512], BF16)
        qxl2 = sb("qxl2", [128, 512], BF16)
        hTh = sb("hTh", [128, 512], BF16)
        hTl = sb("hTl", [128, 512], BF16)
        ybuf = sb("ybuf", [128, D])
        scr = qxq

        CH_OFF = [0]
        for c in CHUNKS:
            CH_OFF.append(CH_OFF[-1] + c)

        wbuf_guard = {0: 0, 1: 0}
        ld_chunk = {}
        ypair_guard = [0]
        n_pairs = n_tiles // 2 if "B" in phases else 0
        for p in range(n_pairs):
            tiles = [2 * p, 2 * p + 1]
            for j, t in enumerate(tiles):
                def _ldm(e, w, t=t, j=j, vg=C["v"], xst=xmid_st[t]):
                    e.wait_ge(SEM["v"], vg)
                    e.wait_ge(SEM["st"], xst * 16)
                    dma(e, "ld", xmidb[j][:], xmid_d[t * 128:(t + 1) * 128, :])
                body("ld", _ldm)
                ld_m = ld_mark()

                def _lnv(e, w, ld_m=ld_m, j=j):
                    w("ld", ld_m)
                    e.reduce_sum(out=stt[:, 0:1], in_=xmidb[j][:], axis=AX.X)
                    e.tensor_tensor(out=qu[:, 0:D], in0=xmidb[j][:], in1=xmidb[j][:],
                                    op=ALU.mult)
                    e.reduce_sum(out=stt[:, 1:2], in_=qu[:, 0:D], axis=AX.X)
                    e.tensor_scalar(out=stt[:, 2:3], in0=stt[:, 0:1], scalar1=1.0 / D,
                                    scalar2=None, op0=ALU.mult)
                    vsp(e)
                    e.tensor_scalar(out=stt[:, 3:4], in0=stt[:, 1:2], scalar1=1.0 / D,
                                    scalar2=EPS_LN, op0=ALU.mult, op1=ALU.add)
                    e.tensor_tensor(out=stt[:, 4:5], in0=stt[:, 2:3], in1=stt[:, 2:3],
                                    op=ALU.mult)
                    vsp(e)
                    e.tensor_tensor(out=stt[:, 5:6], in0=stt[:, 3:4],
                                    in1=stt[:, 4:5], op=ALU.subtract)
                    return vsp(e)
                body("v", _lnv)
                v_var = C["v"]

                def _lna2(e, w, v_var=v_var):
                    w("v", v_var)
                    e.activation(out=stt[:, 6:7], in_=stt[:, 5:6], func=AF.Ln)
                    asp(e, 2)
                    e.activation(out=stt[:, 7:8], in_=stt[:, 6:7], func=AF.Exp,
                                 scale=-0.5)
                    return asp(e, 2)
                body("a", _lna2)
                a_sqrt = C["a"]

                def _lnv3(e, w, a_sqrt=a_sqrt, j=j):
                    w("a", a_sqrt)
                    vsp(e)
                    e.tensor_tensor(out=stt[:, 8:9], in0=stt[:, 2:3], in1=stt[:, 7:8],
                                    op=ALU.mult)
                    vsp(e)
                    return e.tensor_scalar(out=qv[:, 0:D], in0=xmidb[j][:],
                                           scalar1=stt[:, 7:8], scalar2=stt[:, 8:9],
                                           op0=ALU.mult, op1=ALU.subtract)
                body("v", _lnv3)
                emit_quant(D, tdep=C["t"])
                v_q = C["v"]
                emit_T([qxh[:, i * 128:(i + 1) * 128] for i in range(KS)],
                       [(xqTh[j], i * 128) for i in range(KS)], deps=[("v", v_q)])
                emit_T([qxl[:, i * 128:(i + 1) * 128] for i in range(KS)],
                       [(xqTl[j], i * 128) for i in range(KS)])
                if debug and p == 0 and j == 0:
                    def _dq(e, w):
                        return e.tensor_copy(out=ybuf[:], in_=qxq[:, 0:D])
                    body("v", _dq)
                    dv2 = C["v"]
                    def _dq2(e, w, dv2=dv2):
                        w("v", dv2)
                        dma(e, "st", dxq_d[:, 0:D], ybuf[:])
                    body("st", _dq2)

            v_xq_pair = C["a"]

            steps = []
            for ci, CW in enumerate(CHUNKS):
                for j, t in enumerate(tiles):
                    steps.append((ci, CW, j, t))

            st_state = {}

            def b_quant(si):
                (ci, CW, j, t) = steps[si]
                m = st_state[si]
                bqh = qxh if si % 2 == 0 else qxh2
                bql = qxl if si % 2 == 0 else qxl2
                emit_quant(CW, qsrc=qvb[si % 2], qh=bqh, ql=bql,
                           tdep=st_state[si - 2]["t_tr"] if si >= 2 else 0)
                m["v_hq"] = C["v"]

            def b_tr(si):
                (ci, CW, j, t) = steps[si]
                cb = ci % 2
                m = st_state[si]
                bqh = qxh if si % 2 == 0 else qxh2
                bql = qxl if si % 2 == 0 else qxl2
                nsubk = CW // 128
                emit_T([bqh[:, i * 128:(i + 1) * 128] for i in range(nsubk)],
                       [(hTh, i * 128) for i in range(nsubk)],
                       deps=[("v", m["v_hq"])])
                m["v_evh"] = emit_T(
                    [bql[:, i * 128:(i + 1) * 128] for i in range(nsubk)],
                    [(hTl, i * 128) for i in range(nsubk)])
                m["t_tr"] = C["t"]

                pD = pV if j == 0 else pQ
                def _dn(e, w, v_hT=m["v_evh"], cb=cb, CW=CW, ci=ci, pD=pD):
                    w("a", v_hT)
                    last = None
                    nks = CW // 128
                    for ksi in range(nks):
                        sh = hTh[:, ksi * 128:(ksi + 1) * 128]
                        sl = hTl[:, ksi * 128:(ksi + 1) * 128]
                        first = (ci == 0 and ksi == 0)
                        lastk = (ci == len(CHUNKS) - 1 and ksi == nks - 1)
                        for nch in range(2):
                            rh = wdch[cb][:, ksi * D + nch * 512: ksi * D + nch * 512 + 512]
                            rl = wdcl[cb][:, ksi * D + nch * 512: ksi * D + nch * 512 + 512]
                            od = pD[:, nch * 512:(nch + 1) * 512]
                            last = e.matmul(out=od, lhsT=sh, rhs=rh,
                                            start=first, stop=False)
                            last = e.matmul(out=od, lhsT=sh, rhs=rl,
                                            start=False, stop=False)
                            last = e.matmul(out=od, lhsT=sl, rhs=rh,
                                            start=False, stop=lastk)
                    return last
                body("t", _dn)
                if j == 1:
                    wbuf_guard[cb] = C["t"]

            for si, (ci, CW, j, t) in enumerate(steps):
                n0 = CH_OFF[ci]
                cb = ci % 2
                if j == 0:
                    def _ldc(e, w, ci=ci, n0=n0, CW=CW, cb=cb,
                             tg=wbuf_guard[cb], pst=PREP_ST, atd=A_T_DONE):
                        e.wait_ge(SEM["st"], pst * 16)
                        e.wait_ge(SEM["t"], max(tg, atd))
                        for i in range(KS):
                            dma(e, "ld", wgch[cb][:, i * 512:i * 512 + CW],
                                wgqh_d[i, :, n0:n0 + CW])
                            dma(e, "ld", wgcl[cb][:, i * 512:i * 512 + CW],
                                wgql_d[i, :, n0:n0 + CW])
                            dma(e, "ld", wuch[cb][:, i * 512:i * 512 + CW],
                                wuqh_d[i, :, n0:n0 + CW])
                            dma(e, "ld", wucl[cb][:, i * 512:i * 512 + CW],
                                wuql_d[i, :, n0:n0 + CW])
                        for jj in range(CW // 128):
                            dma(e, "ld", wdch[cb][:, jj * D:(jj + 1) * D],
                                wdqh_d[n0 // 128 + jj, :, :])
                            dma(e, "ld", wdcl[cb][:, jj * D:(jj + 1) * D],
                                wdql_d[n0 // 128 + jj, :, :])
                    body("ld", _ldc)
                    ld_chunk[ci] = ld_mark()

                prev = st_state.get(si - 1)
                def _gu(e, w, j=j, cb=cb, CW=CW, ldc=ld_chunk[ci], prev=prev,
                        vxq=v_xq_pair):
                    w("ld", ldc)
                    if prev is not None:
                        w("v", prev["v_guprod"])
                    w("a", vxq if prev is None else max(vxq, prev["a_eneg"]))
                    last = None
                    for ksi in range(KS):
                        sh = xqTh[j][:, ksi * 128:(ksi + 1) * 128]
                        sl = xqTl[j][:, ksi * 128:(ksi + 1) * 128]
                        first = ksi == 0
                        lastk = ksi == KS - 1
                        for (wch_, wcl_, po) in ((wgch[cb], wgcl[cb], 0),
                                                 (wuch[cb], wucl[cb], 512)):
                            rh = wch_[:, ksi * 512:ksi * 512 + CW]
                            rl = wcl_[:, ksi * 512:ksi * 512 + CW]
                            od = pK[:, po:po + CW]
                            last = e.matmul(out=od, lhsT=sh, rhs=rh,
                                            start=first, stop=False)
                            last = e.matmul(out=od, lhsT=sh, rhs=rl,
                                            start=False, stop=False)
                            last = e.matmul(out=od, lhsT=sl, rhs=rh,
                                            start=False, stop=lastk)
                    return last
                body("t", _gu)
                t_gu = C["t"]

                if PIPE_B:
                    if si >= 1:
                        b_quant(si - 1)
                    if si >= 2:
                        b_tr(si - 2)

                def _eneg(e, w, t_gu=t_gu, CW=CW, prev=prev):
                    w("t", t_gu)
                    if prev is not None:
                        w("v", prev["v_h2"])
                    return e.activation(out=hsc[:, 0:CW], in_=pK[:, 0:CW],
                                        func=AF.Exp, scale=-1.0)
                body("a", _eneg)
                a_eneg = C["a"]

                def _h(e, w, t_gu=t_gu, CW=CW):
                    w("t", t_gu)
                    e.tensor_copy(out=hsc2[:, 0:CW], in_=pK[:, 512:512 + CW])
                    return e.tensor_tensor(out=hsc2[:, 0:CW], in0=pK[:, 0:CW],
                                           in1=hsc2[:, 0:CW], op=ALU.mult)
                body("v", _h)
                v_guprod = C["v"]

                def _h2(e, w, a_eneg=a_eneg, CW=CW, si=si):
                    w("a", a_eneg)
                    e.tensor_scalar(out=hsc[:, 0:CW], in0=hsc[:, 0:CW],
                                    scalar1=1.0, scalar2=None, op0=ALU.add)
                    e.reciprocal(out=hsc[:, 0:CW], in_=hsc[:, 0:CW])
                    return e.tensor_tensor(out=qvb[si % 2][:, 0:CW],
                                           in0=hsc2[:, 0:CW],
                                           in1=hsc[:, 0:CW], op=ALU.mult)
                body("v", _h2)
                v_h2 = C["v"]
                st_state[si] = {"v_guprod": v_guprod, "a_eneg": a_eneg,
                                "v_h2": v_h2}
                if not PIPE_B:
                    b_quant(si)
                    b_tr(si)
            if PIPE_B:
                b_quant(len(steps) - 1)
                b_tr(len(steps) - 2)
                b_tr(len(steps) - 1)

            for j, t in enumerate(tiles):
                pD = pV if j == 0 else pQ
                t_done = C["t"]

                def _yev(e, w, pD=pD, j=j, t_done=t_done, yg=ypair_guard[0]):
                    w("t", t_done)
                    if yg:
                        e.wait_ge(SEM["st"], yg * 16)
                    e.tensor_tensor(out=ybuf[:, 0:512], in0=pD[:, 0:512],
                                    in1=xmidb[j][:, 0:512], op=ALU.add)
                    return e.tensor_tensor(out=ybuf[:, 512:1024], in0=pD[:, 512:1024],
                                           in1=xmidb[j][:, 512:1024], op=ALU.add)
                body("v", _yev)
                v_y = C["v"]

                def _sty(e, w, t=t, v_y=v_y):
                    w("v", v_y)
                    dma(e, "st", y_d[t * 128:(t + 1) * 128, :], ybuf[:])
                body("st", _sty)
                ypair_guard[0] = C["st"]

        if "B" not in phases:
            def _ydummy(e, w):
                w("v", None)
                for t in range(n_tiles):
                    dma(e, "st", y_d[t * 128:(t + 1) * 128, :], ybuf[:])
            body("st", _ydummy)

        def _fin(e, w):
            e.wait_ge(SEM["st"], C["st"] * 16)
            return e.engine_nop()
        body("g", _fin)

    return nc


def kernel(L0_attn_x, L0_attn_Wq, L0_attn_Wk, L0_attn_Wv, L0_attn_Wo,
           L0_attn_bo, L0_ffn_w_gate, L0_ffn_w_up, L0_ffn_w_down):
    x = np.ascontiguousarray(np.asarray(L0_attn_x, dtype=np.float32).reshape(-1, D))
    wq = np.ascontiguousarray(np.asarray(L0_attn_Wq, dtype=np.float32))
    wk = np.ascontiguousarray(np.asarray(L0_attn_Wk, dtype=np.float32))
    wv = np.ascontiguousarray(np.asarray(L0_attn_Wv, dtype=np.float32))
    wo = np.ascontiguousarray(np.asarray(L0_attn_Wo, dtype=np.float32))
    bo = np.ascontiguousarray(np.asarray(L0_attn_bo, dtype=np.float32).reshape(1, D))
    wg = np.ascontiguousarray(np.asarray(L0_ffn_w_gate, dtype=np.float32))
    wu = np.ascontiguousarray(np.asarray(L0_ffn_w_up, dtype=np.float32))
    wd = np.ascontiguousarray(np.asarray(L0_ffn_w_down, dtype=np.float32))

    nc = build_program()
    in_maps = []
    for i in range(N_CORES):
        in_maps.append({
            "x": x[i * TOK_PER_CORE:(i + 1) * TOK_PER_CORE],
            "wq": wq, "wk": wk, "wv": wv, "wo": wo, "bo2": bo,
            "wg": wg, "wu": wu, "wd": wd,
        })
    res = run_bass_kernel_spmd(nc, in_maps, core_ids=list(range(N_CORES)))
    y = np.concatenate([np.asarray(r["y"]) for r in res.results], axis=0)
    return y.reshape(4, 4096, D).astype(np.float32)


# version 11
# speedup vs baseline: 1.1606x; 1.1606x over previous
import numpy as np
import concourse.bass as bass
import concourse.mybir as mybir
from concourse.bass_utils import run_bass_kernel_spmd

F32 = mybir.dt.float32
BF16 = mybir.dt.bfloat16
AF = mybir.ActivationFunctionType
ALU = mybir.AluOpType
AX = mybir.AxisListType

N_CORES = 8
D = 1024
NH = 16
HD = 64
I = 2816
TOK_PER_CORE = 2048
N_TILES = TOK_PER_CORE // 128
KS = D // 128
NT_W = I // 128
CHUNKS = [512, 512, 512, 512, 384, 384]
assert sum(CHUNKS) == I
EPS_LN = 1e-5
EPS_ATT = 1e-6
QTHR = [0.25, 0.75, 1.25, 1.75, 2.5, 3.5, 5.0]
PIPE_PREP = True
PIPE_B = True


def build_program(n_tiles=N_TILES, debug=False, phases="PAB"):
    nc = bass.Bass("TRN2", target_bir_lowering=False, detect_race_conditions=False)
    ntok = n_tiles * 128

    x_d = nc.dram_tensor("x", [ntok, D], F32, kind="ExternalInput").ap()
    wq_d = nc.dram_tensor("wq", [D, D], F32, kind="ExternalInput").ap()
    wk_d = nc.dram_tensor("wk", [D, D], F32, kind="ExternalInput").ap()
    wv_d = nc.dram_tensor("wv", [D, D], F32, kind="ExternalInput").ap()
    wo_d = nc.dram_tensor("wo", [D, D], F32, kind="ExternalInput").ap()
    bo_d = nc.dram_tensor("bo2", [1, D], F32, kind="ExternalInput").ap()
    wg_d = nc.dram_tensor("wg", [I, D], F32, kind="ExternalInput").ap()
    wu_d = nc.dram_tensor("wu", [I, D], F32, kind="ExternalInput").ap()
    wd_d = nc.dram_tensor("wd", [D, I], F32, kind="ExternalInput").ap()
    y_d = nc.dram_tensor("y", [ntok, D], F32, kind="ExternalOutput").ap()
    if debug:
        dxmid_d = nc.dram_tensor("dxmid", [ntok, D], F32, kind="ExternalOutput").ap()
        dwg_d = nc.dram_tensor("dwg", [128, I], F32, kind="ExternalOutput").ap()
        dwgl_d = nc.dram_tensor("dwgl", [128, I], F32, kind="ExternalOutput").ap()
        dxq_d = nc.dram_tensor("dxq", [128, D], F32, kind="ExternalOutput").ap()
        dxln_d = nc.dram_tensor("dxln", [128, D], F32, kind="ExternalOutput").ap()
        dq_d = nc.dram_tensor("dq", [128, D], F32, kind="ExternalOutput").ap()
        dk_d = nc.dram_tensor("dk", [128, D], F32, kind="ExternalOutput").ap()
        dv_d = nc.dram_tensor("dv", [128, D], F32, kind="ExternalOutput").ap()
        dattn_d = nc.dram_tensor("dattn", [128, D], F32, kind="ExternalOutput").ap()
        dqn_d = nc.dram_tensor("dqn", [128, D], F32, kind="ExternalOutput").ap()
        dstt_d = nc.dram_tensor("dstt", [128, 8], F32, kind="ExternalOutput").ap()
        dxt_d = nc.dram_tensor("dxt", [128, D], F32, kind="ExternalOutput").ap()

    xmid_d = nc.dram_tensor("xmid", [ntok, D], F32, kind="Internal").ap()
    wgqh_d = nc.dram_tensor("wgqh", [KS, 128, I], BF16, kind="Internal").ap()
    wgql_d = nc.dram_tensor("wgql", [KS, 128, I], BF16, kind="Internal").ap()
    wuqh_d = nc.dram_tensor("wuqh", [KS, 128, I], BF16, kind="Internal").ap()
    wuql_d = nc.dram_tensor("wuql", [KS, 128, I], BF16, kind="Internal").ap()
    wdqh_d = nc.dram_tensor("wdqh", [NT_W, 128, D], BF16, kind="Internal").ap()
    wdql_d = nc.dram_tensor("wdql", [NT_W, 128, D], BF16, kind="Internal").ap()

    from contextlib import ExitStack
    es = ExitStack()
    with es:
        sb = lambda name, shape, dt=F32: es.enter_context(nc.sbuf_tensor(name, shape, dt))
        ps = lambda name, shape, dt=F32: es.enter_context(nc.psum_tensor(name, shape, dt))
        sem = lambda name: es.enter_context(nc.semaphore(name))

        LD_SEMS = [sem(f"s_ld{i}") for i in range(4)]
        C_LD = [0, 0, 0, 0]
        ld_rr = [0]
        s_ld = LD_SEMS[0]
        s_st = sem("s_st")
        s_v = sem("s_v")
        s_a = sem("s_a")
        s_t = sem("s_t")
        s_g = sem("s_g")
        C = {"st": 0, "v": 0, "a": 0, "t": 0, "g": 0}
        SEM = {"st": s_st, "v": s_v, "a": s_a, "t": s_t, "g": s_g}
        ENG = {"v": "vector", "a": "scalar", "t": "tensor", "g": "gpsimd",
               "ld": "sync", "st": "gpsimd"}

        nums = sorted([h.num for h in SEM.values()] + [h.num for h in LD_SEMS])
        nc.all_engine_barrier()
        nc.gpsimd.dma_reset(range(nums[0], nums[-1] + 1))
        nc.gpsimd.sem_clear(range(nums[0], nums[-1] + 1))
        nc.all_engine_barrier()
        nc.all_engine_barrier()

        block = es.enter_context(nc.Block())

        def body(kind, fn):
            def runner(e):
                def w(dep, cnt=None):
                    if dep == "ld":
                        wld(e, ld_mark() if cnt is None else cnt)
                        return
                    c = C[dep] if cnt is None else cnt
                    if c > 0:
                        e.wait_ge(SEM[dep], c * 16 if dep == "st" else c)
                last = fn(e, w)
                if kind in ("ld", "st"):
                    pass
                else:
                    assert last is not None, "compute body must return last instr"
                    if kind == "v":
                        e.engine_nop().then_inc(SEM[kind], 1)
                    elif kind == "g":
                        e.memset(junkg[:], 0.0).then_inc(SEM[kind], 1)
                    else:
                        last.then_inc(SEM[kind], 1)
                    C[kind] += 1
            getattr(block, ENG[kind])(runner)

        LAST_LD = [(0, 0)]

        def dma(e, kind, out, in_):
            if kind == "ld":
                i = ld_rr[0] % 4
                ld_rr[0] += 1
                e.dma_start(out=out, in_=in_).then_inc(LD_SEMS[i], 16)
                C_LD[i] += 1
                LAST_LD[0] = (i, C_LD[i])
            else:
                e.dma_start(out=out, in_=in_).then_inc(SEM[kind], 16)
                C[kind] += 1

        def ld_mark():
            return LAST_LD[0]

        def wld(e, mark):
            i, c = mark
            if c > 0:
                e.wait_ge(LD_SEMS[i], c * 16)

        def vsp(e, n=2):
            last = None
            for _ in range(n):
                last = e.tensor_copy(out=junk[:], in_=junk[:])
            return last

        def asp(e, n=1):
            last = None
            for _ in range(n):
                last = e.activation(out=junka[:], in_=junka[:], func=AF.Copy)
            return last

        bo_rep = sb("bo_rep", [128, D])
        idb = sb("idb", [128, 128], BF16)
        c_eps = sb("c_eps", [128, 1])
        scores = sb("scores", [128, 160])
        stt = sb("stt", [128, 16])
        junk = sb("junk", [128, 128])
        junka = sb("junka", [128, 64])
        junkg = sb("junkg", [128, 4])

        qv = sb("qv", [128, D])
        qv2 = sb("qv2", [128, D])
        qu = sb("qu", [128, D])
        qsg = sb("qsg", [128, D], BF16)
        mA = sb("mA", [128, D], BF16); mB = sb("mB", [128, D], BF16)
        mC = sb("mC", [128, D], BF16)
        qmb = sb("qmb", [128, D // 16])
        qsc = sb("qsc", [128, D // 16])
        qrc = sb("qrc", [128, D // 16])
        qxq = sb("qxq", [128, D])
        qxh = sb("qxh", [128, D], BF16); qxl = sb("qxl", [128, D], BF16)

        es_a = ExitStack()
        sba = lambda name, shape, dt=F32: es_a.enter_context(nc.sbuf_tensor(name, shape, dt))
        stage = sba("stage", [128, D], BF16)
        stage2 = sba("stage2", [128, D], BF16)
        wqh = sba("wqh", [128, KS * D], BF16); wql = sba("wql", [128, KS * D], BF16)
        wkh = sba("wkh", [128, KS * D], BF16); wkl = sba("wkl", [128, KS * D], BF16)
        wvh = sba("wvh", [128, KS * D], BF16); wvl = sba("wvl", [128, KS * D], BF16)
        woh = sba("woh", [128, KS * D], BF16); wol = sba("wol", [128, KS * D], BF16)
        xa = sba("xa", [128, D]); xb = sba("xb", [128, D])
        qb = sba("qb", [128, D]); kb = sba("kb", [128, D]); vb = sba("vb", [128, D])
        kc = sba("kc", [128, D]); attn = sba("attn", [128, D])
        xh = sba("xh", [128, D], BF16); xl = sba("xl", [128, D], BF16)
        xhT = sba("xhT", [128, D], BF16); xlT = sba("xlT", [128, D], BF16)
        xov = sba("xov", [128, D])
        scrb = attn
        scr = qxq
        scr2 = qv

        pTA = ps("pTA", [128, 512])
        pTB = ps("pTB", [128, 512])
        pQ = ps("pQ", [128, D])
        pK = ps("pK", [128, D])
        pV = ps("pV", [128, D])

        def _setup_sync(e, w):
            dma(e, "ld", bo_rep[:], bo_d.partition_broadcast(128))
        body("ld", _setup_sync)

        def _setup_g(e, w):
            e.memset(idb[:], 0.0).then_inc(SEM["g"], 1)
            C["g"] += 1
            e.wait_ge(SEM["g"], C["g"])
            return e.affine_select(out=idb[:], in_=idb[:], compare_op=ALU.not_equal,
                                   fill=1.0, base=0, pattern=[[-1, 128]],
                                   channel_multiplier=1)
        body("g", _setup_g)
        G_SETUP = C["g"]

        def _setup_v(e, w):
            e.memset(c_eps[:], EPS_LN)
            e.memset(junka[:], 0.0)
            return e.memset(junk[:], 0.0)
        body("v", _setup_v)
        V_SETUP = C["v"]

        wd_pairs = [(wq_d, wqh, wql), (wk_d, wkh, wkl),
                    (wv_d, wvh, wvl), (wo_d, woh, wol)]
        raws = [qv, qu]
        last_split_v = {}
        idx = 0
        for (wdram, whi, wlo) in wd_pairs:
            for ksi in range(KS):
                rb = raws[idx % 2]
                vcnt_guard = last_split_v.get(idx % 2, 0)

                def _ld(e, w, rb=rb, wdram=wdram, ksi=ksi, vg=vcnt_guard):
                    if vg:
                        e.wait_ge(SEM["v"], vg)
                    dma(e, "ld", rb[:, 0:D], wdram[ksi * 128:(ksi + 1) * 128, :])
                body("ld", _ld)
                ld_now = ld_mark()

                def _sp(e, w, rb=rb, whi=whi, wlo=wlo, ksi=ksi, ld_now=ld_now):
                    w("ld", ld_now)
                    e.tensor_copy(out=whi[:, ksi * D:(ksi + 1) * D], in_=rb[:, 0:D])
                    return e.tensor_tensor(out=wlo[:, ksi * D:(ksi + 1) * D],
                                           in0=rb[:, 0:D],
                                           in1=whi[:, ksi * D:(ksi + 1) * D],
                                           op=ALU.subtract)
                body("v", _sp)
                last_split_v[idx % 2] = C["v"]
                idx += 1

        def emit_quant(F, qsrc=None, tdep=0, qh=None, ql=None):
            nb = F // 16
            if qsrc is None:
                qsrc = qv
            if qh is None:
                qh = qxh
            if ql is None:
                ql = qxl

            def _q1(e, w):
                if tdep:
                    e.wait_ge(SEM["t"], tdep)
                e.reduce_max(out=qmb[:, 0:nb],
                             in_=qsrc[:, 0:F].rearrange("p (b s) -> p b s", s=16),
                             axis=AX.X, apply_absolute_value=True)
                vsp(e)
                e.tensor_scalar(out=qsc[:, 0:nb], in0=qmb[:, 0:nb],
                                scalar1=1.0 / 6.0, scalar2=1e-12,
                                op0=ALU.mult, op1=ALU.max)
                vsp(e)
                e.reciprocal(out=qrc[:, 0:nb], in_=qsc[:, 0:nb])
                vsp(e)
                return e.tensor_tensor(
                    out=qsrc[:, 0:F].rearrange("p (b s) -> p b s", s=16),
                    in0=qsrc[:, 0:F].rearrange("p (b s) -> p b s", s=16),
                    in1=qrc[:, 0:nb].unsqueeze(2).broadcast_to([128, nb, 16]),
                    op=ALU.mult)
            body("v", _q1)
            v_q1 = C["v"]

            def _qa(e, w):
                w("v", v_q1)
                e.activation(out=qu[:, 0:F], in_=qsrc[:, 0:F], func=AF.Abs)
                return e.activation(out=qsg[:, 0:F], in_=qsrc[:, 0:F], func=AF.Tanh,
                                    scale=1e30)
            body("a", _qa)
            a_q = C["a"]

            def _q2(e, w):
                w("a", a_q)
                u = qu[:, 0:F]
                sp = (F < 512)
                e.tensor_scalar(out=mA[:, 0:F], in0=u, scalar1=QTHR[0], scalar2=None, op0=ALU.is_gt)
                e.tensor_scalar(out=mB[:, 0:F], in0=u, scalar1=QTHR[1], scalar2=None, op0=ALU.is_gt)
                if sp: vsp(e)
                e.tensor_tensor(out=mA[:, 0:F], in0=mA[:, 0:F], in1=mB[:, 0:F], op=ALU.add)
                e.tensor_scalar(out=mB[:, 0:F], in0=u, scalar1=QTHR[2], scalar2=None, op0=ALU.is_gt)
                e.tensor_scalar(out=mC[:, 0:F], in0=u, scalar1=QTHR[3], scalar2=None, op0=ALU.is_gt)
                if sp: vsp(e)
                e.tensor_tensor(out=mB[:, 0:F], in0=mB[:, 0:F], in1=mC[:, 0:F], op=ALU.add)
                if sp: vsp(e)
                e.tensor_tensor(out=mA[:, 0:F], in0=mA[:, 0:F], in1=mB[:, 0:F], op=ALU.add)
                e.tensor_scalar(out=mB[:, 0:F], in0=u, scalar1=QTHR[4], scalar2=None, op0=ALU.is_gt)
                e.tensor_scalar(out=mC[:, 0:F], in0=u, scalar1=QTHR[5], scalar2=None, op0=ALU.is_gt)
                if sp: vsp(e)
                e.tensor_tensor(out=mB[:, 0:F], in0=mB[:, 0:F], in1=mC[:, 0:F], op=ALU.add)
                e.tensor_scalar(out=mC[:, 0:F], in0=u, scalar1=QTHR[6], scalar2=None, op0=ALU.is_gt)
                if sp: vsp(e)
                e.scalar_tensor_tensor(out=mA[:, 0:F], in0=mA[:, 0:F], scalar=0.5,
                                       in1=mB[:, 0:F], op0=ALU.mult, op1=ALU.add)
                if sp: vsp(e)
                e.scalar_tensor_tensor(out=mA[:, 0:F], in0=mC[:, 0:F], scalar=2.0,
                                       in1=mA[:, 0:F], op0=ALU.mult, op1=ALU.add)
                if sp: vsp(e)
                e.tensor_tensor(out=mA[:, 0:F], in0=mA[:, 0:F], in1=qsg[:, 0:F], op=ALU.mult)
                if sp: vsp(e)
                e.tensor_tensor(out=qxq[:, 0:F].rearrange("p (b s) -> p b s", s=16),
                                in0=mA[:, 0:F].rearrange("p (b s) -> p b s", s=16),
                                in1=qsc[:, 0:nb].unsqueeze(2).broadcast_to([128, nb, 16]),
                                op=ALU.mult)
                if sp: vsp(e)
                e.tensor_copy(out=qh[:, 0:F], in_=qxq[:, 0:F])
                if sp: vsp(e)
                return e.tensor_tensor(out=ql[:, 0:F], in0=qxq[:, 0:F],
                                       in1=qh[:, 0:F], op=ALU.subtract)
            body("v", _q2)

        bank_free = {0: 0, 1: 0}

        def emit_T(srcs, dsts, deps=(), st_dep=0):
            gi = 0
            out_a = 0
            for g0 in range(0, len(srcs), 4):
                gs = list(range(g0, min(g0 + 4, len(srcs))))
                bank = pTA if gi % 2 == 0 else pTB
                bsel = gi % 2
                gdeps = list(deps) if g0 == 0 else []
                bf = bank_free[bsel]

                def _tr(e, w, gs=gs, bank=bank, gdeps=gdeps, bf=bf, srcs=srcs):
                    for dk, dc in gdeps:
                        w(dk, dc)
                    if bf:
                        w("v", bf)
                    last = None
                    for i, si_ in enumerate(gs):
                        last = e.matmul(out=bank[:, i * 128:(i + 1) * 128],
                                        lhsT=srcs[si_], rhs=idb[:],
                                        start=(i == 0), stop=(i == len(gs) - 1))
                    return last
                body("t", _tr)
                t_now = C["t"]

                dt0, dc0 = dsts[g0]
                def _ev(e, w, t_now=t_now, bank=bank, n=len(gs), dt0=dt0,
                        dc0=dc0, st_dep=st_dep):
                    if st_dep:
                        e.wait_ge(SEM["st"], st_dep * 16)
                    w("t", t_now)
                    return e.tensor_copy(out=dt0[:, dc0:dc0 + n * 128],
                                         in_=bank[:, 0:n * 128])
                body("v", _ev)
                bank_free[bsel] = C["v"]
                out_a = C["v"]
                gi += 1
            return out_a

        st_guard = {0: 0, 1: 0}
        prep_items = []
        for (wnat_d, dsth, dstl) in [(wg_d, wgqh_d, wgql_d), (wu_d, wuqh_d, wuql_d)]:
            for nt in range(NT_W):
                prep_items.append((wnat_d, nt, 0, D, dsth, dstl,
                                   [(i, nt * 128) for i in range(KS)]))
        WD_SLICES = [(0, 1024), (1024, 1024), (2048, 768)]
        for nt in range(KS):
            for (s0, SW) in WD_SLICES:
                prep_items.append((wd_d, nt, s0, SW, wdqh_d, wdql_d,
                                   [(s0 // 128 + i, nt * 128) for i in range(SW // 128)]))

        prep_m = {}
        def prep_back(n):
            (wnat_d, nt, s0, SW, dsth, dstl, blocks) = prep_items[n]
            m = prep_m[n]
            nblk = len(blocks)
            emit_T([qxh[:, i * 128:(i + 1) * 128] for i in range(nblk)],
                   [(stage, i * 128) for i in range(nblk)],
                   deps=[("v", m["v_q"])], st_dep=st_guard[0])
            v_ev = emit_T([qxl[:, i * 128:(i + 1) * 128] for i in range(nblk)],
                          [(stage2, i * 128) for i in range(nblk)],
                          st_dep=st_guard[0])
            m["t_tr"] = C["t"]

            def _spill(e, w, blocks=blocks, dsth=dsth, dstl=dstl, v_ev=v_ev):
                w("v", v_ev)
                for bi, (ksd, co) in enumerate(blocks):
                    dma(e, "st", dsth[ksd, :, co:co + 128],
                        stage[:, bi * 128:(bi + 1) * 128])
                    dma(e, "st", dstl[ksd, :, co:co + 128],
                        stage2[:, bi * 128:(bi + 1) * 128])
            body("st", _spill)
            st_guard[0] = C["st"]

        qvb = [qv, qv2]
        S1_V_DONE = C["v"]
        for n, (wnat_d, nt, s0, SW, dsth, dstl, blocks) in enumerate(prep_items):
            pm = prep_m.get(n - 2)
            def _ldw(e, w, wnat_d=wnat_d, nt=nt, s0=s0, SW=SW, n=n, pm=pm):
                if pm is not None:
                    e.wait_ge(SEM["v"], pm["v_q1"])
                    e.wait_ge(SEM["a"], pm["a_qa"])
                else:
                    e.wait_ge(SEM["v"], S1_V_DONE)
                dma(e, "ld", qvb[n % 2][:, 0:SW],
                    wnat_d[nt * 128:(nt + 1) * 128, s0:s0 + SW])
            body("ld", _ldw)
            ld_now = ld_mark()

            if PIPE_PREP and n >= 1:
                prep_back(n - 1)

            def _qdep(e, w, ld_now=ld_now):
                w("ld", ld_now)
                return e.engine_nop()
            body("v", _qdep)
            v_q1_pre = C["v"]
            emit_quant(SW, qsrc=qvb[n % 2],
                       tdep=(prep_m[n - 1]["t_tr"] if (PIPE_PREP and n >= 1) else 0))
            prep_m[n] = {"v_q": C["v"], "v_q1": v_q1_pre + 2,
                         "a_qa": C["a"]}
            if not PIPE_PREP:
                prep_back(n)
        if PIPE_PREP:
            prep_back(len(prep_items) - 1)

        PREP_ST = C["st"]
        if debug:
            def _dldw(e, w, pst=PREP_ST):
                e.wait_ge(SEM["st"], pst * 16)
                dma(e, "ld", qxh[:, 0:D], wgqh_d[0, :, 0:D])
                dma(e, "ld", qxl[:, 0:D], wgql_d[0, :, 0:D])
            body("ld", _dldw)
            dld = ld_mark()
            def _dcvt(e, w, dld=dld):
                w("ld", dld)
                e.tensor_copy(out=qv[:, 0:D], in_=qxh[:, 0:D])
                return e.tensor_copy(out=qu[:, 0:D], in_=qxl[:, 0:D])
            body("v", _dcvt)
            dv = C["v"]
            def _dst(e, w, dv=dv):
                w("v", dv)
                dma(e, "st", dwg_d[:, 0:D], qv[:, 0:D])
                dma(e, "st", dwgl_d[:, 0:D], qu[:, 0:D])
            body("st", _dst)

        TRI_OFF = [h * (h + 1) // 2 for h in range(NH + 1)]

        PREP_A_DONE = C["a"]
        a_resid_v = {}
        xmid_st = {}
        AM = {}

        def a_front(t):
            xt = xa if t % 2 == 0 else xb
            m = {}
            AM[t] = m

            def _ldx(e, w, t=t, xt=xt, vg=a_resid_v.get(t - 2, 0)):
                if vg:
                    e.wait_ge(SEM["v"], vg)
                dma(e, "ld", xt[:], x_d[t * 128:(t + 1) * 128, :])
            body("ld", _ldx)
            ld_x = ld_mark()

            def _lnv(e, w, ld_x=ld_x, xt=xt, t0=(t == 0)):
                w("ld", ld_x)
                if t0:
                    w("a", PREP_A_DONE)
                e.reduce_sum(out=stt[:, 0:1], in_=xt[:], axis=AX.X)
                e.tensor_tensor(out=scr[:, 0:D], in0=xt[:], in1=xt[:], op=ALU.mult)
                e.reduce_sum(out=stt[:, 1:2], in_=scr[:, 0:D], axis=AX.X)
                e.tensor_scalar(out=stt[:, 2:3], in0=stt[:, 0:1], scalar1=1.0 / D,
                                scalar2=None, op0=ALU.mult)
                vsp(e)
                e.tensor_scalar(out=stt[:, 3:4], in0=stt[:, 1:2], scalar1=1.0 / D,
                                scalar2=EPS_LN, op0=ALU.mult, op1=ALU.add)
                e.tensor_tensor(out=stt[:, 4:5], in0=stt[:, 2:3], in1=stt[:, 2:3],
                                op=ALU.mult)
                vsp(e)
                e.tensor_tensor(out=stt[:, 5:6], in0=stt[:, 3:4],
                                in1=stt[:, 4:5], op=ALU.subtract)
                return vsp(e)
            body("v", _lnv)
            v_var = C["v"]

            def _lna2(e, w, v_var=v_var):
                w("v", v_var)
                e.activation(out=stt[:, 6:7], in_=stt[:, 5:6], func=AF.Ln)
                asp(e, 2)
                e.activation(out=stt[:, 7:8], in_=stt[:, 6:7], func=AF.Exp,
                             scale=-0.5)
                return asp(e, 2)
            body("a", _lna2)
            a_sqrt = C["a"]

            def _lnv3(e, w, a_sqrt=a_sqrt, xt=xt,
                      tg=(AM.get(t - 2, {}).get("t_atr", 0) if t != 1
                          else AM[0]["t_xtr"])):
                w("a", a_sqrt)
                if tg:
                    w("t", tg)
                vsp(e)
                e.tensor_tensor(out=stt[:, 8:9], in0=stt[:, 2:3], in1=stt[:, 7:8],
                                op=ALU.mult)
                vsp(e)
                e.tensor_scalar(out=scr[:, 0:D], in0=xt[:], scalar1=stt[:, 7:8],
                                scalar2=stt[:, 8:9], op0=ALU.mult, op1=ALU.subtract)
                e.tensor_copy(out=xh[:], in_=scr[:, 0:D])
                return e.tensor_tensor(out=xl[:], in0=scr[:, 0:D], in1=xh[:],
                                       op=ALU.subtract)
            body("v", _lnv3)
            v_split = C["v"]

            emit_T([xh[:, i * 128:(i + 1) * 128] for i in range(KS)],
                   [(xhT, i * 128) for i in range(KS)], deps=[("v", v_split)])
            m["t_xtr"] = C["t"]
            v_xT = emit_T([xl[:, i * 128:(i + 1) * 128] for i in range(KS)],
                          [(xlT, i * 128) for i in range(KS)])
            m["t_xtr"] = C["t"]

            def _qkv(e, w, v_xT=v_xT):
                w("v", v_xT)
                last = None
                for ksi in range(KS):
                    sh = xhT[:, ksi * 128:(ksi + 1) * 128]
                    sl = xlT[:, ksi * 128:(ksi + 1) * 128]
                    first = ksi == 0
                    lastk = ksi == KS - 1
                    for (whi, wlo, pdst) in ((wqh, wql, pQ), (wkh, wkl, pK),
                                             (wvh, wvl, pV)):
                        for nch in range(2):
                            rh = whi[:, ksi * D + nch * 512: ksi * D + nch * 512 + 512]
                            rl = wlo[:, ksi * D + nch * 512: ksi * D + nch * 512 + 512]
                            od = pdst[:, nch * 512:(nch + 1) * 512]
                            last = e.matmul(out=od, lhsT=sh, rhs=rh,
                                            start=first, stop=False)
                            last = e.matmul(out=od, lhsT=sh, rhs=rl,
                                            start=False, stop=False)
                            last = e.matmul(out=od, lhsT=sl, rhs=rh,
                                            start=False, stop=lastk)
                return last
            body("t", _qkv)
            m["t_qkv"] = C["t"]

        def a_evqkv(t):
            def _evqkv(e, w, tq=AM[t]["t_qkv"]):
                w("t", tq)
                e.tensor_copy(out=qb[:], in_=pQ[:])
                e.tensor_copy(out=kb[:], in_=pK[:])
                return e.tensor_copy(out=vb[:], in_=pV[:])
            body("v", _evqkv)
            AM[t]["v_qkv"] = C["v"]

        a_front(0)
        a_evqkv(0)
        for t in range(n_tiles):
            xt = xa if t % 2 == 0 else xb
            m = AM[t]

            def _smx(e, w, vq=m["v_qkv"]):
                w("v", vq)
                e.reduce_max(out=scores[:, 136:136 + NH],
                             in_=qb.rearrange("p (h d) -> p h d", d=HD), axis=AX.X)
                e.tensor_scalar(out=scrb[:], in0=kb[:], scalar1=0.0, scalar2=None,
                                op0=ALU.min)
                e.tensor_scalar(out=kb[:], in0=kb[:], scalar1=0.0,
                                scalar2=None, op0=ALU.max)
                return e.tensor_tensor(out=qb.rearrange("p (h d) -> p h d", d=HD),
                                in0=qb.rearrange("p (h d) -> p h d", d=HD),
                                in1=scores[:, 136:136 + NH].unsqueeze(2)
                                    .broadcast_to([128, NH, HD]), op=ALU.subtract)
            body("v", _smx)
            v_smx = C["v"]

            def _exp(e, w, v_smx=v_smx):
                w("v", v_smx)
                e.activation(out=qb[:], in_=qb[:], func=AF.Exp)
                return e.activation(out=scrb[:], in_=scrb[:], func=AF.Exp)
            body("a", _exp)
            a_exp = C["a"]

            if t + 1 < n_tiles:
                a_front(t + 1)

            def _mix1(e, w, a_exp=a_exp):
                w("a", a_exp)
                e.tensor_tensor(out=kb[:], in0=kb[:], in1=scrb[:], op=ALU.add)
                e.reduce_sum(out=scores[:, 136:136 + NH],
                             in_=qb.rearrange("p (h d) -> p h d", d=HD), axis=AX.X)
                def dstep(out_t, in_t, off):
                    e.tensor_copy(out=out_t[:, 0:off * HD], in_=in_t[:, 0:off * HD])
                    e.tensor_tensor(out=out_t[:, off * HD:D],
                                    in0=in_t[:, off * HD:D],
                                    in1=in_t[:, 0:D - off * HD], op=ALU.add)
                dstep(scrb, kb, 1)
                dstep(kc, scrb, 2)
                dstep(scrb, kc, 4)
                dstep(kc, scrb, 8)
                e.reciprocal(out=scores[:, 136:136 + NH], in_=scores[:, 136:136 + NH])
                e.tensor_scalar(out=kc[:], in0=kc[:], scalar1=EPS_ATT, scalar2=None,
                                op0=ALU.add)
                e.reciprocal(out=kc[:], in_=kc[:])
                e.tensor_tensor(out=qb.rearrange("p (h d) -> p h d", d=HD),
                                in0=qb.rearrange("p (h d) -> p h d", d=HD),
                                in1=scores[:, 136:136 + NH].unsqueeze(2)
                                    .broadcast_to([128, NH, HD]), op=ALU.mult)
                return e.tensor_tensor(out=qb[:], in0=qb[:], in1=kc[:], op=ALU.mult)
            body("v", _mix1)
            v_mix1 = C["v"]

            def _mix2(e, w, v_mix1=v_mix1, tg=AM.get(t + 1, {}).get("t_xtr", 0)):
                w("v", v_mix1)
                if tg:
                    w("t", tg)
                pbuf = [scr2, xov]

                def smult(h):
                    nh2 = h + 1
                    e.tensor_tensor(
                        out=pbuf[h % 2][:, 0:nh2 * HD].rearrange("p (a d) -> p a d", d=HD),
                        in0=qb[:, h * HD:(h + 1) * HD].unsqueeze(1)
                            .broadcast_to([128, nh2, HD]),
                        in1=kb[:, 0:nh2 * HD].rearrange("p (a d) -> p a d", d=HD),
                        op=ALU.mult)

                def sred(h):
                    nh2 = h + 1
                    e.reduce_sum(out=scores[:, TRI_OFF[h]:TRI_OFF[h] + nh2],
                                 in_=pbuf[h % 2][:, 0:nh2 * HD]
                                     .rearrange("p (a d) -> p a d", d=HD),
                                 axis=AX.X)

                for h in range(NH):
                    smult(h)
                    if h >= 1:
                        sred(h - 1)
                sred(NH - 1)

                def amult(h):
                    nh2 = h + 1
                    e.tensor_tensor(
                        out=pbuf[h % 2][:, 0:nh2 * HD].rearrange("p (a d) -> p a d", d=HD),
                        in0=vb[:, 0:nh2 * HD].rearrange("p (a d) -> p a d", d=HD),
                        in1=scores[:, TRI_OFF[h]:TRI_OFF[h] + nh2].unsqueeze(2)
                            .broadcast_to([128, nh2, HD]),
                        op=ALU.mult)

                def ared(h):
                    nh2 = h + 1
                    e.reduce_sum(
                        out=attn[:, h * HD:(h + 1) * HD],
                        in_=pbuf[h % 2][:, 0:nh2 * HD]
                            .rearrange("p (a d) -> p a d", d=HD).transpose([0, 2, 1]),
                        axis=AX.X)

                for h in range(NH):
                    amult(h)
                    if h >= 1:
                        ared(h - 1)
                ared(NH - 1)
                e.tensor_copy(out=xh[:], in_=attn[:])
                return e.tensor_tensor(out=xl[:], in0=attn[:], in1=xh[:],
                                       op=ALU.subtract)
            body("v", _mix2)
            v_mix2 = C["v"]

            emit_T([xh[:, i * 128:(i + 1) * 128] for i in range(KS)],
                   [(xhT, i * 128) for i in range(KS)], deps=[("v", v_mix2)])
            m["t_atr"] = C["t"]
            v_aT = emit_T([xl[:, i * 128:(i + 1) * 128] for i in range(KS)],
                          [(xlT, i * 128) for i in range(KS)])
            m["t_atr"] = C["t"]

            if t + 1 < n_tiles:
                a_evqkv(t + 1)

            def _omm(e, w, v_aT=v_aT, vq1=AM.get(t + 1, {}).get("v_qkv", 0)):
                w("v", max(v_aT, vq1))
                last = None
                for ksi in range(KS):
                    sh = xhT[:, ksi * 128:(ksi + 1) * 128]
                    sl = xlT[:, ksi * 128:(ksi + 1) * 128]
                    first = ksi == 0
                    lastk = ksi == KS - 1
                    for nch in range(2):
                        rh = woh[:, ksi * D + nch * 512: ksi * D + nch * 512 + 512]
                        rl = wol[:, ksi * D + nch * 512: ksi * D + nch * 512 + 512]
                        od = pQ[:, nch * 512:(nch + 1) * 512]
                        last = e.matmul(out=od, lhsT=sh, rhs=rh, start=first, stop=False)
                        last = e.matmul(out=od, lhsT=sh, rhs=rl, start=False, stop=False)
                        last = e.matmul(out=od, lhsT=sl, rhs=rh, start=False, stop=lastk)
                return last
            body("t", _omm)
            t_o = C["t"]

            def _oev(e, w, t_o=t_o, xt=xt, sg=st_guard[1]):
                w("t", t_o)
                if sg:
                    e.wait_ge(SEM["st"], sg * 16)
                e.scalar_tensor_tensor(out=xov[:, 0:512], in0=pQ[:, 0:512],
                                       scalar=1.0, in1=bo_rep[:, 0:512],
                                       op0=ALU.mult, op1=ALU.add)
                e.scalar_tensor_tensor(out=xov[:, 512:1024], in0=pQ[:, 512:1024],
                                       scalar=1.0, in1=bo_rep[:, 512:1024],
                                       op0=ALU.mult, op1=ALU.add)
                return e.tensor_tensor(out=xov[:], in0=xov[:], in1=xt[:], op=ALU.add)
            body("v", _oev)
            a_resid_v[t] = C["v"]
            v_resid = C["v"]

            def _stx(e, w, t=t, v_resid=v_resid):
                w("v", v_resid)
                dma(e, "st", xmid_d[t * 128:(t + 1) * 128, :], xov[:])
                if debug:
                    dma(e, "st", dxmid_d[t * 128:(t + 1) * 128, :], xov[:])
            body("st", _stx)
            st_guard[1] = C["st"]
            xmid_st[t] = C["st"]

        A_T_DONE = C["t"]
        es_a.close()
        wgch = [sb(f"wgch{i}", [128, KS * 512], BF16) for i in range(2)]
        wgcl = [sb(f"wgcl{i}", [128, KS * 512], BF16) for i in range(2)]
        wuch = [sb(f"wuch{i}", [128, KS * 512], BF16) for i in range(2)]
        wucl = [sb(f"wucl{i}", [128, KS * 512], BF16) for i in range(2)]
        wdch = [sb(f"wdch{i}", [128, 4 * D], BF16) for i in range(2)]
        wdcl = [sb(f"wdcl{i}", [128, 4 * D], BF16) for i in range(2)]
        xqTh = [sb(f"xqTh{i}", [128, D], BF16) for i in range(2)]
        xqTl = [sb(f"xqTl{i}", [128, D], BF16) for i in range(2)]
        xmidb = [sb(f"xmidb{i}", [128, D]) for i in range(2)]
        hsc = sb("hsc", [128, 512])
        hsc2 = sb("hsc2", [128, 512])
        qxh2 = sb("qxh2", [128, 512], BF16)
        qxl2 = sb("qxl2", [128, 512], BF16)
        hTh = sb("hTh", [128, 512], BF16)
        hTl = sb("hTl", [128, 512], BF16)
        ybuf = sb("ybuf", [128, D])
        scr = qxq

        CH_OFF = [0]
        for c in CHUNKS:
            CH_OFF.append(CH_OFF[-1] + c)

        wbuf_guard = {0: 0, 1: 0}
        ld_chunk = {}
        ypair_guard = [0]
        n_pairs = n_tiles // 2 if "B" in phases else 0
        for p in range(n_pairs):
            tiles = [2 * p, 2 * p + 1]
            for j, t in enumerate(tiles):
                def _ldm(e, w, t=t, j=j, vg=C["v"], xst=xmid_st[t]):
                    e.wait_ge(SEM["v"], vg)
                    e.wait_ge(SEM["st"], xst * 16)
                    dma(e, "ld", xmidb[j][:], xmid_d[t * 128:(t + 1) * 128, :])
                body("ld", _ldm)
                ld_m = ld_mark()

                def _lnv(e, w, ld_m=ld_m, j=j):
                    w("ld", ld_m)
                    e.reduce_sum(out=stt[:, 0:1], in_=xmidb[j][:], axis=AX.X)
                    e.tensor_tensor(out=qu[:, 0:D], in0=xmidb[j][:], in1=xmidb[j][:],
                                    op=ALU.mult)
                    e.reduce_sum(out=stt[:, 1:2], in_=qu[:, 0:D], axis=AX.X)
                    e.tensor_scalar(out=stt[:, 2:3], in0=stt[:, 0:1], scalar1=1.0 / D,
                                    scalar2=None, op0=ALU.mult)
                    vsp(e)
                    e.tensor_scalar(out=stt[:, 3:4], in0=stt[:, 1:2], scalar1=1.0 / D,
                                    scalar2=EPS_LN, op0=ALU.mult, op1=ALU.add)
                    e.tensor_tensor(out=stt[:, 4:5], in0=stt[:, 2:3], in1=stt[:, 2:3],
                                    op=ALU.mult)
                    vsp(e)
                    e.tensor_tensor(out=stt[:, 5:6], in0=stt[:, 3:4],
                                    in1=stt[:, 4:5], op=ALU.subtract)
                    return vsp(e)
                body("v", _lnv)
                v_var = C["v"]

                def _lna2(e, w, v_var=v_var):
                    w("v", v_var)
                    e.activation(out=stt[:, 6:7], in_=stt[:, 5:6], func=AF.Ln)
                    asp(e, 2)
                    e.activation(out=stt[:, 7:8], in_=stt[:, 6:7], func=AF.Exp,
                                 scale=-0.5)
                    return asp(e, 2)
                body("a", _lna2)
                a_sqrt = C["a"]

                def _lnv3(e, w, a_sqrt=a_sqrt, j=j):
                    w("a", a_sqrt)
                    vsp(e)
                    e.tensor_tensor(out=stt[:, 8:9], in0=stt[:, 2:3], in1=stt[:, 7:8],
                                    op=ALU.mult)
                    vsp(e)
                    return e.tensor_scalar(out=qv[:, 0:D], in0=xmidb[j][:],
                                           scalar1=stt[:, 7:8], scalar2=stt[:, 8:9],
                                           op0=ALU.mult, op1=ALU.subtract)
                body("v", _lnv3)
                emit_quant(D, tdep=C["t"])
                v_q = C["v"]
                emit_T([qxh[:, i * 128:(i + 1) * 128] for i in range(KS)],
                       [(xqTh[j], i * 128) for i in range(KS)], deps=[("v", v_q)])
                emit_T([qxl[:, i * 128:(i + 1) * 128] for i in range(KS)],
                       [(xqTl[j], i * 128) for i in range(KS)])
                if debug and p == 0 and j == 0:
                    def _dq(e, w):
                        return e.tensor_copy(out=ybuf[:], in_=qxq[:, 0:D])
                    body("v", _dq)
                    dv2 = C["v"]
                    def _dq2(e, w, dv2=dv2):
                        w("v", dv2)
                        dma(e, "st", dxq_d[:, 0:D], ybuf[:])
                    body("st", _dq2)

            v_xq_pair = C["v"]

            steps = []
            for ci, CW in enumerate(CHUNKS):
                for j, t in enumerate(tiles):
                    steps.append((ci, CW, j, t))

            st_state = {}

            def b_quant(si):
                (ci, CW, j, t) = steps[si]
                m = st_state[si]
                bqh = qxh if si % 2 == 0 else qxh2
                bql = qxl if si % 2 == 0 else qxl2
                emit_quant(CW, qsrc=qvb[si % 2], qh=bqh, ql=bql,
                           tdep=st_state[si - 2]["t_tr"] if si >= 2 else 0)
                m["v_hq"] = C["v"]

            def b_tr(si):
                (ci, CW, j, t) = steps[si]
                cb = ci % 2
                m = st_state[si]
                bqh = qxh if si % 2 == 0 else qxh2
                bql = qxl if si % 2 == 0 else qxl2
                nsubk = CW // 128
                emit_T([bqh[:, i * 128:(i + 1) * 128] for i in range(nsubk)],
                       [(hTh, i * 128) for i in range(nsubk)],
                       deps=[("v", m["v_hq"])])
                m["v_evh"] = emit_T(
                    [bql[:, i * 128:(i + 1) * 128] for i in range(nsubk)],
                    [(hTl, i * 128) for i in range(nsubk)])
                m["t_tr"] = C["t"]

                pD = pV if j == 0 else pQ
                def _dn(e, w, v_hT=m["v_evh"], cb=cb, CW=CW, ci=ci, pD=pD):
                    w("v", v_hT)
                    last = None
                    nks = CW // 128
                    for ksi in range(nks):
                        sh = hTh[:, ksi * 128:(ksi + 1) * 128]
                        sl = hTl[:, ksi * 128:(ksi + 1) * 128]
                        first = (ci == 0 and ksi == 0)
                        lastk = (ci == len(CHUNKS) - 1 and ksi == nks - 1)
                        for nch in range(2):
                            rh = wdch[cb][:, ksi * D + nch * 512: ksi * D + nch * 512 + 512]
                            rl = wdcl[cb][:, ksi * D + nch * 512: ksi * D + nch * 512 + 512]
                            od = pD[:, nch * 512:(nch + 1) * 512]
                            last = e.matmul(out=od, lhsT=sh, rhs=rh,
                                            start=first, stop=False)
                            last = e.matmul(out=od, lhsT=sh, rhs=rl,
                                            start=False, stop=False)
                            last = e.matmul(out=od, lhsT=sl, rhs=rh,
                                            start=False, stop=lastk)
                    return last
                body("t", _dn)
                if j == 1:
                    wbuf_guard[cb] = C["t"]

            for si, (ci, CW, j, t) in enumerate(steps):
                n0 = CH_OFF[ci]
                cb = ci % 2
                if j == 0:
                    def _ldc(e, w, ci=ci, n0=n0, CW=CW, cb=cb,
                             tg=wbuf_guard[cb], pst=PREP_ST, atd=A_T_DONE):
                        e.wait_ge(SEM["st"], pst * 16)
                        e.wait_ge(SEM["t"], max(tg, atd))
                        for i in range(KS):
                            dma(e, "ld", wgch[cb][:, i * 512:i * 512 + CW],
                                wgqh_d[i, :, n0:n0 + CW])
                            dma(e, "ld", wgcl[cb][:, i * 512:i * 512 + CW],
                                wgql_d[i, :, n0:n0 + CW])
                            dma(e, "ld", wuch[cb][:, i * 512:i * 512 + CW],
                                wuqh_d[i, :, n0:n0 + CW])
                            dma(e, "ld", wucl[cb][:, i * 512:i * 512 + CW],
                                wuql_d[i, :, n0:n0 + CW])
                        for jj in range(CW // 128):
                            dma(e, "ld", wdch[cb][:, jj * D:(jj + 1) * D],
                                wdqh_d[n0 // 128 + jj, :, :])
                            dma(e, "ld", wdcl[cb][:, jj * D:(jj + 1) * D],
                                wdql_d[n0 // 128 + jj, :, :])
                    body("ld", _ldc)
                    ld_chunk[ci] = ld_mark()

                prev = st_state.get(si - 1)
                def _gu(e, w, j=j, cb=cb, CW=CW, ldc=ld_chunk[ci], prev=prev,
                        vxq=v_xq_pair):
                    w("ld", ldc)
                    w("v", vxq if prev is None else max(vxq, prev["v_guprod"]))
                    if prev is not None:
                        w("a", prev["a_eneg"])
                    last = None
                    for ksi in range(KS):
                        sh = xqTh[j][:, ksi * 128:(ksi + 1) * 128]
                        sl = xqTl[j][:, ksi * 128:(ksi + 1) * 128]
                        first = ksi == 0
                        lastk = ksi == KS - 1
                        for (wch_, wcl_, po) in ((wgch[cb], wgcl[cb], 0),
                                                 (wuch[cb], wucl[cb], 512)):
                            rh = wch_[:, ksi * 512:ksi * 512 + CW]
                            rl = wcl_[:, ksi * 512:ksi * 512 + CW]
                            od = pK[:, po:po + CW]
                            last = e.matmul(out=od, lhsT=sh, rhs=rh,
                                            start=first, stop=False)
                            last = e.matmul(out=od, lhsT=sh, rhs=rl,
                                            start=False, stop=False)
                            last = e.matmul(out=od, lhsT=sl, rhs=rh,
                                            start=False, stop=lastk)
                    return last
                body("t", _gu)
                t_gu = C["t"]

                if PIPE_B:
                    if si >= 1:
                        b_quant(si - 1)
                    if si >= 2:
                        b_tr(si - 2)

                def _eneg(e, w, t_gu=t_gu, CW=CW, prev=prev):
                    w("t", t_gu)
                    if prev is not None:
                        w("v", prev["v_h2"])
                    return e.activation(out=hsc[:, 0:CW], in_=pK[:, 0:CW],
                                        func=AF.Exp, scale=-1.0)
                body("a", _eneg)
                a_eneg = C["a"]

                def _h(e, w, t_gu=t_gu, CW=CW):
                    w("t", t_gu)
                    e.tensor_copy(out=hsc2[:, 0:CW], in_=pK[:, 512:512 + CW])
                    return e.tensor_tensor(out=hsc2[:, 0:CW], in0=pK[:, 0:CW],
                                           in1=hsc2[:, 0:CW], op=ALU.mult)
                body("v", _h)
                v_guprod = C["v"]

                def _h2(e, w, a_eneg=a_eneg, CW=CW, si=si):
                    w("a", a_eneg)
                    e.tensor_scalar(out=hsc[:, 0:CW], in0=hsc[:, 0:CW],
                                    scalar1=1.0, scalar2=None, op0=ALU.add)
                    e.reciprocal(out=hsc[:, 0:CW], in_=hsc[:, 0:CW])
                    return e.tensor_tensor(out=qvb[si % 2][:, 0:CW],
                                           in0=hsc2[:, 0:CW],
                                           in1=hsc[:, 0:CW], op=ALU.mult)
                body("v", _h2)
                v_h2 = C["v"]
                st_state[si] = {"v_guprod": v_guprod, "a_eneg": a_eneg,
                                "v_h2": v_h2}
                if not PIPE_B:
                    b_quant(si)
                    b_tr(si)
            if PIPE_B:
                b_quant(len(steps) - 1)
                b_tr(len(steps) - 2)
                b_tr(len(steps) - 1)

            for j, t in enumerate(tiles):
                pD = pV if j == 0 else pQ
                t_done = C["t"]

                def _yev(e, w, pD=pD, j=j, t_done=t_done, yg=ypair_guard[0]):
                    w("t", t_done)
                    if yg:
                        e.wait_ge(SEM["st"], yg * 16)
                    e.tensor_tensor(out=ybuf[:, 0:512], in0=pD[:, 0:512],
                                    in1=xmidb[j][:, 0:512], op=ALU.add)
                    return e.tensor_tensor(out=ybuf[:, 512:1024], in0=pD[:, 512:1024],
                                           in1=xmidb[j][:, 512:1024], op=ALU.add)
                body("v", _yev)
                v_y = C["v"]

                def _sty(e, w, t=t, v_y=v_y):
                    w("v", v_y)
                    dma(e, "st", y_d[t * 128:(t + 1) * 128, :], ybuf[:])
                body("st", _sty)
                ypair_guard[0] = C["st"]

        if "B" not in phases:
            def _ydummy(e, w):
                w("v", None)
                for t in range(n_tiles):
                    dma(e, "st", y_d[t * 128:(t + 1) * 128, :], ybuf[:])
            body("st", _ydummy)

        def _fin(e, w):
            e.wait_ge(SEM["st"], C["st"] * 16)
            return e.engine_nop()
        body("g", _fin)

    return nc


def kernel(L0_attn_x, L0_attn_Wq, L0_attn_Wk, L0_attn_Wv, L0_attn_Wo,
           L0_attn_bo, L0_ffn_w_gate, L0_ffn_w_up, L0_ffn_w_down):
    x = np.ascontiguousarray(np.asarray(L0_attn_x, dtype=np.float32).reshape(-1, D))
    wq = np.ascontiguousarray(np.asarray(L0_attn_Wq, dtype=np.float32))
    wk = np.ascontiguousarray(np.asarray(L0_attn_Wk, dtype=np.float32))
    wv = np.ascontiguousarray(np.asarray(L0_attn_Wv, dtype=np.float32))
    wo = np.ascontiguousarray(np.asarray(L0_attn_Wo, dtype=np.float32))
    bo = np.ascontiguousarray(np.asarray(L0_attn_bo, dtype=np.float32).reshape(1, D))
    wg = np.ascontiguousarray(np.asarray(L0_ffn_w_gate, dtype=np.float32))
    wu = np.ascontiguousarray(np.asarray(L0_ffn_w_up, dtype=np.float32))
    wd = np.ascontiguousarray(np.asarray(L0_ffn_w_down, dtype=np.float32))

    nc = build_program()
    in_maps = []
    for i in range(N_CORES):
        in_maps.append({
            "x": x[i * TOK_PER_CORE:(i + 1) * TOK_PER_CORE],
            "wq": wq, "wk": wk, "wv": wv, "wo": wo, "bo2": bo,
            "wg": wg, "wu": wu, "wd": wd,
        })
    res = run_bass_kernel_spmd(nc, in_maps, core_ids=list(range(N_CORES)))
    y = np.concatenate([np.asarray(r["y"]) for r in res.results], axis=0)
    return y.reshape(4, 4096, D).astype(np.float32)


# version 12
# speedup vs baseline: 1.1672x; 1.0057x over previous
import numpy as np
import concourse.bass as bass
import concourse.mybir as mybir
from concourse.bass_utils import run_bass_kernel_spmd

F32 = mybir.dt.float32
BF16 = mybir.dt.bfloat16
AF = mybir.ActivationFunctionType
ALU = mybir.AluOpType
AX = mybir.AxisListType

N_CORES = 8
D = 1024
NH = 16
HD = 64
I = 2816
TOK_PER_CORE = 2048
N_TILES = TOK_PER_CORE // 128
KS = D // 128
NT_W = I // 128
CHUNKS = [384, 512, 512, 512, 512, 384]
assert sum(CHUNKS) == I
EPS_LN = 1e-5
EPS_ATT = 1e-6
QTHR = [0.25, 0.75, 1.25, 1.75, 2.5, 3.5, 5.0]
PIPE_PREP = True
PIPE_B = True


def build_program(n_tiles=N_TILES, debug=False, phases="PAB"):
    nc = bass.Bass("TRN2", target_bir_lowering=False, detect_race_conditions=False)
    ntok = n_tiles * 128

    x_d = nc.dram_tensor("x", [ntok, D], F32, kind="ExternalInput").ap()
    wq_d = nc.dram_tensor("wq", [D, D], F32, kind="ExternalInput").ap()
    wk_d = nc.dram_tensor("wk", [D, D], F32, kind="ExternalInput").ap()
    wv_d = nc.dram_tensor("wv", [D, D], F32, kind="ExternalInput").ap()
    wo_d = nc.dram_tensor("wo", [D, D], F32, kind="ExternalInput").ap()
    bo_d = nc.dram_tensor("bo2", [1, D], F32, kind="ExternalInput").ap()
    wg_d = nc.dram_tensor("wg", [I, D], F32, kind="ExternalInput").ap()
    wu_d = nc.dram_tensor("wu", [I, D], F32, kind="ExternalInput").ap()
    wd_d = nc.dram_tensor("wd", [D, I], F32, kind="ExternalInput").ap()
    y_d = nc.dram_tensor("y", [ntok, D], F32, kind="ExternalOutput").ap()
    if debug:
        dxmid_d = nc.dram_tensor("dxmid", [ntok, D], F32, kind="ExternalOutput").ap()
        dwg_d = nc.dram_tensor("dwg", [128, I], F32, kind="ExternalOutput").ap()
        dwgl_d = nc.dram_tensor("dwgl", [128, I], F32, kind="ExternalOutput").ap()
        dxq_d = nc.dram_tensor("dxq", [128, D], F32, kind="ExternalOutput").ap()
        dxln_d = nc.dram_tensor("dxln", [128, D], F32, kind="ExternalOutput").ap()
        dq_d = nc.dram_tensor("dq", [128, D], F32, kind="ExternalOutput").ap()
        dk_d = nc.dram_tensor("dk", [128, D], F32, kind="ExternalOutput").ap()
        dv_d = nc.dram_tensor("dv", [128, D], F32, kind="ExternalOutput").ap()
        dattn_d = nc.dram_tensor("dattn", [128, D], F32, kind="ExternalOutput").ap()
        dqn_d = nc.dram_tensor("dqn", [128, D], F32, kind="ExternalOutput").ap()
        dstt_d = nc.dram_tensor("dstt", [128, 8], F32, kind="ExternalOutput").ap()
        dxt_d = nc.dram_tensor("dxt", [128, D], F32, kind="ExternalOutput").ap()

    xmid_d = nc.dram_tensor("xmid", [ntok, D], F32, kind="Internal").ap()
    wgqh_d = nc.dram_tensor("wgqh", [KS, 128, I], BF16, kind="Internal").ap()
    wgql_d = nc.dram_tensor("wgql", [KS, 128, I], BF16, kind="Internal").ap()
    wuqh_d = nc.dram_tensor("wuqh", [KS, 128, I], BF16, kind="Internal").ap()
    wuql_d = nc.dram_tensor("wuql", [KS, 128, I], BF16, kind="Internal").ap()
    wdqh_d = nc.dram_tensor("wdqh", [NT_W, 128, D], BF16, kind="Internal").ap()
    wdql_d = nc.dram_tensor("wdql", [NT_W, 128, D], BF16, kind="Internal").ap()

    from contextlib import ExitStack
    es = ExitStack()
    with es:
        sb = lambda name, shape, dt=F32: es.enter_context(nc.sbuf_tensor(name, shape, dt))
        ps = lambda name, shape, dt=F32: es.enter_context(nc.psum_tensor(name, shape, dt))
        sem = lambda name: es.enter_context(nc.semaphore(name))

        LD_SEMS = [sem(f"s_ld{i}") for i in range(4)]
        C_LD = [0, 0, 0, 0]
        ld_rr = [0]
        s_ld = LD_SEMS[0]
        s_st = sem("s_st")
        s_v = sem("s_v")
        s_a = sem("s_a")
        s_t = sem("s_t")
        s_g = sem("s_g")
        C = {"st": 0, "v": 0, "a": 0, "t": 0, "g": 0}
        SEM = {"st": s_st, "v": s_v, "a": s_a, "t": s_t, "g": s_g}
        ENG = {"v": "vector", "a": "scalar", "t": "tensor", "g": "gpsimd",
               "ld": "sync", "st": "gpsimd"}

        nums = sorted([h.num for h in SEM.values()] + [h.num for h in LD_SEMS])
        nc.all_engine_barrier()
        nc.gpsimd.dma_reset(range(nums[0], nums[-1] + 1))
        nc.gpsimd.sem_clear(range(nums[0], nums[-1] + 1))
        nc.all_engine_barrier()
        nc.all_engine_barrier()

        block = es.enter_context(nc.Block())

        def body(kind, fn):
            def runner(e):
                def w(dep, cnt=None):
                    if dep == "ld":
                        wld(e, ld_mark() if cnt is None else cnt)
                        return
                    c = C[dep] if cnt is None else cnt
                    if c > 0:
                        e.wait_ge(SEM[dep], c * 16 if dep == "st" else c)
                last = fn(e, w)
                if kind in ("ld", "st"):
                    pass
                else:
                    assert last is not None, "compute body must return last instr"
                    if kind == "v":
                        e.engine_nop().then_inc(SEM[kind], 1)
                    elif kind == "g":
                        e.memset(junkg[:], 0.0).then_inc(SEM[kind], 1)
                    else:
                        last.then_inc(SEM[kind], 1)
                    C[kind] += 1
            getattr(block, ENG[kind])(runner)

        LAST_LD = [(0, 0)]

        def dma(e, kind, out, in_):
            if kind == "ld":
                i = ld_rr[0] % 4
                ld_rr[0] += 1
                e.dma_start(out=out, in_=in_).then_inc(LD_SEMS[i], 16)
                C_LD[i] += 1
                LAST_LD[0] = (i, C_LD[i])
            else:
                e.dma_start(out=out, in_=in_).then_inc(SEM[kind], 16)
                C[kind] += 1

        def ld_mark():
            return LAST_LD[0]

        def wld(e, mark):
            i, c = mark
            if c > 0:
                e.wait_ge(LD_SEMS[i], c * 16)

        def vsp(e, n=2):
            last = None
            for _ in range(n):
                last = e.tensor_copy(out=junk[:], in_=junk[:])
            return last

        def asp(e, n=1):
            last = None
            for _ in range(n):
                last = e.activation(out=junka[:], in_=junka[:], func=AF.Copy)
            return last

        bo_rep = sb("bo_rep", [128, D])
        idb = sb("idb", [128, 128], BF16)
        c_eps = sb("c_eps", [128, 1])
        scores = sb("scores", [128, 160])
        stt = sb("stt", [128, 16])
        junk = sb("junk", [128, 128])
        junka = sb("junka", [128, 64])
        junkg = sb("junkg", [128, 4])

        qv = sb("qv", [128, D])
        qv2 = sb("qv2", [128, D])
        qu = sb("qu", [128, D])
        qsg = sb("qsg", [128, D], BF16)
        mA = sb("mA", [128, D], BF16); mB = sb("mB", [128, D], BF16)
        mC = sb("mC", [128, D], BF16)
        qmb = sb("qmb", [128, D // 16])
        qsc = sb("qsc", [128, D // 16])
        qrc = sb("qrc", [128, D // 16])
        qxq = sb("qxq", [128, D])
        qxh = sb("qxh", [128, D], BF16); qxl = sb("qxl", [128, D], BF16)

        es_a = ExitStack()
        sba = lambda name, shape, dt=F32: es_a.enter_context(nc.sbuf_tensor(name, shape, dt))
        stage = sba("stage", [128, D], BF16)
        stage2 = sba("stage2", [128, D], BF16)
        wqh = sba("wqh", [128, KS * D], BF16); wql = sba("wql", [128, KS * D], BF16)
        wkh = sba("wkh", [128, KS * D], BF16); wkl = sba("wkl", [128, KS * D], BF16)
        wvh = sba("wvh", [128, KS * D], BF16); wvl = sba("wvl", [128, KS * D], BF16)
        woh = sba("woh", [128, KS * D], BF16); wol = sba("wol", [128, KS * D], BF16)
        xa = sba("xa", [128, D]); xb = sba("xb", [128, D])
        qb = sba("qb", [128, D]); kb = sba("kb", [128, D]); vb = sba("vb", [128, D])
        kc = sba("kc", [128, D]); attn = sba("attn", [128, D])
        xh = sba("xh", [128, D], BF16); xl = sba("xl", [128, D], BF16)
        xhT = sba("xhT", [128, D], BF16); xlT = sba("xlT", [128, D], BF16)
        xov = sba("xov", [128, D])
        scrb = attn
        scr = qxq
        scr2 = qv

        pTA = ps("pTA", [128, 512])
        pTB = ps("pTB", [128, 512])
        pQ = ps("pQ", [128, D])
        pK = ps("pK", [128, D])
        pV = ps("pV", [128, D])

        def _setup_sync(e, w):
            dma(e, "ld", bo_rep[:], bo_d.partition_broadcast(128))
        body("ld", _setup_sync)

        def _setup_g(e, w):
            e.memset(idb[:], 0.0).then_inc(SEM["g"], 1)
            C["g"] += 1
            e.wait_ge(SEM["g"], C["g"])
            return e.affine_select(out=idb[:], in_=idb[:], compare_op=ALU.not_equal,
                                   fill=1.0, base=0, pattern=[[-1, 128]],
                                   channel_multiplier=1)
        body("g", _setup_g)
        G_SETUP = C["g"]

        def _setup_v(e, w):
            e.memset(c_eps[:], EPS_LN)
            e.memset(junka[:], 0.0)
            return e.memset(junk[:], 0.0)
        body("v", _setup_v)
        V_SETUP = C["v"]

        wd_pairs = [(wq_d, wqh, wql), (wk_d, wkh, wkl),
                    (wv_d, wvh, wvl), (wo_d, woh, wol)]
        raws = [qv, qu]
        last_split_v = {}
        idx = 0
        for (wdram, whi, wlo) in wd_pairs:
            for ksi in range(KS):
                rb = raws[idx % 2]
                vcnt_guard = last_split_v.get(idx % 2, 0)

                def _ld(e, w, rb=rb, wdram=wdram, ksi=ksi, vg=vcnt_guard):
                    if vg:
                        e.wait_ge(SEM["v"], vg)
                    dma(e, "ld", rb[:, 0:D], wdram[ksi * 128:(ksi + 1) * 128, :])
                body("ld", _ld)
                ld_now = ld_mark()

                def _sp(e, w, rb=rb, whi=whi, wlo=wlo, ksi=ksi, ld_now=ld_now):
                    w("ld", ld_now)
                    e.tensor_copy(out=whi[:, ksi * D:(ksi + 1) * D], in_=rb[:, 0:D])
                    return e.tensor_tensor(out=wlo[:, ksi * D:(ksi + 1) * D],
                                           in0=rb[:, 0:D],
                                           in1=whi[:, ksi * D:(ksi + 1) * D],
                                           op=ALU.subtract)
                body("v", _sp)
                last_split_v[idx % 2] = C["v"]
                idx += 1

        def emit_quant(F, qsrc=None, tdep=0, qh=None, ql=None):
            nb = F // 16
            if qsrc is None:
                qsrc = qv
            if qh is None:
                qh = qxh
            if ql is None:
                ql = qxl

            def _q1(e, w):
                if tdep:
                    e.wait_ge(SEM["t"], tdep)
                e.reduce_max(out=qmb[:, 0:nb],
                             in_=qsrc[:, 0:F].rearrange("p (b s) -> p b s", s=16),
                             axis=AX.X, apply_absolute_value=True)
                vsp(e)
                e.tensor_scalar(out=qsc[:, 0:nb], in0=qmb[:, 0:nb],
                                scalar1=1.0 / 6.0, scalar2=1e-12,
                                op0=ALU.mult, op1=ALU.max)
                vsp(e)
                e.reciprocal(out=qrc[:, 0:nb], in_=qsc[:, 0:nb])
                vsp(e)
                return e.tensor_tensor(
                    out=qsrc[:, 0:F].rearrange("p (b s) -> p b s", s=16),
                    in0=qsrc[:, 0:F].rearrange("p (b s) -> p b s", s=16),
                    in1=qrc[:, 0:nb].unsqueeze(2).broadcast_to([128, nb, 16]),
                    op=ALU.mult)
            body("v", _q1)
            v_q1 = C["v"]

            def _qa(e, w):
                w("v", v_q1)
                e.activation(out=qu[:, 0:F], in_=qsrc[:, 0:F], func=AF.Abs)
                return e.activation(out=qsg[:, 0:F], in_=qsrc[:, 0:F], func=AF.Tanh,
                                    scale=1e30)
            body("a", _qa)
            a_q = C["a"]

            def _q2(e, w):
                w("a", a_q)
                u = qu[:, 0:F]
                sp = (F < 512)
                e.tensor_scalar(out=mA[:, 0:F], in0=u, scalar1=QTHR[0], scalar2=None, op0=ALU.is_gt)
                e.tensor_scalar(out=mB[:, 0:F], in0=u, scalar1=QTHR[1], scalar2=None, op0=ALU.is_gt)
                if sp: vsp(e)
                e.tensor_tensor(out=mA[:, 0:F], in0=mA[:, 0:F], in1=mB[:, 0:F], op=ALU.add)
                e.tensor_scalar(out=mB[:, 0:F], in0=u, scalar1=QTHR[2], scalar2=None, op0=ALU.is_gt)
                e.tensor_scalar(out=mC[:, 0:F], in0=u, scalar1=QTHR[3], scalar2=None, op0=ALU.is_gt)
                if sp: vsp(e)
                e.tensor_tensor(out=mB[:, 0:F], in0=mB[:, 0:F], in1=mC[:, 0:F], op=ALU.add)
                if sp: vsp(e)
                e.tensor_tensor(out=mA[:, 0:F], in0=mA[:, 0:F], in1=mB[:, 0:F], op=ALU.add)
                e.tensor_scalar(out=mB[:, 0:F], in0=u, scalar1=QTHR[4], scalar2=None, op0=ALU.is_gt)
                e.tensor_scalar(out=mC[:, 0:F], in0=u, scalar1=QTHR[5], scalar2=None, op0=ALU.is_gt)
                if sp: vsp(e)
                e.tensor_tensor(out=mB[:, 0:F], in0=mB[:, 0:F], in1=mC[:, 0:F], op=ALU.add)
                e.tensor_scalar(out=mC[:, 0:F], in0=u, scalar1=QTHR[6], scalar2=None, op0=ALU.is_gt)
                if sp: vsp(e)
                e.scalar_tensor_tensor(out=mA[:, 0:F], in0=mA[:, 0:F], scalar=0.5,
                                       in1=mB[:, 0:F], op0=ALU.mult, op1=ALU.add)
                if sp: vsp(e)
                e.scalar_tensor_tensor(out=mA[:, 0:F], in0=mC[:, 0:F], scalar=2.0,
                                       in1=mA[:, 0:F], op0=ALU.mult, op1=ALU.add)
                if sp: vsp(e)
                e.tensor_tensor(out=mA[:, 0:F], in0=mA[:, 0:F], in1=qsg[:, 0:F], op=ALU.mult)
                if sp: vsp(e)
                e.tensor_tensor(out=qxq[:, 0:F].rearrange("p (b s) -> p b s", s=16),
                                in0=mA[:, 0:F].rearrange("p (b s) -> p b s", s=16),
                                in1=qsc[:, 0:nb].unsqueeze(2).broadcast_to([128, nb, 16]),
                                op=ALU.mult)
                if sp: vsp(e)
                e.tensor_copy(out=qh[:, 0:F], in_=qxq[:, 0:F])
                if sp: vsp(e)
                return e.tensor_tensor(out=ql[:, 0:F], in0=qxq[:, 0:F],
                                       in1=qh[:, 0:F], op=ALU.subtract)
            body("v", _q2)

        bank_free = {0: 0, 1: 0}

        def emit_T(srcs, dsts, deps=(), st_dep=0):
            gi = 0
            out_a = 0
            for g0 in range(0, len(srcs), 4):
                gs = list(range(g0, min(g0 + 4, len(srcs))))
                bank = pTA if gi % 2 == 0 else pTB
                bsel = gi % 2
                gdeps = list(deps) if g0 == 0 else []
                bf = bank_free[bsel]

                def _tr(e, w, gs=gs, bank=bank, gdeps=gdeps, bf=bf, srcs=srcs):
                    for dk, dc in gdeps:
                        w(dk, dc)
                    if bf:
                        w("v", bf)
                    last = None
                    for i, si_ in enumerate(gs):
                        last = e.matmul(out=bank[:, i * 128:(i + 1) * 128],
                                        lhsT=srcs[si_], rhs=idb[:],
                                        start=(i == 0), stop=(i == len(gs) - 1))
                    return last
                body("t", _tr)
                t_now = C["t"]

                dt0, dc0 = dsts[g0]
                def _ev(e, w, t_now=t_now, bank=bank, n=len(gs), dt0=dt0,
                        dc0=dc0, st_dep=st_dep):
                    if st_dep:
                        e.wait_ge(SEM["st"], st_dep * 16)
                    w("t", t_now)
                    return e.tensor_copy(out=dt0[:, dc0:dc0 + n * 128],
                                         in_=bank[:, 0:n * 128])
                body("v", _ev)
                bank_free[bsel] = C["v"]
                out_a = C["v"]
                gi += 1
            return out_a

        st_guard = {0: 0, 1: 0}
        prep_items = []
        for (wnat_d, dsth, dstl) in [(wg_d, wgqh_d, wgql_d), (wu_d, wuqh_d, wuql_d)]:
            for nt in range(NT_W):
                prep_items.append((wnat_d, nt, 0, D, dsth, dstl,
                                   [(i, nt * 128) for i in range(KS)]))
        WD_SLICES = [(0, 1024), (1024, 1024), (2048, 768)]
        for nt in range(KS):
            for (s0, SW) in WD_SLICES:
                prep_items.append((wd_d, nt, s0, SW, wdqh_d, wdql_d,
                                   [(s0 // 128 + i, nt * 128) for i in range(SW // 128)]))

        prep_m = {}
        def prep_back(n):
            (wnat_d, nt, s0, SW, dsth, dstl, blocks) = prep_items[n]
            m = prep_m[n]
            nblk = len(blocks)
            emit_T([qxh[:, i * 128:(i + 1) * 128] for i in range(nblk)],
                   [(stage, i * 128) for i in range(nblk)],
                   deps=[("v", m["v_q"])], st_dep=st_guard[0])
            v_ev = emit_T([qxl[:, i * 128:(i + 1) * 128] for i in range(nblk)],
                          [(stage2, i * 128) for i in range(nblk)],
                          st_dep=st_guard[0])
            m["t_tr"] = C["t"]

            def _spill(e, w, blocks=blocks, dsth=dsth, dstl=dstl, v_ev=v_ev):
                w("v", v_ev)
                for bi, (ksd, co) in enumerate(blocks):
                    dma(e, "st", dsth[ksd, :, co:co + 128],
                        stage[:, bi * 128:(bi + 1) * 128])
                    dma(e, "st", dstl[ksd, :, co:co + 128],
                        stage2[:, bi * 128:(bi + 1) * 128])
            body("st", _spill)
            st_guard[0] = C["st"]

        qvb = [qv, qv2]
        S1_V_DONE = C["v"]
        for n, (wnat_d, nt, s0, SW, dsth, dstl, blocks) in enumerate(prep_items):
            pm = prep_m.get(n - 2)
            def _ldw(e, w, wnat_d=wnat_d, nt=nt, s0=s0, SW=SW, n=n, pm=pm):
                if pm is not None:
                    e.wait_ge(SEM["v"], pm["v_q1"])
                    e.wait_ge(SEM["a"], pm["a_qa"])
                else:
                    e.wait_ge(SEM["v"], S1_V_DONE)
                dma(e, "ld", qvb[n % 2][:, 0:SW],
                    wnat_d[nt * 128:(nt + 1) * 128, s0:s0 + SW])
            body("ld", _ldw)
            ld_now = ld_mark()

            if PIPE_PREP and n >= 1:
                prep_back(n - 1)

            def _qdep(e, w, ld_now=ld_now):
                w("ld", ld_now)
                return e.engine_nop()
            body("v", _qdep)
            v_q1_pre = C["v"]
            emit_quant(SW, qsrc=qvb[n % 2],
                       tdep=(prep_m[n - 1]["t_tr"] if (PIPE_PREP and n >= 1) else 0))
            prep_m[n] = {"v_q": C["v"], "v_q1": v_q1_pre + 2,
                         "a_qa": C["a"]}
            if not PIPE_PREP:
                prep_back(n)
        if PIPE_PREP:
            prep_back(len(prep_items) - 1)

        PREP_ST = C["st"]
        if debug:
            def _dldw(e, w, pst=PREP_ST):
                e.wait_ge(SEM["st"], pst * 16)
                dma(e, "ld", qxh[:, 0:D], wgqh_d[0, :, 0:D])
                dma(e, "ld", qxl[:, 0:D], wgql_d[0, :, 0:D])
            body("ld", _dldw)
            dld = ld_mark()
            def _dcvt(e, w, dld=dld):
                w("ld", dld)
                e.tensor_copy(out=qv[:, 0:D], in_=qxh[:, 0:D])
                return e.tensor_copy(out=qu[:, 0:D], in_=qxl[:, 0:D])
            body("v", _dcvt)
            dv = C["v"]
            def _dst(e, w, dv=dv):
                w("v", dv)
                dma(e, "st", dwg_d[:, 0:D], qv[:, 0:D])
                dma(e, "st", dwgl_d[:, 0:D], qu[:, 0:D])
            body("st", _dst)

        TRI_OFF = [h * (h + 1) // 2 for h in range(NH + 1)]

        PREP_A_DONE = C["a"]
        a_resid_v = {}
        xmid_st = {}
        AM = {}

        def a_front(t):
            xt = xa if t % 2 == 0 else xb
            m = {}
            AM[t] = m

            def _ldx(e, w, t=t, xt=xt, vg=a_resid_v.get(t - 2, 0)):
                if vg:
                    e.wait_ge(SEM["v"], vg)
                dma(e, "ld", xt[:], x_d[t * 128:(t + 1) * 128, :])
            body("ld", _ldx)
            ld_x = ld_mark()

            def _lnv(e, w, ld_x=ld_x, xt=xt, t0=(t == 0)):
                w("ld", ld_x)
                if t0:
                    w("a", PREP_A_DONE)
                e.reduce_sum(out=stt[:, 0:1], in_=xt[:], axis=AX.X)
                e.tensor_tensor(out=scr[:, 0:D], in0=xt[:], in1=xt[:], op=ALU.mult)
                e.reduce_sum(out=stt[:, 1:2], in_=scr[:, 0:D], axis=AX.X)
                e.tensor_scalar(out=stt[:, 2:3], in0=stt[:, 0:1], scalar1=1.0 / D,
                                scalar2=None, op0=ALU.mult)
                vsp(e)
                e.tensor_scalar(out=stt[:, 3:4], in0=stt[:, 1:2], scalar1=1.0 / D,
                                scalar2=EPS_LN, op0=ALU.mult, op1=ALU.add)
                e.tensor_tensor(out=stt[:, 4:5], in0=stt[:, 2:3], in1=stt[:, 2:3],
                                op=ALU.mult)
                vsp(e)
                e.tensor_tensor(out=stt[:, 5:6], in0=stt[:, 3:4],
                                in1=stt[:, 4:5], op=ALU.subtract)
                return vsp(e)
            body("v", _lnv)
            v_var = C["v"]

            def _lna2(e, w, v_var=v_var):
                w("v", v_var)
                e.activation(out=stt[:, 6:7], in_=stt[:, 5:6], func=AF.Ln)
                asp(e, 2)
                e.activation(out=stt[:, 7:8], in_=stt[:, 6:7], func=AF.Exp,
                             scale=-0.5)
                return asp(e, 2)
            body("a", _lna2)
            a_sqrt = C["a"]

            def _lnv3(e, w, a_sqrt=a_sqrt, xt=xt,
                      tg=(AM.get(t - 2, {}).get("t_atr", 0) if t != 1
                          else AM[0]["t_xtr"])):
                w("a", a_sqrt)
                if tg:
                    w("t", tg)
                vsp(e)
                e.tensor_tensor(out=stt[:, 8:9], in0=stt[:, 2:3], in1=stt[:, 7:8],
                                op=ALU.mult)
                vsp(e)
                e.tensor_scalar(out=scr[:, 0:D], in0=xt[:], scalar1=stt[:, 7:8],
                                scalar2=stt[:, 8:9], op0=ALU.mult, op1=ALU.subtract)
                e.tensor_copy(out=xh[:], in_=scr[:, 0:D])
                return e.tensor_tensor(out=xl[:], in0=scr[:, 0:D], in1=xh[:],
                                       op=ALU.subtract)
            body("v", _lnv3)
            v_split = C["v"]

            emit_T([xh[:, i * 128:(i + 1) * 128] for i in range(KS)],
                   [(xhT, i * 128) for i in range(KS)], deps=[("v", v_split)])
            m["t_xtr"] = C["t"]
            v_xT = emit_T([xl[:, i * 128:(i + 1) * 128] for i in range(KS)],
                          [(xlT, i * 128) for i in range(KS)])
            m["t_xtr"] = C["t"]

            def _qkv(e, w, v_xT=v_xT):
                w("v", v_xT)
                last = None
                for ksi in range(KS):
                    sh = xhT[:, ksi * 128:(ksi + 1) * 128]
                    sl = xlT[:, ksi * 128:(ksi + 1) * 128]
                    first = ksi == 0
                    lastk = ksi == KS - 1
                    for (whi, wlo, pdst) in ((wqh, wql, pQ), (wkh, wkl, pK),
                                             (wvh, wvl, pV)):
                        for nch in range(2):
                            rh = whi[:, ksi * D + nch * 512: ksi * D + nch * 512 + 512]
                            rl = wlo[:, ksi * D + nch * 512: ksi * D + nch * 512 + 512]
                            od = pdst[:, nch * 512:(nch + 1) * 512]
                            last = e.matmul(out=od, lhsT=sh, rhs=rh,
                                            start=first, stop=False)
                            last = e.matmul(out=od, lhsT=sh, rhs=rl,
                                            start=False, stop=False)
                            last = e.matmul(out=od, lhsT=sl, rhs=rh,
                                            start=False, stop=lastk)
                return last
            body("t", _qkv)
            m["t_qkv"] = C["t"]

        def a_evqkv(t):
            def _evqkv(e, w, tq=AM[t]["t_qkv"]):
                w("t", tq)
                e.tensor_copy(out=qb[:], in_=pQ[:])
                e.tensor_copy(out=kb[:], in_=pK[:])
                return e.tensor_copy(out=vb[:], in_=pV[:])
            body("v", _evqkv)
            AM[t]["v_qkv"] = C["v"]

        a_front(0)
        a_evqkv(0)
        for t in range(n_tiles):
            xt = xa if t % 2 == 0 else xb
            m = AM[t]

            def _smx(e, w, vq=m["v_qkv"]):
                w("v", vq)
                e.reduce_max(out=scores[:, 136:136 + NH],
                             in_=qb.rearrange("p (h d) -> p h d", d=HD), axis=AX.X)
                e.tensor_scalar(out=scrb[:], in0=kb[:], scalar1=0.0, scalar2=None,
                                op0=ALU.min)
                e.tensor_scalar(out=kb[:], in0=kb[:], scalar1=0.0,
                                scalar2=None, op0=ALU.max)
                return e.tensor_tensor(out=qb.rearrange("p (h d) -> p h d", d=HD),
                                in0=qb.rearrange("p (h d) -> p h d", d=HD),
                                in1=scores[:, 136:136 + NH].unsqueeze(2)
                                    .broadcast_to([128, NH, HD]), op=ALU.subtract)
            body("v", _smx)
            v_smx = C["v"]

            def _exp(e, w, v_smx=v_smx):
                w("v", v_smx)
                e.activation(out=qb[:], in_=qb[:], func=AF.Exp)
                return e.activation(out=scrb[:], in_=scrb[:], func=AF.Exp)
            body("a", _exp)
            a_exp = C["a"]

            if t + 1 < n_tiles:
                a_front(t + 1)

            def _mix1(e, w, a_exp=a_exp):
                w("a", a_exp)
                e.tensor_tensor(out=kb[:], in0=kb[:], in1=scrb[:], op=ALU.add)
                e.reduce_sum(out=scores[:, 136:136 + NH],
                             in_=qb.rearrange("p (h d) -> p h d", d=HD), axis=AX.X)
                def dstep(out_t, in_t, off):
                    e.tensor_copy(out=out_t[:, 0:off * HD], in_=in_t[:, 0:off * HD])
                    e.tensor_tensor(out=out_t[:, off * HD:D],
                                    in0=in_t[:, off * HD:D],
                                    in1=in_t[:, 0:D - off * HD], op=ALU.add)
                dstep(scrb, kb, 1)
                dstep(kc, scrb, 2)
                dstep(scrb, kc, 4)
                dstep(kc, scrb, 8)
                e.reciprocal(out=scores[:, 136:136 + NH], in_=scores[:, 136:136 + NH])
                e.tensor_scalar(out=kc[:], in0=kc[:], scalar1=EPS_ATT, scalar2=None,
                                op0=ALU.add)
                e.reciprocal(out=kc[:], in_=kc[:])
                e.tensor_tensor(out=qb.rearrange("p (h d) -> p h d", d=HD),
                                in0=qb.rearrange("p (h d) -> p h d", d=HD),
                                in1=scores[:, 136:136 + NH].unsqueeze(2)
                                    .broadcast_to([128, NH, HD]), op=ALU.mult)
                return e.tensor_tensor(out=qb[:], in0=qb[:], in1=kc[:], op=ALU.mult)
            body("v", _mix1)
            v_mix1 = C["v"]

            def _mix2(e, w, v_mix1=v_mix1, tg=AM.get(t + 1, {}).get("t_xtr", 0)):
                w("v", v_mix1)
                if tg:
                    w("t", tg)
                pbuf = [scr2, xov]

                def smult(h):
                    nh2 = h + 1
                    e.tensor_tensor(
                        out=pbuf[h % 2][:, 0:nh2 * HD].rearrange("p (a d) -> p a d", d=HD),
                        in0=qb[:, h * HD:(h + 1) * HD].unsqueeze(1)
                            .broadcast_to([128, nh2, HD]),
                        in1=kb[:, 0:nh2 * HD].rearrange("p (a d) -> p a d", d=HD),
                        op=ALU.mult)

                def sred(h):
                    nh2 = h + 1
                    e.reduce_sum(out=scores[:, TRI_OFF[h]:TRI_OFF[h] + nh2],
                                 in_=pbuf[h % 2][:, 0:nh2 * HD]
                                     .rearrange("p (a d) -> p a d", d=HD),
                                 axis=AX.X)

                for h in range(NH):
                    smult(h)
                    if h >= 1:
                        sred(h - 1)
                sred(NH - 1)

                def amult(h):
                    nh2 = h + 1
                    e.tensor_tensor(
                        out=pbuf[h % 2][:, 0:nh2 * HD].rearrange("p (a d) -> p a d", d=HD),
                        in0=vb[:, 0:nh2 * HD].rearrange("p (a d) -> p a d", d=HD),
                        in1=scores[:, TRI_OFF[h]:TRI_OFF[h] + nh2].unsqueeze(2)
                            .broadcast_to([128, nh2, HD]),
                        op=ALU.mult)

                def ared(h):
                    nh2 = h + 1
                    e.reduce_sum(
                        out=attn[:, h * HD:(h + 1) * HD],
                        in_=pbuf[h % 2][:, 0:nh2 * HD]
                            .rearrange("p (a d) -> p a d", d=HD).transpose([0, 2, 1]),
                        axis=AX.X)

                for h in range(NH):
                    amult(h)
                    if h >= 1:
                        ared(h - 1)
                ared(NH - 1)
                e.tensor_copy(out=xh[:], in_=attn[:])
                return e.tensor_tensor(out=xl[:], in0=attn[:], in1=xh[:],
                                       op=ALU.subtract)
            body("v", _mix2)
            v_mix2 = C["v"]

            emit_T([xh[:, i * 128:(i + 1) * 128] for i in range(KS)],
                   [(xhT, i * 128) for i in range(KS)], deps=[("v", v_mix2)])
            m["t_atr"] = C["t"]
            v_aT = emit_T([xl[:, i * 128:(i + 1) * 128] for i in range(KS)],
                          [(xlT, i * 128) for i in range(KS)])
            m["t_atr"] = C["t"]

            if t + 1 < n_tiles:
                a_evqkv(t + 1)

            def _omm(e, w, v_aT=v_aT, vq1=AM.get(t + 1, {}).get("v_qkv", 0)):
                w("v", max(v_aT, vq1))
                last = None
                for ksi in range(KS):
                    sh = xhT[:, ksi * 128:(ksi + 1) * 128]
                    sl = xlT[:, ksi * 128:(ksi + 1) * 128]
                    first = ksi == 0
                    lastk = ksi == KS - 1
                    for nch in range(2):
                        rh = woh[:, ksi * D + nch * 512: ksi * D + nch * 512 + 512]
                        rl = wol[:, ksi * D + nch * 512: ksi * D + nch * 512 + 512]
                        od = pQ[:, nch * 512:(nch + 1) * 512]
                        last = e.matmul(out=od, lhsT=sh, rhs=rh, start=first, stop=False)
                        last = e.matmul(out=od, lhsT=sh, rhs=rl, start=False, stop=False)
                        last = e.matmul(out=od, lhsT=sl, rhs=rh, start=False, stop=lastk)
                return last
            body("t", _omm)
            t_o = C["t"]

            def _oev(e, w, t_o=t_o, xt=xt, sg=st_guard[1]):
                w("t", t_o)
                if sg:
                    e.wait_ge(SEM["st"], sg * 16)
                e.scalar_tensor_tensor(out=xov[:, 0:512], in0=pQ[:, 0:512],
                                       scalar=1.0, in1=bo_rep[:, 0:512],
                                       op0=ALU.mult, op1=ALU.add)
                e.scalar_tensor_tensor(out=xov[:, 512:1024], in0=pQ[:, 512:1024],
                                       scalar=1.0, in1=bo_rep[:, 512:1024],
                                       op0=ALU.mult, op1=ALU.add)
                return e.tensor_tensor(out=xov[:], in0=xov[:], in1=xt[:], op=ALU.add)
            body("v", _oev)
            a_resid_v[t] = C["v"]
            v_resid = C["v"]

            def _stx(e, w, t=t, v_resid=v_resid):
                w("v", v_resid)
                dma(e, "st", xmid_d[t * 128:(t + 1) * 128, :], xov[:])
                if debug:
                    dma(e, "st", dxmid_d[t * 128:(t + 1) * 128, :], xov[:])
            body("st", _stx)
            st_guard[1] = C["st"]
            xmid_st[t] = C["st"]

        A_T_DONE = C["t"]
        es_a.close()
        wgch = [sb(f"wgch{i}", [128, KS * 512], BF16) for i in range(2)]
        wgcl = [sb(f"wgcl{i}", [128, KS * 512], BF16) for i in range(2)]
        wuch = [sb(f"wuch{i}", [128, KS * 512], BF16) for i in range(2)]
        wucl = [sb(f"wucl{i}", [128, KS * 512], BF16) for i in range(2)]
        wdch = [sb(f"wdch{i}", [128, 4 * D], BF16) for i in range(2)]
        wdcl = [sb(f"wdcl{i}", [128, 4 * D], BF16) for i in range(2)]
        xqTh = [sb(f"xqTh{i}", [128, D], BF16) for i in range(2)]
        xqTl = [sb(f"xqTl{i}", [128, D], BF16) for i in range(2)]
        xmidb = [sb(f"xmidb{i}", [128, D]) for i in range(2)]
        hsc = sb("hsc", [128, 512])
        hsc2 = sb("hsc2", [128, 512])
        qxh2 = sb("qxh2", [128, 512], BF16)
        qxl2 = sb("qxl2", [128, 512], BF16)
        hTh = sb("hTh", [128, 512], BF16)
        hTl = sb("hTl", [128, 512], BF16)
        ybuf = sb("ybuf", [128, D])
        scr = qxq

        CH_OFF = [0]
        for c in CHUNKS:
            CH_OFF.append(CH_OFF[-1] + c)

        wbuf_guard = {0: 0, 1: 0}
        ld_chunk = {}
        ypair_guard = [0]
        n_pairs = n_tiles // 2 if "B" in phases else 0
        for p in range(n_pairs):
            tiles = [2 * p, 2 * p + 1]
            for j, t in enumerate(tiles):
                def _ldm(e, w, t=t, j=j, vg=C["v"], xst=xmid_st[t]):
                    e.wait_ge(SEM["v"], vg)
                    e.wait_ge(SEM["st"], xst * 16)
                    dma(e, "ld", xmidb[j][:], xmid_d[t * 128:(t + 1) * 128, :])
                body("ld", _ldm)
                ld_m = ld_mark()

                def _lnv(e, w, ld_m=ld_m, j=j):
                    w("ld", ld_m)
                    e.reduce_sum(out=stt[:, 0:1], in_=xmidb[j][:], axis=AX.X)
                    e.tensor_tensor(out=qu[:, 0:D], in0=xmidb[j][:], in1=xmidb[j][:],
                                    op=ALU.mult)
                    e.reduce_sum(out=stt[:, 1:2], in_=qu[:, 0:D], axis=AX.X)
                    e.tensor_scalar(out=stt[:, 2:3], in0=stt[:, 0:1], scalar1=1.0 / D,
                                    scalar2=None, op0=ALU.mult)
                    vsp(e)
                    e.tensor_scalar(out=stt[:, 3:4], in0=stt[:, 1:2], scalar1=1.0 / D,
                                    scalar2=EPS_LN, op0=ALU.mult, op1=ALU.add)
                    e.tensor_tensor(out=stt[:, 4:5], in0=stt[:, 2:3], in1=stt[:, 2:3],
                                    op=ALU.mult)
                    vsp(e)
                    e.tensor_tensor(out=stt[:, 5:6], in0=stt[:, 3:4],
                                    in1=stt[:, 4:5], op=ALU.subtract)
                    return vsp(e)
                body("v", _lnv)
                v_var = C["v"]

                def _lna2(e, w, v_var=v_var):
                    w("v", v_var)
                    e.activation(out=stt[:, 6:7], in_=stt[:, 5:6], func=AF.Ln)
                    asp(e, 2)
                    e.activation(out=stt[:, 7:8], in_=stt[:, 6:7], func=AF.Exp,
                                 scale=-0.5)
                    return asp(e, 2)
                body("a", _lna2)
                a_sqrt = C["a"]

                def _lnv3(e, w, a_sqrt=a_sqrt, j=j):
                    w("a", a_sqrt)
                    vsp(e)
                    e.tensor_tensor(out=stt[:, 8:9], in0=stt[:, 2:3], in1=stt[:, 7:8],
                                    op=ALU.mult)
                    vsp(e)
                    return e.tensor_scalar(out=qv[:, 0:D], in0=xmidb[j][:],
                                           scalar1=stt[:, 7:8], scalar2=stt[:, 8:9],
                                           op0=ALU.mult, op1=ALU.subtract)
                body("v", _lnv3)
                emit_quant(D, tdep=C["t"])
                v_q = C["v"]
                emit_T([qxh[:, i * 128:(i + 1) * 128] for i in range(KS)],
                       [(xqTh[j], i * 128) for i in range(KS)], deps=[("v", v_q)])
                emit_T([qxl[:, i * 128:(i + 1) * 128] for i in range(KS)],
                       [(xqTl[j], i * 128) for i in range(KS)])
                if debug and p == 0 and j == 0:
                    def _dq(e, w):
                        return e.tensor_copy(out=ybuf[:], in_=qxq[:, 0:D])
                    body("v", _dq)
                    dv2 = C["v"]
                    def _dq2(e, w, dv2=dv2):
                        w("v", dv2)
                        dma(e, "st", dxq_d[:, 0:D], ybuf[:])
                    body("st", _dq2)

            v_xq_pair = C["v"]

            steps = []
            for ci, CW in enumerate(CHUNKS):
                for j, t in enumerate(tiles):
                    steps.append((ci, CW, j, t))

            st_state = {}

            def b_quant(si):
                (ci, CW, j, t) = steps[si]
                m = st_state[si]
                bqh = qxh if si % 2 == 0 else qxh2
                bql = qxl if si % 2 == 0 else qxl2
                emit_quant(CW, qsrc=qvb[si % 2], qh=bqh, ql=bql,
                           tdep=st_state[si - 2]["t_tr"] if si >= 2 else 0)
                m["v_hq"] = C["v"]

            def b_tr(si):
                (ci, CW, j, t) = steps[si]
                cb = ci % 2
                m = st_state[si]
                bqh = qxh if si % 2 == 0 else qxh2
                bql = qxl if si % 2 == 0 else qxl2
                nsubk = CW // 128
                emit_T([bqh[:, i * 128:(i + 1) * 128] for i in range(nsubk)],
                       [(hTh, i * 128) for i in range(nsubk)],
                       deps=[("v", m["v_hq"])])
                m["v_evh"] = emit_T(
                    [bql[:, i * 128:(i + 1) * 128] for i in range(nsubk)],
                    [(hTl, i * 128) for i in range(nsubk)])
                m["t_tr"] = C["t"]

                pD = pV if j == 0 else pQ
                def _dn(e, w, v_hT=m["v_evh"], cb=cb, CW=CW, ci=ci, pD=pD):
                    w("v", v_hT)
                    last = None
                    nks = CW // 128
                    for ksi in range(nks):
                        sh = hTh[:, ksi * 128:(ksi + 1) * 128]
                        sl = hTl[:, ksi * 128:(ksi + 1) * 128]
                        first = (ci == 0 and ksi == 0)
                        lastk = (ci == len(CHUNKS) - 1 and ksi == nks - 1)
                        for nch in range(2):
                            rh = wdch[cb][:, ksi * D + nch * 512: ksi * D + nch * 512 + 512]
                            rl = wdcl[cb][:, ksi * D + nch * 512: ksi * D + nch * 512 + 512]
                            od = pD[:, nch * 512:(nch + 1) * 512]
                            last = e.matmul(out=od, lhsT=sh, rhs=rh,
                                            start=first, stop=False)
                            last = e.matmul(out=od, lhsT=sh, rhs=rl,
                                            start=False, stop=False)
                            last = e.matmul(out=od, lhsT=sl, rhs=rh,
                                            start=False, stop=lastk)
                    return last
                body("t", _dn)
                if j == 1:
                    wbuf_guard[cb] = C["t"]

            for si, (ci, CW, j, t) in enumerate(steps):
                n0 = CH_OFF[ci]
                cb = ci % 2
                if j == 0:
                    def _ldc(e, w, ci=ci, n0=n0, CW=CW, cb=cb,
                             tg=wbuf_guard[cb], pst=PREP_ST, atd=A_T_DONE):
                        e.wait_ge(SEM["st"], pst * 16)
                        e.wait_ge(SEM["t"], max(tg, atd))
                        for i in range(KS):
                            dma(e, "ld", wgch[cb][:, i * 512:i * 512 + CW],
                                wgqh_d[i, :, n0:n0 + CW])
                            dma(e, "ld", wgcl[cb][:, i * 512:i * 512 + CW],
                                wgql_d[i, :, n0:n0 + CW])
                            dma(e, "ld", wuch[cb][:, i * 512:i * 512 + CW],
                                wuqh_d[i, :, n0:n0 + CW])
                            dma(e, "ld", wucl[cb][:, i * 512:i * 512 + CW],
                                wuql_d[i, :, n0:n0 + CW])
                        for jj in range(CW // 128):
                            dma(e, "ld", wdch[cb][:, jj * D:(jj + 1) * D],
                                wdqh_d[n0 // 128 + jj, :, :])
                            dma(e, "ld", wdcl[cb][:, jj * D:(jj + 1) * D],
                                wdql_d[n0 // 128 + jj, :, :])
                    body("ld", _ldc)
                    ld_chunk[ci] = ld_mark()

                prev = st_state.get(si - 1)
                def _gu(e, w, j=j, cb=cb, CW=CW, ldc=ld_chunk[ci], prev=prev,
                        vxq=v_xq_pair):
                    w("ld", ldc)
                    w("v", vxq if prev is None else max(vxq, prev["v_guprod"]))
                    if prev is not None:
                        w("a", prev["a_eneg"])
                    last = None
                    for ksi in range(KS):
                        sh = xqTh[j][:, ksi * 128:(ksi + 1) * 128]
                        sl = xqTl[j][:, ksi * 128:(ksi + 1) * 128]
                        first = ksi == 0
                        lastk = ksi == KS - 1
                        for (wch_, wcl_, po) in ((wgch[cb], wgcl[cb], 0),
                                                 (wuch[cb], wucl[cb], 512)):
                            rh = wch_[:, ksi * 512:ksi * 512 + CW]
                            rl = wcl_[:, ksi * 512:ksi * 512 + CW]
                            od = pK[:, po:po + CW]
                            last = e.matmul(out=od, lhsT=sh, rhs=rh,
                                            start=first, stop=False)
                            last = e.matmul(out=od, lhsT=sh, rhs=rl,
                                            start=False, stop=False)
                            last = e.matmul(out=od, lhsT=sl, rhs=rh,
                                            start=False, stop=lastk)
                    return last
                body("t", _gu)
                t_gu = C["t"]

                if PIPE_B:
                    if si >= 1:
                        b_quant(si - 1)
                    if si >= 2:
                        b_tr(si - 2)

                def _eneg(e, w, t_gu=t_gu, CW=CW, prev=prev):
                    w("t", t_gu)
                    if prev is not None:
                        w("v", prev["v_h2"])
                    return e.activation(out=hsc[:, 0:CW], in_=pK[:, 0:CW],
                                        func=AF.Exp, scale=-1.0)
                body("a", _eneg)
                a_eneg = C["a"]

                def _h(e, w, t_gu=t_gu, CW=CW):
                    w("t", t_gu)
                    e.tensor_copy(out=hsc2[:, 0:CW], in_=pK[:, 512:512 + CW])
                    return e.tensor_tensor(out=hsc2[:, 0:CW], in0=pK[:, 0:CW],
                                           in1=hsc2[:, 0:CW], op=ALU.mult)
                body("v", _h)
                v_guprod = C["v"]

                def _h2(e, w, a_eneg=a_eneg, CW=CW, si=si):
                    w("a", a_eneg)
                    e.tensor_scalar(out=hsc[:, 0:CW], in0=hsc[:, 0:CW],
                                    scalar1=1.0, scalar2=None, op0=ALU.add)
                    e.reciprocal(out=hsc[:, 0:CW], in_=hsc[:, 0:CW])
                    return e.tensor_tensor(out=qvb[si % 2][:, 0:CW],
                                           in0=hsc2[:, 0:CW],
                                           in1=hsc[:, 0:CW], op=ALU.mult)
                body("v", _h2)
                v_h2 = C["v"]
                st_state[si] = {"v_guprod": v_guprod, "a_eneg": a_eneg,
                                "v_h2": v_h2}
                if not PIPE_B:
                    b_quant(si)
                    b_tr(si)
            if PIPE_B:
                b_quant(len(steps) - 1)
                b_tr(len(steps) - 2)
                b_tr(len(steps) - 1)

            for j, t in enumerate(tiles):
                pD = pV if j == 0 else pQ
                t_done = C["t"]

                def _yev(e, w, pD=pD, j=j, t_done=t_done, yg=ypair_guard[0]):
                    w("t", t_done)
                    if yg:
                        e.wait_ge(SEM["st"], yg * 16)
                    e.tensor_tensor(out=ybuf[:, 0:512], in0=pD[:, 0:512],
                                    in1=xmidb[j][:, 0:512], op=ALU.add)
                    return e.tensor_tensor(out=ybuf[:, 512:1024], in0=pD[:, 512:1024],
                                           in1=xmidb[j][:, 512:1024], op=ALU.add)
                body("v", _yev)
                v_y = C["v"]

                def _sty(e, w, t=t, v_y=v_y):
                    w("v", v_y)
                    dma(e, "st", y_d[t * 128:(t + 1) * 128, :], ybuf[:])
                body("st", _sty)
                ypair_guard[0] = C["st"]

        if "B" not in phases:
            def _ydummy(e, w):
                w("v", None)
                for t in range(n_tiles):
                    dma(e, "st", y_d[t * 128:(t + 1) * 128, :], ybuf[:])
            body("st", _ydummy)

        def _fin(e, w):
            e.wait_ge(SEM["st"], C["st"] * 16)
            return e.engine_nop()
        body("g", _fin)

    return nc


def kernel(L0_attn_x, L0_attn_Wq, L0_attn_Wk, L0_attn_Wv, L0_attn_Wo,
           L0_attn_bo, L0_ffn_w_gate, L0_ffn_w_up, L0_ffn_w_down):
    x = np.ascontiguousarray(np.asarray(L0_attn_x, dtype=np.float32).reshape(-1, D))
    wq = np.ascontiguousarray(np.asarray(L0_attn_Wq, dtype=np.float32))
    wk = np.ascontiguousarray(np.asarray(L0_attn_Wk, dtype=np.float32))
    wv = np.ascontiguousarray(np.asarray(L0_attn_Wv, dtype=np.float32))
    wo = np.ascontiguousarray(np.asarray(L0_attn_Wo, dtype=np.float32))
    bo = np.ascontiguousarray(np.asarray(L0_attn_bo, dtype=np.float32).reshape(1, D))
    wg = np.ascontiguousarray(np.asarray(L0_ffn_w_gate, dtype=np.float32))
    wu = np.ascontiguousarray(np.asarray(L0_ffn_w_up, dtype=np.float32))
    wd = np.ascontiguousarray(np.asarray(L0_ffn_w_down, dtype=np.float32))

    nc = build_program()
    in_maps = []
    for i in range(N_CORES):
        in_maps.append({
            "x": x[i * TOK_PER_CORE:(i + 1) * TOK_PER_CORE],
            "wq": wq, "wk": wk, "wv": wv, "wo": wo, "bo2": bo,
            "wg": wg, "wu": wu, "wd": wd,
        })
    res = run_bass_kernel_spmd(nc, in_maps, core_ids=list(range(N_CORES)))
    y = np.concatenate([np.asarray(r["y"]) for r in res.results], axis=0)
    return y.reshape(4, 4096, D).astype(np.float32)


# version 13 (final)
# speedup vs baseline: 1.2086x; 1.0354x over previous
import numpy as np
import concourse.bass as bass
import concourse.mybir as mybir
from concourse.bass_utils import run_bass_kernel_spmd

F32 = mybir.dt.float32
BF16 = mybir.dt.bfloat16
AF = mybir.ActivationFunctionType
ALU = mybir.AluOpType
AX = mybir.AxisListType

N_CORES = 8
D = 1024
NH = 16
HD = 64
I = 2816
TOK_PER_CORE = 2048
N_TILES = TOK_PER_CORE // 128
KS = D // 128
NT_W = I // 128
CHUNKS = [384, 512, 512, 512, 512, 384]
assert sum(CHUNKS) == I
EPS_LN = 1e-5
EPS_ATT = 1e-6
QTHR = [0.25, 0.75, 1.25, 1.75, 2.5, 3.5, 5.0]
PIPE_PREP = True
PIPE_B = True


def build_program(n_tiles=N_TILES, debug=False, phases="PAB"):
    nc = bass.Bass("TRN2", target_bir_lowering=False, detect_race_conditions=False)
    ntok = n_tiles * 128

    x_d = nc.dram_tensor("x", [ntok, D], F32, kind="ExternalInput").ap()
    wq_d = nc.dram_tensor("wq", [D, D], F32, kind="ExternalInput").ap()
    wk_d = nc.dram_tensor("wk", [D, D], F32, kind="ExternalInput").ap()
    wv_d = nc.dram_tensor("wv", [D, D], F32, kind="ExternalInput").ap()
    wo_d = nc.dram_tensor("wo", [D, D], F32, kind="ExternalInput").ap()
    bo_d = nc.dram_tensor("bo2", [1, D], F32, kind="ExternalInput").ap()
    wg_d = nc.dram_tensor("wg", [I, D], F32, kind="ExternalInput").ap()
    wu_d = nc.dram_tensor("wu", [I, D], F32, kind="ExternalInput").ap()
    wd_d = nc.dram_tensor("wd", [D, I], F32, kind="ExternalInput").ap()
    y_d = nc.dram_tensor("y", [ntok, D], F32, kind="ExternalOutput").ap()
    if debug:
        dxmid_d = nc.dram_tensor("dxmid", [ntok, D], F32, kind="ExternalOutput").ap()
        dwg_d = nc.dram_tensor("dwg", [128, I], F32, kind="ExternalOutput").ap()
        dwgl_d = nc.dram_tensor("dwgl", [128, I], F32, kind="ExternalOutput").ap()
        dxq_d = nc.dram_tensor("dxq", [128, D], F32, kind="ExternalOutput").ap()
        dxln_d = nc.dram_tensor("dxln", [128, D], F32, kind="ExternalOutput").ap()
        dq_d = nc.dram_tensor("dq", [128, D], F32, kind="ExternalOutput").ap()
        dk_d = nc.dram_tensor("dk", [128, D], F32, kind="ExternalOutput").ap()
        dv_d = nc.dram_tensor("dv", [128, D], F32, kind="ExternalOutput").ap()
        dattn_d = nc.dram_tensor("dattn", [128, D], F32, kind="ExternalOutput").ap()
        dqn_d = nc.dram_tensor("dqn", [128, D], F32, kind="ExternalOutput").ap()
        dstt_d = nc.dram_tensor("dstt", [128, 8], F32, kind="ExternalOutput").ap()
        dxt_d = nc.dram_tensor("dxt", [128, D], F32, kind="ExternalOutput").ap()

    xmid_d = nc.dram_tensor("xmid", [ntok, D], F32, kind="Internal").ap()
    wgqh_d = nc.dram_tensor("wgqh", [KS, 128, I], BF16, kind="Internal").ap()
    wgql_d = nc.dram_tensor("wgql", [KS, 128, I], BF16, kind="Internal").ap()
    wuqh_d = nc.dram_tensor("wuqh", [KS, 128, I], BF16, kind="Internal").ap()
    wuql_d = nc.dram_tensor("wuql", [KS, 128, I], BF16, kind="Internal").ap()
    wdqh_d = nc.dram_tensor("wdqh", [NT_W, 128, D], BF16, kind="Internal").ap()
    wdql_d = nc.dram_tensor("wdql", [NT_W, 128, D], BF16, kind="Internal").ap()

    from contextlib import ExitStack
    es = ExitStack()
    with es:
        sb = lambda name, shape, dt=F32: es.enter_context(nc.sbuf_tensor(name, shape, dt))
        ps = lambda name, shape, dt=F32: es.enter_context(nc.psum_tensor(name, shape, dt))
        sem = lambda name: es.enter_context(nc.semaphore(name))

        LD_SEMS = [sem(f"s_ld{i}") for i in range(4)]
        C_LD = [0, 0, 0, 0]
        ld_rr = [0]
        s_ld = LD_SEMS[0]
        s_st = sem("s_st")
        s_v = sem("s_v")
        s_a = sem("s_a")
        s_t = sem("s_t")
        s_g = sem("s_g")
        C = {"st": 0, "v": 0, "a": 0, "t": 0, "g": 0}
        SEM = {"st": s_st, "v": s_v, "a": s_a, "t": s_t, "g": s_g}
        ENG = {"v": "vector", "a": "scalar", "t": "tensor", "g": "gpsimd",
               "ld": "sync", "st": "gpsimd"}

        nums = sorted([h.num for h in SEM.values()] + [h.num for h in LD_SEMS])
        nc.all_engine_barrier()
        nc.gpsimd.dma_reset(range(nums[0], nums[-1] + 1))
        nc.gpsimd.sem_clear(range(nums[0], nums[-1] + 1))
        nc.all_engine_barrier()
        nc.all_engine_barrier()

        block = es.enter_context(nc.Block())

        def body(kind, fn):
            def runner(e):
                def w(dep, cnt=None):
                    if dep == "ld":
                        wld(e, ld_mark() if cnt is None else cnt)
                        return
                    c = C[dep] if cnt is None else cnt
                    if c > 0:
                        e.wait_ge(SEM[dep], c * 16 if dep == "st" else c)
                last = fn(e, w)
                if kind in ("ld", "st"):
                    pass
                else:
                    assert last is not None, "compute body must return last instr"
                    if kind == "v":
                        e.engine_nop().then_inc(SEM[kind], 1)
                    elif kind == "g":
                        e.memset(junkg[:], 0.0).then_inc(SEM[kind], 1)
                    else:
                        last.then_inc(SEM[kind], 1)
                    C[kind] += 1
            getattr(block, ENG[kind])(runner)

        LAST_LD = [(0, 0)]

        def dma(e, kind, out, in_):
            if kind == "ld":
                i = ld_rr[0] % 4
                ld_rr[0] += 1
                e.dma_start(out=out, in_=in_).then_inc(LD_SEMS[i], 16)
                C_LD[i] += 1
                LAST_LD[0] = (i, C_LD[i])
            else:
                e.dma_start(out=out, in_=in_).then_inc(SEM[kind], 16)
                C[kind] += 1

        def ld_mark():
            return LAST_LD[0]

        def wld(e, mark):
            i, c = mark
            if c > 0:
                e.wait_ge(LD_SEMS[i], c * 16)

        def vsp(e, n=2):
            last = None
            for _ in range(n):
                last = e.tensor_copy(out=junk[:], in_=junk[:])
            return last

        def asp(e, n=1):
            last = None
            for _ in range(n):
                last = e.activation(out=junka[:], in_=junka[:], func=AF.Copy)
            return last

        bo_rep = sb("bo_rep", [128, D])
        idb = sb("idb", [128, 128], BF16)
        c_eps = sb("c_eps", [128, 1])
        scores = sb("scores", [128, 160])
        stt = sb("stt", [128, 16])
        junk = sb("junk", [128, 128])
        junka = sb("junka", [128, 64])
        junkg = sb("junkg", [128, 4])

        qv = sb("qv", [128, D])
        qv2 = sb("qv2", [128, D])
        qu = sb("qu", [128, D])
        qsg = sb("qsg", [128, D], BF16)
        mA = sb("mA", [128, D], BF16); mB = sb("mB", [128, D], BF16)
        mC = sb("mC", [128, D], BF16)
        qmb = sb("qmb", [128, D // 16])
        qsc = sb("qsc", [128, D // 16])
        qrc = sb("qrc", [128, D // 16])
        qxq = sb("qxq", [128, D])
        qxh = sb("qxh", [128, D], BF16); qxl = sb("qxl", [128, D], BF16)

        es_a = ExitStack()
        sba = lambda name, shape, dt=F32: es_a.enter_context(nc.sbuf_tensor(name, shape, dt))
        stage = sba("stage", [128, D], BF16)
        stage2 = sba("stage2", [128, D], BF16)
        wqh = sba("wqh", [128, KS * D], BF16); wql = sba("wql", [128, KS * D], BF16)
        wkh = sba("wkh", [128, KS * D], BF16); wkl = sba("wkl", [128, KS * D], BF16)
        wvh = sba("wvh", [128, KS * D], BF16); wvl = sba("wvl", [128, KS * D], BF16)
        woh = sba("woh", [128, KS * D], BF16); wol = sba("wol", [128, KS * D], BF16)
        xa = sba("xa", [128, D]); xb = sba("xb", [128, D])
        qb = sba("qb", [128, D]); kb = sba("kb", [128, D]); vb = sba("vb", [128, D])
        kc = sba("kc", [128, D]); attn = sba("attn", [128, D])
        xh = sba("xh", [128, D], BF16); xl = sba("xl", [128, D], BF16)
        xhT = sba("xhT", [128, D], BF16); xlT = sba("xlT", [128, D], BF16)
        xov = sba("xov", [128, D])
        scrb = attn
        scr = qxq
        scr2 = qv

        pTA = ps("pTA", [128, 512])
        pTB = ps("pTB", [128, 512])
        pQ = ps("pQ", [128, D])
        pK = ps("pK", [128, D])
        pV = ps("pV", [128, D])

        def _setup_sync(e, w):
            dma(e, "ld", bo_rep[:], bo_d.partition_broadcast(128))
        body("ld", _setup_sync)

        def _setup_g(e, w):
            e.memset(idb[:], 0.0).then_inc(SEM["g"], 1)
            C["g"] += 1
            e.wait_ge(SEM["g"], C["g"])
            return e.affine_select(out=idb[:], in_=idb[:], compare_op=ALU.not_equal,
                                   fill=1.0, base=0, pattern=[[-1, 128]],
                                   channel_multiplier=1)
        body("g", _setup_g)
        G_SETUP = C["g"]

        def _setup_v(e, w):
            e.memset(c_eps[:], EPS_LN)
            e.memset(junka[:], 0.0)
            return e.memset(junk[:], 0.0)
        body("v", _setup_v)
        V_SETUP = C["v"]

        wd_pairs = [(wq_d, wqh, wql), (wk_d, wkh, wkl),
                    (wv_d, wvh, wvl), (wo_d, woh, wol)]
        raws = [qv, qu]
        last_split_v = {}
        idx = 0
        for (wdram, whi, wlo) in wd_pairs:
            for ksi in range(KS):
                rb = raws[idx % 2]
                vcnt_guard = last_split_v.get(idx % 2, 0)

                def _ld(e, w, rb=rb, wdram=wdram, ksi=ksi, vg=vcnt_guard):
                    if vg:
                        e.wait_ge(SEM["v"], vg)
                    dma(e, "ld", rb[:, 0:D], wdram[ksi * 128:(ksi + 1) * 128, :])
                body("ld", _ld)
                ld_now = ld_mark()

                def _sp(e, w, rb=rb, whi=whi, wlo=wlo, ksi=ksi, ld_now=ld_now):
                    w("ld", ld_now)
                    e.tensor_copy(out=whi[:, ksi * D:(ksi + 1) * D], in_=rb[:, 0:D])
                    return e.tensor_tensor(out=wlo[:, ksi * D:(ksi + 1) * D],
                                           in0=rb[:, 0:D],
                                           in1=whi[:, ksi * D:(ksi + 1) * D],
                                           op=ALU.subtract)
                body("v", _sp)
                last_split_v[idx % 2] = C["v"]
                idx += 1

        def emit_quant(F, qsrc=None, tdep=0, qh=None, ql=None):
            nb = F // 16
            if qsrc is None:
                qsrc = qv
            if qh is None:
                qh = qxh
            if ql is None:
                ql = qxl

            def _q1(e, w):
                if tdep:
                    e.wait_ge(SEM["t"], tdep)
                e.reduce_max(out=qmb[:, 0:nb],
                             in_=qsrc[:, 0:F].rearrange("p (b s) -> p b s", s=16),
                             axis=AX.X, apply_absolute_value=True)
                vsp(e)
                e.tensor_scalar(out=qsc[:, 0:nb], in0=qmb[:, 0:nb],
                                scalar1=1.0 / 6.0, scalar2=1e-12,
                                op0=ALU.mult, op1=ALU.max)
                vsp(e)
                e.reciprocal(out=qrc[:, 0:nb], in_=qsc[:, 0:nb])
                vsp(e)
                return e.tensor_tensor(
                    out=qsrc[:, 0:F].rearrange("p (b s) -> p b s", s=16),
                    in0=qsrc[:, 0:F].rearrange("p (b s) -> p b s", s=16),
                    in1=qrc[:, 0:nb].unsqueeze(2).broadcast_to([128, nb, 16]),
                    op=ALU.mult)
            body("v", _q1)
            v_q1 = C["v"]

            def _qa(e, w):
                w("v", v_q1)
                e.activation(out=qu[:, 0:F], in_=qsrc[:, 0:F], func=AF.Abs)
                return e.activation(out=qsg[:, 0:F], in_=qsrc[:, 0:F], func=AF.Tanh,
                                    scale=1e30)
            body("a", _qa)
            a_q = C["a"]

            def _q2(e, w):
                w("a", a_q)
                u = qu[:, 0:F]
                sp = (F < 512)
                e.tensor_scalar(out=mA[:, 0:F], in0=u, scalar1=QTHR[0], scalar2=None, op0=ALU.is_gt)
                e.tensor_scalar(out=mB[:, 0:F], in0=u, scalar1=QTHR[1], scalar2=None, op0=ALU.is_gt)
                if sp: vsp(e)
                e.tensor_tensor(out=mA[:, 0:F], in0=mA[:, 0:F], in1=mB[:, 0:F], op=ALU.add)
                e.tensor_scalar(out=mB[:, 0:F], in0=u, scalar1=QTHR[2], scalar2=None, op0=ALU.is_gt)
                e.tensor_scalar(out=mC[:, 0:F], in0=u, scalar1=QTHR[3], scalar2=None, op0=ALU.is_gt)
                if sp: vsp(e)
                e.tensor_tensor(out=mB[:, 0:F], in0=mB[:, 0:F], in1=mC[:, 0:F], op=ALU.add)
                if sp: vsp(e)
                e.tensor_tensor(out=mA[:, 0:F], in0=mA[:, 0:F], in1=mB[:, 0:F], op=ALU.add)
                e.tensor_scalar(out=mB[:, 0:F], in0=u, scalar1=QTHR[4], scalar2=None, op0=ALU.is_gt)
                e.tensor_scalar(out=mC[:, 0:F], in0=u, scalar1=QTHR[5], scalar2=None, op0=ALU.is_gt)
                if sp: vsp(e)
                e.tensor_tensor(out=mB[:, 0:F], in0=mB[:, 0:F], in1=mC[:, 0:F], op=ALU.add)
                e.tensor_scalar(out=mC[:, 0:F], in0=u, scalar1=QTHR[6], scalar2=None, op0=ALU.is_gt)
                if sp: vsp(e)
                e.scalar_tensor_tensor(out=mA[:, 0:F], in0=mA[:, 0:F], scalar=0.5,
                                       in1=mB[:, 0:F], op0=ALU.mult, op1=ALU.add)
                if sp: vsp(e)
                e.scalar_tensor_tensor(out=mA[:, 0:F], in0=mC[:, 0:F], scalar=2.0,
                                       in1=mA[:, 0:F], op0=ALU.mult, op1=ALU.add)
                if sp: vsp(e)
                e.tensor_tensor(out=mA[:, 0:F], in0=mA[:, 0:F], in1=qsg[:, 0:F], op=ALU.mult)
                if sp: vsp(e)
                e.tensor_tensor(out=qxq[:, 0:F].rearrange("p (b s) -> p b s", s=16),
                                in0=mA[:, 0:F].rearrange("p (b s) -> p b s", s=16),
                                in1=qsc[:, 0:nb].unsqueeze(2).broadcast_to([128, nb, 16]),
                                op=ALU.mult)
                if sp: vsp(e)
                e.tensor_copy(out=qh[:, 0:F], in_=qxq[:, 0:F])
                if sp: vsp(e)
                return e.tensor_tensor(out=ql[:, 0:F], in0=qxq[:, 0:F],
                                       in1=qh[:, 0:F], op=ALU.subtract)
            body("v", _q2)

        bank_free = {0: 0, 1: 0}

        def emit_T(srcs, dsts, deps=(), st_dep=0):
            gi = 0
            out_a = 0
            for g0 in range(0, len(srcs), 4):
                gs = list(range(g0, min(g0 + 4, len(srcs))))
                bank = pTA if gi % 2 == 0 else pTB
                bsel = gi % 2
                gdeps = list(deps) if g0 == 0 else []
                bf = bank_free[bsel]

                def _tr(e, w, gs=gs, bank=bank, gdeps=gdeps, bf=bf, srcs=srcs):
                    for dk, dc in gdeps:
                        w(dk, dc)
                    if bf:
                        w("v", bf)
                    last = None
                    for i, si_ in enumerate(gs):
                        last = e.matmul(out=bank[:, i * 128:(i + 1) * 128],
                                        lhsT=srcs[si_], rhs=idb[:],
                                        start=(i == 0), stop=(i == len(gs) - 1))
                    return last
                body("t", _tr)
                t_now = C["t"]

                dt0, dc0 = dsts[g0]
                def _ev(e, w, t_now=t_now, bank=bank, n=len(gs), dt0=dt0,
                        dc0=dc0, st_dep=st_dep):
                    if st_dep:
                        e.wait_ge(SEM["st"], st_dep * 16)
                    w("t", t_now)
                    return e.tensor_copy(out=dt0[:, dc0:dc0 + n * 128],
                                         in_=bank[:, 0:n * 128])
                body("v", _ev)
                bank_free[bsel] = C["v"]
                out_a = C["v"]
                gi += 1
            return out_a

        st_guard = {0: 0, 1: 0}
        prep_items = []
        for (wnat_d, dsth, dstl) in [(wg_d, wgqh_d, wgql_d), (wu_d, wuqh_d, wuql_d)]:
            for nt in range(NT_W):
                prep_items.append((wnat_d, nt, 0, D, dsth, dstl,
                                   [(i, nt * 128) for i in range(KS)]))
        WD_SLICES = [(0, 1024), (1024, 1024), (2048, 768)]
        for nt in range(KS):
            for (s0, SW) in WD_SLICES:
                prep_items.append((wd_d, nt, s0, SW, wdqh_d, wdql_d,
                                   [(s0 // 128 + i, nt * 128) for i in range(SW // 128)]))

        prep_m = {}
        def prep_back(n):
            (wnat_d, nt, s0, SW, dsth, dstl, blocks) = prep_items[n]
            m = prep_m[n]
            nblk = len(blocks)
            emit_T([qxh[:, i * 128:(i + 1) * 128] for i in range(nblk)],
                   [(stage, i * 128) for i in range(nblk)],
                   deps=[("v", m["v_q"])], st_dep=st_guard[0])
            v_ev = emit_T([qxl[:, i * 128:(i + 1) * 128] for i in range(nblk)],
                          [(stage2, i * 128) for i in range(nblk)],
                          st_dep=st_guard[0])
            m["t_tr"] = C["t"]

            def _spill(e, w, blocks=blocks, dsth=dsth, dstl=dstl, v_ev=v_ev):
                w("v", v_ev)
                for bi, (ksd, co) in enumerate(blocks):
                    dma(e, "st", dsth[ksd, :, co:co + 128],
                        stage[:, bi * 128:(bi + 1) * 128])
                    dma(e, "st", dstl[ksd, :, co:co + 128],
                        stage2[:, bi * 128:(bi + 1) * 128])
            body("st", _spill)
            st_guard[0] = C["st"]

        qvb = [qv, qv2]
        S1_V_DONE = C["v"]
        for n, (wnat_d, nt, s0, SW, dsth, dstl, blocks) in enumerate(prep_items):
            pm = prep_m.get(n - 2)
            def _ldw(e, w, wnat_d=wnat_d, nt=nt, s0=s0, SW=SW, n=n, pm=pm):
                if pm is not None:
                    e.wait_ge(SEM["v"], pm["v_q1"])
                    e.wait_ge(SEM["a"], pm["a_qa"])
                else:
                    e.wait_ge(SEM["v"], S1_V_DONE)
                dma(e, "ld", qvb[n % 2][:, 0:SW],
                    wnat_d[nt * 128:(nt + 1) * 128, s0:s0 + SW])
            body("ld", _ldw)
            ld_now = ld_mark()

            if PIPE_PREP and n >= 1:
                prep_back(n - 1)

            def _qdep(e, w, ld_now=ld_now):
                w("ld", ld_now)
                return e.engine_nop()
            body("v", _qdep)
            v_q1_pre = C["v"]
            emit_quant(SW, qsrc=qvb[n % 2],
                       tdep=(prep_m[n - 1]["t_tr"] if (PIPE_PREP and n >= 1) else 0))
            prep_m[n] = {"v_q": C["v"], "v_q1": v_q1_pre + 2,
                         "a_qa": C["a"]}
            if not PIPE_PREP:
                prep_back(n)
        if PIPE_PREP:
            prep_back(len(prep_items) - 1)

        PREP_ST = C["st"]
        if debug:
            def _dldw(e, w, pst=PREP_ST):
                e.wait_ge(SEM["st"], pst * 16)
                dma(e, "ld", qxh[:, 0:D], wgqh_d[0, :, 0:D])
                dma(e, "ld", qxl[:, 0:D], wgql_d[0, :, 0:D])
            body("ld", _dldw)
            dld = ld_mark()
            def _dcvt(e, w, dld=dld):
                w("ld", dld)
                e.tensor_copy(out=qv[:, 0:D], in_=qxh[:, 0:D])
                return e.tensor_copy(out=qu[:, 0:D], in_=qxl[:, 0:D])
            body("v", _dcvt)
            dv = C["v"]
            def _dst(e, w, dv=dv):
                w("v", dv)
                dma(e, "st", dwg_d[:, 0:D], qv[:, 0:D])
                dma(e, "st", dwgl_d[:, 0:D], qu[:, 0:D])
            body("st", _dst)

        TRI_OFF = [h * (h + 1) // 2 for h in range(NH + 1)]

        PREP_A_DONE = C["a"]
        a_resid_v = {}
        xmid_st = {}
        AM = {}

        def a_front(t):
            xt = xa if t % 2 == 0 else xb
            m = {}
            AM[t] = m

            def _ldx(e, w, t=t, xt=xt, vg=a_resid_v.get(t - 2, 0)):
                if vg:
                    e.wait_ge(SEM["v"], vg)
                dma(e, "ld", xt[:], x_d[t * 128:(t + 1) * 128, :])
            body("ld", _ldx)
            ld_x = ld_mark()

            def _lnv(e, w, ld_x=ld_x, xt=xt, t0=(t == 0)):
                w("ld", ld_x)
                if t0:
                    w("a", PREP_A_DONE)
                e.reduce_sum(out=stt[:, 0:1], in_=xt[:], axis=AX.X)
                e.tensor_tensor(out=scr[:, 0:D], in0=xt[:], in1=xt[:], op=ALU.mult)
                e.reduce_sum(out=stt[:, 1:2], in_=scr[:, 0:D], axis=AX.X)
                e.tensor_scalar(out=stt[:, 2:3], in0=stt[:, 0:1], scalar1=1.0 / D,
                                scalar2=None, op0=ALU.mult)
                vsp(e)
                e.tensor_scalar(out=stt[:, 3:4], in0=stt[:, 1:2], scalar1=1.0 / D,
                                scalar2=EPS_LN, op0=ALU.mult, op1=ALU.add)
                e.tensor_tensor(out=stt[:, 4:5], in0=stt[:, 2:3], in1=stt[:, 2:3],
                                op=ALU.mult)
                vsp(e)
                e.tensor_tensor(out=stt[:, 5:6], in0=stt[:, 3:4],
                                in1=stt[:, 4:5], op=ALU.subtract)
                return vsp(e)
            body("v", _lnv)
            v_var = C["v"]

            def _lna2(e, w, v_var=v_var):
                w("v", v_var)
                e.activation(out=stt[:, 6:7], in_=stt[:, 5:6], func=AF.Ln)
                asp(e, 2)
                e.activation(out=stt[:, 7:8], in_=stt[:, 6:7], func=AF.Exp,
                             scale=-0.5)
                return asp(e, 2)
            body("a", _lna2)
            a_sqrt = C["a"]

            def _lnv3(e, w, a_sqrt=a_sqrt, xt=xt,
                      tg=(AM.get(t - 2, {}).get("t_atr", 0) if t != 1
                          else AM[0]["t_xtr"])):
                w("a", a_sqrt)
                if tg:
                    w("t", tg)
                vsp(e)
                e.tensor_tensor(out=stt[:, 8:9], in0=stt[:, 2:3], in1=stt[:, 7:8],
                                op=ALU.mult)
                vsp(e)
                e.tensor_scalar(out=scr[:, 0:D], in0=xt[:], scalar1=stt[:, 7:8],
                                scalar2=stt[:, 8:9], op0=ALU.mult, op1=ALU.subtract)
                e.tensor_copy(out=xh[:], in_=scr[:, 0:D])
                return e.tensor_tensor(out=xl[:], in0=scr[:, 0:D], in1=xh[:],
                                       op=ALU.subtract)
            body("v", _lnv3)
            v_split = C["v"]

            emit_T([xh[:, i * 128:(i + 1) * 128] for i in range(KS)],
                   [(xhT, i * 128) for i in range(KS)], deps=[("v", v_split)])
            m["t_xtr"] = C["t"]
            v_xT = emit_T([xl[:, i * 128:(i + 1) * 128] for i in range(KS)],
                          [(xlT, i * 128) for i in range(KS)])
            m["t_xtr"] = C["t"]

            def _qkv(e, w, v_xT=v_xT):
                w("v", v_xT)
                last = None
                for ksi in range(KS):
                    sh = xhT[:, ksi * 128:(ksi + 1) * 128]
                    sl = xlT[:, ksi * 128:(ksi + 1) * 128]
                    first = ksi == 0
                    lastk = ksi == KS - 1
                    for (whi, wlo, pdst) in ((wqh, wql, pQ), (wkh, wkl, pK),
                                             (wvh, wvl, pV)):
                        for nch in range(2):
                            rh = whi[:, ksi * D + nch * 512: ksi * D + nch * 512 + 512]
                            rl = wlo[:, ksi * D + nch * 512: ksi * D + nch * 512 + 512]
                            od = pdst[:, nch * 512:(nch + 1) * 512]
                            last = e.matmul(out=od, lhsT=sh, rhs=rh,
                                            start=first, stop=False)
                            last = e.matmul(out=od, lhsT=sh, rhs=rl,
                                            start=False, stop=False)
                            last = e.matmul(out=od, lhsT=sl, rhs=rh,
                                            start=False, stop=lastk)
                return last
            body("t", _qkv)
            m["t_qkv"] = C["t"]

        def a_evqkv(t):
            def _evqkv(e, w, tq=AM[t]["t_qkv"]):
                w("t", tq)
                e.tensor_copy(out=qb[:], in_=pQ[:])
                e.tensor_copy(out=kb[:], in_=pK[:])
                return e.tensor_copy(out=vb[:], in_=pV[:])
            body("v", _evqkv)
            AM[t]["v_qkv"] = C["v"]

        a_front(0)
        a_evqkv(0)
        for t in range(n_tiles):
            xt = xa if t % 2 == 0 else xb
            m = AM[t]

            def _smx(e, w, vq=m["v_qkv"]):
                w("v", vq)
                e.reduce_max(out=scores[:, 136:136 + NH],
                             in_=qb.rearrange("p (h d) -> p h d", d=HD), axis=AX.X)
                e.tensor_scalar(out=scrb[:], in0=kb[:], scalar1=0.0, scalar2=None,
                                op0=ALU.min)
                e.tensor_scalar(out=kb[:], in0=kb[:], scalar1=0.0,
                                scalar2=None, op0=ALU.max)
                return e.tensor_tensor(out=qb.rearrange("p (h d) -> p h d", d=HD),
                                in0=qb.rearrange("p (h d) -> p h d", d=HD),
                                in1=scores[:, 136:136 + NH].unsqueeze(2)
                                    .broadcast_to([128, NH, HD]), op=ALU.subtract)
            body("v", _smx)
            v_smx = C["v"]

            def _exp(e, w, v_smx=v_smx):
                w("v", v_smx)
                e.activation(out=qb[:], in_=qb[:], func=AF.Exp)
                return e.activation(out=scrb[:], in_=scrb[:], func=AF.Exp)
            body("a", _exp)
            a_exp = C["a"]

            if t + 1 < n_tiles:
                a_front(t + 1)

            def _mix1(e, w, a_exp=a_exp):
                w("a", a_exp)
                e.tensor_tensor(out=kb[:], in0=kb[:], in1=scrb[:], op=ALU.add)
                e.reduce_sum(out=scores[:, 136:136 + NH],
                             in_=qb.rearrange("p (h d) -> p h d", d=HD), axis=AX.X)
                def dstep(out_t, in_t, off):
                    e.tensor_copy(out=out_t[:, 0:off * HD], in_=in_t[:, 0:off * HD])
                    e.tensor_tensor(out=out_t[:, off * HD:D],
                                    in0=in_t[:, off * HD:D],
                                    in1=in_t[:, 0:D - off * HD], op=ALU.add)
                dstep(scrb, kb, 1)
                dstep(kc, scrb, 2)
                dstep(scrb, kc, 4)
                dstep(kc, scrb, 8)
                e.reciprocal(out=scores[:, 136:136 + NH], in_=scores[:, 136:136 + NH])
                e.tensor_scalar(out=kc[:], in0=kc[:], scalar1=EPS_ATT, scalar2=None,
                                op0=ALU.add)
                e.reciprocal(out=kc[:], in_=kc[:])
                e.tensor_tensor(out=qb.rearrange("p (h d) -> p h d", d=HD),
                                in0=qb.rearrange("p (h d) -> p h d", d=HD),
                                in1=scores[:, 136:136 + NH].unsqueeze(2)
                                    .broadcast_to([128, NH, HD]), op=ALU.mult)
                return e.tensor_tensor(out=qb[:], in0=qb[:], in1=kc[:], op=ALU.mult)
            body("v", _mix1)
            v_mix1 = C["v"]

            def _mix2(e, w, v_mix1=v_mix1, tg=AM.get(t + 1, {}).get("t_xtr", 0)):
                w("v", v_mix1)
                if tg:
                    w("t", tg)
                pbuf = [scr2, xov]

                def smult(h):
                    nh2 = h + 1
                    e.tensor_tensor(
                        out=pbuf[h % 2][:, 0:nh2 * HD].rearrange("p (a d) -> p a d", d=HD),
                        in0=qb[:, h * HD:(h + 1) * HD].unsqueeze(1)
                            .broadcast_to([128, nh2, HD]),
                        in1=kb[:, 0:nh2 * HD].rearrange("p (a d) -> p a d", d=HD),
                        op=ALU.mult)

                def sred(h):
                    nh2 = h + 1
                    e.reduce_sum(out=scores[:, TRI_OFF[h]:TRI_OFF[h] + nh2],
                                 in_=pbuf[h % 2][:, 0:nh2 * HD]
                                     .rearrange("p (a d) -> p a d", d=HD),
                                 axis=AX.X)

                for h in range(NH):
                    smult(h)
                    if h >= 1:
                        sred(h - 1)
                sred(NH - 1)

                def amult(h):
                    nh2 = h + 1
                    e.tensor_tensor(
                        out=pbuf[h % 2][:, 0:nh2 * HD].rearrange("p (a d) -> p a d", d=HD),
                        in0=vb[:, 0:nh2 * HD].rearrange("p (a d) -> p a d", d=HD),
                        in1=scores[:, TRI_OFF[h]:TRI_OFF[h] + nh2].unsqueeze(2)
                            .broadcast_to([128, nh2, HD]),
                        op=ALU.mult)

                def ared(h):
                    nh2 = h + 1
                    e.reduce_sum(
                        out=attn[:, h * HD:(h + 1) * HD],
                        in_=pbuf[h % 2][:, 0:nh2 * HD]
                            .rearrange("p (a d) -> p a d", d=HD).transpose([0, 2, 1]),
                        axis=AX.X)

                for h in range(NH):
                    amult(h)
                    if h >= 1:
                        ared(h - 1)
                ared(NH - 1)
                e.tensor_copy(out=xh[:], in_=attn[:])
                return e.tensor_tensor(out=xl[:], in0=attn[:], in1=xh[:],
                                       op=ALU.subtract)
            body("v", _mix2)
            v_mix2 = C["v"]

            emit_T([xh[:, i * 128:(i + 1) * 128] for i in range(KS)],
                   [(xhT, i * 128) for i in range(KS)], deps=[("v", v_mix2)])
            m["t_atr"] = C["t"]
            v_aT = emit_T([xl[:, i * 128:(i + 1) * 128] for i in range(KS)],
                          [(xlT, i * 128) for i in range(KS)])
            m["t_atr"] = C["t"]

            if t + 1 < n_tiles:
                a_evqkv(t + 1)

            def _omm(e, w, v_aT=v_aT, vq1=AM.get(t + 1, {}).get("v_qkv", 0)):
                w("v", max(v_aT, vq1))
                last = None
                for ksi in range(KS):
                    sh = xhT[:, ksi * 128:(ksi + 1) * 128]
                    sl = xlT[:, ksi * 128:(ksi + 1) * 128]
                    first = ksi == 0
                    lastk = ksi == KS - 1
                    for nch in range(2):
                        rh = woh[:, ksi * D + nch * 512: ksi * D + nch * 512 + 512]
                        rl = wol[:, ksi * D + nch * 512: ksi * D + nch * 512 + 512]
                        od = pQ[:, nch * 512:(nch + 1) * 512]
                        last = e.matmul(out=od, lhsT=sh, rhs=rh, start=first, stop=False)
                        last = e.matmul(out=od, lhsT=sh, rhs=rl, start=False, stop=False)
                        last = e.matmul(out=od, lhsT=sl, rhs=rh, start=False, stop=lastk)
                return last
            body("t", _omm)
            t_o = C["t"]

            def _oev(e, w, t_o=t_o, xt=xt, sg=st_guard[1]):
                w("t", t_o)
                if sg:
                    e.wait_ge(SEM["st"], sg * 16)
                e.scalar_tensor_tensor(out=xov[:, 0:512], in0=pQ[:, 0:512],
                                       scalar=1.0, in1=bo_rep[:, 0:512],
                                       op0=ALU.mult, op1=ALU.add)
                e.scalar_tensor_tensor(out=xov[:, 512:1024], in0=pQ[:, 512:1024],
                                       scalar=1.0, in1=bo_rep[:, 512:1024],
                                       op0=ALU.mult, op1=ALU.add)
                return e.tensor_tensor(out=xov[:], in0=xov[:], in1=xt[:], op=ALU.add)
            body("v", _oev)
            a_resid_v[t] = C["v"]
            v_resid = C["v"]

            def _stx(e, w, t=t, v_resid=v_resid):
                w("v", v_resid)
                dma(e, "st", xmid_d[t * 128:(t + 1) * 128, :], xov[:])
                if debug:
                    dma(e, "st", dxmid_d[t * 128:(t + 1) * 128, :], xov[:])
            body("st", _stx)
            st_guard[1] = C["st"]
            xmid_st[t] = C["st"]

        A_T_DONE = C["t"]
        es_a.close()
        wgch = [sb(f"wgch{i}", [128, KS * 512], BF16) for i in range(2)]
        wgcl = [sb(f"wgcl{i}", [128, KS * 512], BF16) for i in range(2)]
        wuch = [sb(f"wuch{i}", [128, KS * 512], BF16) for i in range(2)]
        wucl = [sb(f"wucl{i}", [128, KS * 512], BF16) for i in range(2)]
        wdch = [sb(f"wdch{i}", [128, 4 * D], BF16) for i in range(2)]
        wdcl = [sb(f"wdcl{i}", [128, 4 * D], BF16) for i in range(2)]
        xqTh = [sb(f"xqTh{i}", [128, D], BF16) for i in range(2)]
        xqTl = [sb(f"xqTl{i}", [128, D], BF16) for i in range(2)]
        xmidb = [sb(f"xmidb{i}", [128, D]) for i in range(2)]
        hsc = sb("hsc", [128, 512])
        hsc2 = sb("hsc2", [128, 512])
        qxh2 = sb("qxh2", [128, 512], BF16)
        qxl2 = sb("qxl2", [128, 512], BF16)
        hTh = sb("hTh", [128, 512], BF16)
        hTl = sb("hTl", [128, 512], BF16)
        ybuf = sb("ybuf", [128, D])
        scr = qxq

        CH_OFF = [0]
        for c in CHUNKS:
            CH_OFF.append(CH_OFF[-1] + c)

        wbuf_guard = {0: 0, 1: 0}
        ld_chunk = {}
        ld_chunk_wd = {}
        ypair_guard = [0]
        n_pairs = n_tiles // 2 if "B" in phases else 0
        for p in range(n_pairs):
            tiles = [2 * p, 2 * p + 1]
            for j, t in enumerate(tiles):
                def _ldm(e, w, t=t, j=j, vg=C["v"], xst=xmid_st[t]):
                    e.wait_ge(SEM["v"], vg)
                    e.wait_ge(SEM["st"], xst * 16)
                    dma(e, "ld", xmidb[j][:], xmid_d[t * 128:(t + 1) * 128, :])
                body("ld", _ldm)
                ld_m = ld_mark()

                def _lnv(e, w, ld_m=ld_m, j=j):
                    w("ld", ld_m)
                    e.reduce_sum(out=stt[:, 0:1], in_=xmidb[j][:], axis=AX.X)
                    e.tensor_tensor(out=qu[:, 0:D], in0=xmidb[j][:], in1=xmidb[j][:],
                                    op=ALU.mult)
                    e.reduce_sum(out=stt[:, 1:2], in_=qu[:, 0:D], axis=AX.X)
                    e.tensor_scalar(out=stt[:, 2:3], in0=stt[:, 0:1], scalar1=1.0 / D,
                                    scalar2=None, op0=ALU.mult)
                    vsp(e)
                    e.tensor_scalar(out=stt[:, 3:4], in0=stt[:, 1:2], scalar1=1.0 / D,
                                    scalar2=EPS_LN, op0=ALU.mult, op1=ALU.add)
                    e.tensor_tensor(out=stt[:, 4:5], in0=stt[:, 2:3], in1=stt[:, 2:3],
                                    op=ALU.mult)
                    vsp(e)
                    e.tensor_tensor(out=stt[:, 5:6], in0=stt[:, 3:4],
                                    in1=stt[:, 4:5], op=ALU.subtract)
                    return vsp(e)
                body("v", _lnv)
                v_var = C["v"]

                def _lna2(e, w, v_var=v_var):
                    w("v", v_var)
                    e.activation(out=stt[:, 6:7], in_=stt[:, 5:6], func=AF.Ln)
                    asp(e, 2)
                    e.activation(out=stt[:, 7:8], in_=stt[:, 6:7], func=AF.Exp,
                                 scale=-0.5)
                    return asp(e, 2)
                body("a", _lna2)
                a_sqrt = C["a"]

                def _lnv3(e, w, a_sqrt=a_sqrt, j=j):
                    w("a", a_sqrt)
                    vsp(e)
                    e.tensor_tensor(out=stt[:, 8:9], in0=stt[:, 2:3], in1=stt[:, 7:8],
                                    op=ALU.mult)
                    vsp(e)
                    return e.tensor_scalar(out=qv[:, 0:D], in0=xmidb[j][:],
                                           scalar1=stt[:, 7:8], scalar2=stt[:, 8:9],
                                           op0=ALU.mult, op1=ALU.subtract)
                body("v", _lnv3)
                emit_quant(D, tdep=C["t"])
                v_q = C["v"]
                emit_T([qxh[:, i * 128:(i + 1) * 128] for i in range(KS)],
                       [(xqTh[j], i * 128) for i in range(KS)], deps=[("v", v_q)])
                emit_T([qxl[:, i * 128:(i + 1) * 128] for i in range(KS)],
                       [(xqTl[j], i * 128) for i in range(KS)])
                if debug and p == 0 and j == 0:
                    def _dq(e, w):
                        return e.tensor_copy(out=ybuf[:], in_=qxq[:, 0:D])
                    body("v", _dq)
                    dv2 = C["v"]
                    def _dq2(e, w, dv2=dv2):
                        w("v", dv2)
                        dma(e, "st", dxq_d[:, 0:D], ybuf[:])
                    body("st", _dq2)

            v_xq_pair = C["v"]

            steps = []
            for ci, CW in enumerate(CHUNKS):
                for j, t in enumerate(tiles):
                    steps.append((ci, CW, j, t))

            st_state = {}

            def b_quant(si):
                (ci, CW, j, t) = steps[si]
                m = st_state[si]
                bqh = qxh if si % 2 == 0 else qxh2
                bql = qxl if si % 2 == 0 else qxl2
                emit_quant(CW, qsrc=qvb[si % 2], qh=bqh, ql=bql,
                           tdep=st_state[si - 2]["t_tr"] if si >= 2 else 0)
                m["v_hq"] = C["v"]

            def b_tr(si):
                (ci, CW, j, t) = steps[si]
                cb = ci % 2
                m = st_state[si]
                bqh = qxh if si % 2 == 0 else qxh2
                bql = qxl if si % 2 == 0 else qxl2
                nsubk = CW // 128
                emit_T([bqh[:, i * 128:(i + 1) * 128] for i in range(nsubk)],
                       [(hTh, i * 128) for i in range(nsubk)],
                       deps=[("v", m["v_hq"])])
                m["v_evh"] = emit_T(
                    [bql[:, i * 128:(i + 1) * 128] for i in range(nsubk)],
                    [(hTl, i * 128) for i in range(nsubk)])
                m["t_tr"] = C["t"]

                pD = pV if j == 0 else pQ
                def _dn(e, w, v_hT=m["v_evh"], cb=cb, CW=CW, ci=ci, pD=pD,
                        ldw=ld_chunk_wd[ci]):
                    w("ld", ldw)
                    w("v", v_hT)
                    last = None
                    nks = CW // 128
                    for ksi in range(nks):
                        sh = hTh[:, ksi * 128:(ksi + 1) * 128]
                        sl = hTl[:, ksi * 128:(ksi + 1) * 128]
                        first = (ci == 0 and ksi == 0)
                        lastk = (ci == len(CHUNKS) - 1 and ksi == nks - 1)
                        for nch in range(2):
                            rh = wdch[cb][:, ksi * D + nch * 512: ksi * D + nch * 512 + 512]
                            rl = wdcl[cb][:, ksi * D + nch * 512: ksi * D + nch * 512 + 512]
                            od = pD[:, nch * 512:(nch + 1) * 512]
                            last = e.matmul(out=od, lhsT=sh, rhs=rh,
                                            start=first, stop=False)
                            last = e.matmul(out=od, lhsT=sh, rhs=rl,
                                            start=False, stop=False)
                            last = e.matmul(out=od, lhsT=sl, rhs=rh,
                                            start=False, stop=lastk)
                    return last
                body("t", _dn)
                if j == 1:
                    wbuf_guard[cb] = C["t"]

            for si, (ci, CW, j, t) in enumerate(steps):
                n0 = CH_OFF[ci]
                cb = ci % 2
                if j == 0:
                    def _ldc(e, w, ci=ci, n0=n0, CW=CW, cb=cb,
                             tg=wbuf_guard[cb], pst=PREP_ST, atd=A_T_DONE):
                        e.wait_ge(SEM["st"], pst * 16)
                        e.wait_ge(SEM["t"], max(tg, atd))
                        for i in range(KS):
                            dma(e, "ld", wgch[cb][:, i * 512:i * 512 + CW],
                                wgqh_d[i, :, n0:n0 + CW])
                            dma(e, "ld", wgcl[cb][:, i * 512:i * 512 + CW],
                                wgql_d[i, :, n0:n0 + CW])
                            dma(e, "ld", wuch[cb][:, i * 512:i * 512 + CW],
                                wuqh_d[i, :, n0:n0 + CW])
                            dma(e, "ld", wucl[cb][:, i * 512:i * 512 + CW],
                                wuql_d[i, :, n0:n0 + CW])
                    body("ld", _ldc)
                    ld_chunk[ci] = ld_mark()

                    def _ldcw(e, w, n0=n0, CW=CW, cb=cb):
                        for jj in range(CW // 128):
                            dma(e, "ld", wdch[cb][:, jj * D:(jj + 1) * D],
                                wdqh_d[n0 // 128 + jj, :, :])
                            dma(e, "ld", wdcl[cb][:, jj * D:(jj + 1) * D],
                                wdql_d[n0 // 128 + jj, :, :])
                    body("ld", _ldcw)
                    ld_chunk_wd[ci] = ld_mark()

                prev = st_state.get(si - 1)
                def _gu(e, w, j=j, cb=cb, CW=CW, ldc=ld_chunk[ci], prev=prev,
                        vxq=v_xq_pair):
                    w("ld", ldc)
                    w("v", vxq if prev is None else max(vxq, prev["v_guprod"]))
                    if prev is not None:
                        w("a", prev["a_eneg"])
                    last = None
                    for ksi in range(KS):
                        sh = xqTh[j][:, ksi * 128:(ksi + 1) * 128]
                        sl = xqTl[j][:, ksi * 128:(ksi + 1) * 128]
                        first = ksi == 0
                        lastk = ksi == KS - 1
                        for (wch_, wcl_, po) in ((wgch[cb], wgcl[cb], 0),
                                                 (wuch[cb], wucl[cb], 512)):
                            rh = wch_[:, ksi * 512:ksi * 512 + CW]
                            rl = wcl_[:, ksi * 512:ksi * 512 + CW]
                            od = pK[:, po:po + CW]
                            last = e.matmul(out=od, lhsT=sh, rhs=rh,
                                            start=first, stop=False)
                            last = e.matmul(out=od, lhsT=sh, rhs=rl,
                                            start=False, stop=False)
                            last = e.matmul(out=od, lhsT=sl, rhs=rh,
                                            start=False, stop=lastk)
                    return last
                body("t", _gu)
                t_gu = C["t"]

                if PIPE_B:
                    if si >= 1:
                        b_quant(si - 1)
                    if si >= 2:
                        b_tr(si - 2)

                def _eneg(e, w, t_gu=t_gu, CW=CW, prev=prev):
                    w("t", t_gu)
                    if prev is not None:
                        w("v", prev["v_h2"])
                    return e.activation(out=hsc[:, 0:CW], in_=pK[:, 0:CW],
                                        func=AF.Exp, scale=-1.0)
                body("a", _eneg)
                a_eneg = C["a"]

                def _h(e, w, t_gu=t_gu, CW=CW):
                    w("t", t_gu)
                    e.tensor_copy(out=hsc2[:, 0:CW], in_=pK[:, 512:512 + CW])
                    return e.tensor_tensor(out=hsc2[:, 0:CW], in0=pK[:, 0:CW],
                                           in1=hsc2[:, 0:CW], op=ALU.mult)
                body("v", _h)
                v_guprod = C["v"]

                def _h2(e, w, a_eneg=a_eneg, CW=CW, si=si):
                    w("a", a_eneg)
                    e.tensor_scalar(out=hsc[:, 0:CW], in0=hsc[:, 0:CW],
                                    scalar1=1.0, scalar2=None, op0=ALU.add)
                    e.reciprocal(out=hsc[:, 0:CW], in_=hsc[:, 0:CW])
                    return e.tensor_tensor(out=qvb[si % 2][:, 0:CW],
                                           in0=hsc2[:, 0:CW],
                                           in1=hsc[:, 0:CW], op=ALU.mult)
                body("v", _h2)
                v_h2 = C["v"]
                st_state[si] = {"v_guprod": v_guprod, "a_eneg": a_eneg,
                                "v_h2": v_h2}
                if not PIPE_B:
                    b_quant(si)
                    b_tr(si)
            if PIPE_B:
                b_quant(len(steps) - 1)
                b_tr(len(steps) - 2)
                b_tr(len(steps) - 1)

            for j, t in enumerate(tiles):
                pD = pV if j == 0 else pQ
                t_done = C["t"]

                def _yev(e, w, pD=pD, j=j, t_done=t_done, yg=ypair_guard[0]):
                    w("t", t_done)
                    if yg:
                        e.wait_ge(SEM["st"], yg * 16)
                    e.tensor_tensor(out=ybuf[:, 0:512], in0=pD[:, 0:512],
                                    in1=xmidb[j][:, 0:512], op=ALU.add)
                    return e.tensor_tensor(out=ybuf[:, 512:1024], in0=pD[:, 512:1024],
                                           in1=xmidb[j][:, 512:1024], op=ALU.add)
                body("v", _yev)
                v_y = C["v"]

                def _sty(e, w, t=t, v_y=v_y):
                    w("v", v_y)
                    dma(e, "st", y_d[t * 128:(t + 1) * 128, :], ybuf[:])
                body("st", _sty)
                ypair_guard[0] = C["st"]

        if "B" not in phases:
            def _ydummy(e, w):
                w("v", None)
                for t in range(n_tiles):
                    dma(e, "st", y_d[t * 128:(t + 1) * 128, :], ybuf[:])
            body("st", _ydummy)

        def _fin(e, w):
            e.wait_ge(SEM["st"], C["st"] * 16)
            return e.engine_nop()
        body("g", _fin)

    return nc


def kernel(L0_attn_x, L0_attn_Wq, L0_attn_Wk, L0_attn_Wv, L0_attn_Wo,
           L0_attn_bo, L0_ffn_w_gate, L0_ffn_w_up, L0_ffn_w_down):
    x = np.ascontiguousarray(np.asarray(L0_attn_x, dtype=np.float32).reshape(-1, D))
    wq = np.ascontiguousarray(np.asarray(L0_attn_Wq, dtype=np.float32))
    wk = np.ascontiguousarray(np.asarray(L0_attn_Wk, dtype=np.float32))
    wv = np.ascontiguousarray(np.asarray(L0_attn_Wv, dtype=np.float32))
    wo = np.ascontiguousarray(np.asarray(L0_attn_Wo, dtype=np.float32))
    bo = np.ascontiguousarray(np.asarray(L0_attn_bo, dtype=np.float32).reshape(1, D))
    wg = np.ascontiguousarray(np.asarray(L0_ffn_w_gate, dtype=np.float32))
    wu = np.ascontiguousarray(np.asarray(L0_ffn_w_up, dtype=np.float32))
    wd = np.ascontiguousarray(np.asarray(L0_ffn_w_down, dtype=np.float32))

    nc = build_program()
    in_maps = []
    for i in range(N_CORES):
        in_maps.append({
            "x": x[i * TOK_PER_CORE:(i + 1) * TOK_PER_CORE],
            "wq": wq, "wk": wk, "wv": wv, "wo": wo, "bo2": bo,
            "wg": wg, "wu": wu, "wd": wd,
        })
    res = run_bass_kernel_spmd(nc, in_maps, core_ids=list(range(N_CORES)))
    y = np.concatenate([np.asarray(r["y"]) for r in res.results], axis=0)
    return y.reshape(4, 4096, D).astype(np.float32)
